# version 17
# baseline (speedup 1.0000x reference)
import contextlib

import numpy as np
import ml_dtypes
import concourse.bass as bass
import concourse.mybir as mybir
from concourse.bass_utils import run_bass_kernel_spmd

F32 = mybir.dt.float32
BF16 = mybir.dt.bfloat16
U32 = mybir.dt.uint32
AF = mybir.ActivationFunctionType
ALU = mybir.AluOpType
AX = mybir.AxisListType

S = 8192
D = 2048
NCOR = 8
TSL = S // NCOR
HPC = 2
DOUT = 256
T1 = 256
NST = S // T1
TQ = 256
NQT = S // TQ
NCH = S // 128
RC = 12582912.0
SQHD = float(np.sqrt(128.0))
CS = 1.0 / (1e4 * SQHD)
C2 = 0.5 * CS
NTILE = HPC * NQT
TOTAL_PAIRS = HPC * (NQT * (NQT + 1) // 2)


def _batch_plan():
    tiles_plan = []
    gb_count = 0
    for T in range(NTILE):
        h, qt = divmod(T, NQT)
        NP = qt + 1
        PB = h * (NQT * (NQT + 1) // 2) + qt * (qt + 1) // 2
        batches = []
        p = 0
        while p < NP:
            pl = [p] if p + 1 >= NP else [p, p + 1]
            batches.append((gb_count, pl))
            gb_count += 1
            p += len(pl)
        tiles_plan.append((h, qt, NP, PB, batches))
    return tiles_plan, gb_count


TILES_PLAN, GBTOT = _batch_plan()
GB_END_TILE = [TILES_PLAN[T][4][-1][0] + 1 for T in range(NTILE)]


def npairs(qt):
    return qt + 1


def pbase(h, qt):
    return h * (NQT * (NQT + 1) // 2) + qt * (qt + 1) // 2


def pend(T):
    if T < 0:
        return 0
    h, qt = divmod(T, NQT)
    return pbase(h, qt) + npairs(qt)


def build(stop=5, b1mode=3):
    nc = bass.Bass("TRN2", target_bir_lowering=False)

    xt = nc.dram_tensor("xt", [D, S], F32, kind="ExternalInput").ap()
    xtm = nc.dram_tensor("xtm", [TSL, D], F32, kind="ExternalInput").ap()
    xfm = nc.dram_tensor("xfm", [D, TSL], F32, kind="ExternalInput").ap()
    wq = nc.dram_tensor("wq", [D, DOUT], F32, kind="ExternalInput").ap()
    wk = nc.dram_tensor("wk", [D, DOUT], F32, kind="ExternalInput").ap()
    wv = nc.dram_tensor("wv", [D, DOUT], F32, kind="ExternalInput").ap()
    wol = nc.dram_tensor("wol", [16 * 128, D], F32, kind="ExternalInput").ap()
    wfl = nc.dram_tensor("wfl", [16 * 128, D], F32, kind="ExternalInput").ap()
    gam = nc.dram_tensor("gam", [1, D], F32, kind="ExternalInput").ap()
    idb = nc.dram_tensor("idb", [128, 128], BF16, kind="ExternalInput").ap()
    idf = nc.dram_tensor("idf", [128, 128], F32, kind="ExternalInput").ap()
    mskab = nc.dram_tensor("mskab", [128, 512], BF16, kind="ExternalInput").ap()
    coff = nc.dram_tensor("coff", [1, 1], U32, kind="ExternalInput").ap()
    out = nc.dram_tensor("out", [D, TSL], F32, kind="ExternalOutput").ap()

    sc_rstd = nc.dram_tensor("sc_rstd", [1, S], F32)
    sc_r1 = nc.dram_tensor("sc_r1", [1, S], BF16)
    sc_r100 = nc.dram_tensor("sc_r100", [2, 256], BF16)
    cc_st_in = nc.dram_tensor("cc_st_in", [2, TSL], F32)
    cc_st_out = nc.dram_tensor("cc_st_out", [16, TSL], F32)
    cc_at_in = nc.dram_tensor("cc_at_in", [DOUT, S], BF16)
    cc_at_out0 = nc.dram_tensor("cc_at_out0", [NCOR * 128, S], BF16, addr_space="Shared")
    cc_at_out1 = nc.dram_tensor("cc_at_out1", [NCOR * 128, S], BF16, addr_space="Shared")

    sem_names = [
        "sw0", "s0xa", "s0xb", "s0g", "s0v", "s0t", "s0c", "scc0", "s0r", "s0ch", "s0w",
        "swga", "swgb", "srsa", "srsb", "srsc", "srsd", "sxa", "sxb", "sxga", "sxgb", "sbc", "sRe", "sxs", "spj", "sev", "str_", "svt",
        "srs1", "srs2",
        "svq", "spn", "svr", "svm1", "spm", "svm",
        "sst", "se", "smask", "sz", "sr", "sbc2", "srb", "sp", "spv", "sao", "scwa", "scwb",
        "sgm", "s3xa", "s3xb", "s3a", "s3b", "s3c", "s3d", "s3q", "s3t2", "s3e",
        "swla", "swlb", "sab", "swm", "sws1", "sws",
        "spg", "svg", "ssl", "sst3", "ssr", "scc", "sag",
        "s4wa", "s4wb", "s4wc", "s4fa", "s4fb", "s4fc", "s4p", "s4g", "s4e", "s4oa", "s4ob", "s4c",
    ]

    with contextlib.ExitStack() as ctx:
        sems = {n: ctx.enter_context(nc.semaphore(n)) for n in sem_names}
        g = lambda n: sems[n]

        ones_bf = ctx.enter_context(nc.sbuf_tensor("ones_bf", [128, 1], BF16))
        ones_b1 = ctx.enter_context(nc.sbuf_tensor("ones_b1", [1, 128], BF16))
        ones_f1 = ctx.enter_context(nc.sbuf_tensor("ones_f1", [1, 128], F32))
        iden_b = ctx.enter_context(nc.sbuf_tensor("iden_b", [128, 128], BF16))
        iden_f = ctx.enter_context(nc.sbuf_tensor("iden_f", [128, 128], F32))
        maskab = ctx.enter_context(nc.sbuf_tensor("maskab", [128, 512], BF16))
        eps5_1 = ctx.enter_context(nc.sbuf_tensor("eps5_1", [1, 1], F32))
        eps5_128 = ctx.enter_context(nc.sbuf_tensor("eps5_128", [128, 1], F32))
        eps6_128 = ctx.enter_context(nc.sbuf_tensor("eps6_128", [128, 1], F32))
        rc_128 = ctx.enter_context(nc.sbuf_tensor("rc_128", [128, 1], F32))
        nrc_128 = ctx.enter_context(nc.sbuf_tensor("nrc_128", [128, 1], F32))
        stats_tm = ctx.enter_context(nc.sbuf_tensor("stats_tm", [128, 16], F32))
        s_coff = ctx.enter_context(nc.sbuf_tensor("s_coff", [1, 1], U32))

        with contextlib.ExitStack() as c12:
            q_sb = c12.enter_context(nc.sbuf_tensor("q_sb", [128, HPC * S], BF16))
            k_sb = c12.enter_context(nc.sbuf_tensor("k_sb", [128, HPC * S], BF16))
            vt_sb = c12.enter_context(nc.sbuf_tensor("vt_sb", [128, HPC * NCH * 128], BF16))

            with contextlib.ExitStack() as c1:
                xtm_t = c1.enter_context(nc.sbuf_tensor("xtm_t", [128, 2 * D], F32))
                st_tr = c1.enter_context(nc.sbuf_tensor("st_tr", [16, 128], F32))
                srA = c1.enter_context(nc.sbuf_tensor("srA", [128, 64], F32))
                srB = c1.enter_context(nc.sbuf_tensor("srB", [128, 64], F32))
                srC = c1.enter_context(nc.sbuf_tensor("srC", [128, 64], F32))
                sr1 = c1.enter_context(nc.sbuf_tensor("sr1", [128, 64], BF16))
                wstg = c1.enter_context(nc.sbuf_tensor("wstg", [128, 2 * DOUT], F32))
                wq_bf = c1.enter_context(nc.sbuf_tensor("wq_bf", [128, 16 * DOUT], BF16))
                wk_bf = c1.enter_context(nc.sbuf_tensor("wk_bf", [128, 16 * DOUT], BF16))
                wv_bf = c1.enter_context(nc.sbuf_tensor("wv_bf", [128, 16 * DOUT], BF16))
                rsum = c1.enter_context(nc.sbuf_tensor("rsum", [1, 3 * DOUT], BF16))
                xstage = c1.enter_context(nc.sbuf_tensor("xstage", [128, 2 * 16 * T1], F32))
                xs_sb = c1.enter_context(nc.sbuf_tensor("xs_sb", [128, 2 * 16 * T1], BF16))
                rsl = c1.enter_context(nc.sbuf_tensor("rsl", [1, 4 * T1], F32))
                r1sl = c1.enter_context(nc.sbuf_tensor("r1sl", [1, 4 * T1], BF16))
                R100b = c1.enter_context(nc.sbuf_tensor("R100b", [128, 2 * T1], F32))
                vtmp = c1.enter_context(nc.sbuf_tensor("vtmp", [128, 2 * 2 * T1], BF16))
                p_pj = [c1.enter_context(nc.psum_tensor(f"p_pj{i}", [128, T1], F32)) for i in range(6)]
                p_bc1 = c1.enter_context(nc.psum_tensor("p_bc1", [128, T1], F32))
                p_tr1 = c1.enter_context(nc.psum_tensor("p_tr1", [128, 512], BF16))

                with nc.Block() as b1:
                    @b1.sync
                    def _(sync):
                        sync.dma_start(out=iden_b[:], in_=idb).then_inc(g("sw0"), 16)
                        sync.dma_start(out=iden_f[:], in_=idf).then_inc(g("sw0"), 16)
                        sync.dma_start(out=maskab[:], in_=mskab).then_inc(g("sw0"), 16)
                        sync.dma_start(out=s_coff[:], in_=coff).then_inc(g("sw0"), 16)
                        for t8 in range(2):
                            sync.dma_start(
                                out=xtm_t[:, (t8 % 2) * D:(t8 % 2 + 1) * D],
                                in_=xtm[t8 * 128:(t8 + 1) * 128, :],
                            ).then_inc(g("s0xa" if t8 % 2 == 0 else "s0xb"), 16)
                        for pj, wsrc in enumerate((wq, wk, wv)):
                            for c in range(16):
                                i = pj * 16 + c
                                sync.wait_ge(g("srs1"), max(0, i - 1))
                                sync.dma_start(
                                    out=wstg[:, (i % 2) * DOUT:(i % 2 + 1) * DOUT],
                                    in_=wsrc[c * 128:(c + 1) * 128, :],
                                ).then_inc(g("swga" if i % 2 == 0 else "swgb"), 16)
                        for t8 in range(2, 8):
                            sync.wait_ge(g("s0v"), t8 - 1)
                            sync.dma_start(
                                out=xtm_t[:, (t8 % 2) * D:(t8 % 2 + 1) * D],
                                in_=xtm[t8 * 128:(t8 + 1) * 128, :],
                            ).then_inc(g("s0xa" if t8 % 2 == 0 else "s0xb"), 16)
                        sync.wait_ge(g("s0ch"), 2)
                        sync.dma_start(out=sc_rstd.ap().rearrange("a (p c) -> (a p) c", p=128),
                                       in_=srA[:]).then_inc(g("s0w"), 16)
                        sync.wait_ge(g("s0ch"), 3)
                        sync.dma_start(out=sc_r1.ap().rearrange("a (p c) -> (a p) c", p=128),
                                       in_=sr1[:]).then_inc(g("s0w"), 16)
                        for i in range(NST if b1mode >= 2 else 0):
                            if i < 2:
                                sync.wait_ge(g("s0v"), 8)
                            sync.wait_ge(g("sxs"), 16 * max(0, i - 1))
                            for c in range(8):
                                sync.dma_start(
                                    out=xstage[:, ((i % 2) * 16 + c) * T1:((i % 2) * 16 + c) * T1 + T1],
                                    in_=xt[c * 128:(c + 1) * 128, i * T1:(i + 1) * T1],
                                ).then_inc(g("sxa" if i % 2 == 0 else "sxb"), 16)

                    @b1.scalar
                    def _(scalar):
                        scalar.wait_ge(g("s0r"), 1)
                        scalar.activation(srB[:], srB[:], AF.Ln, bias=eps5_128[:, 0:1])
                        scalar.drain()
                        scalar.activation(srB[:], srB[:], AF.Exp, scale=-0.5).then_inc(g("s0ch"), 1)
                        scalar.wait_ge(g("s0w"), 32)
                        for i in range(NST if b1mode >= 2 else 0):
                            scalar.wait_ge(g("sbc"), max(0, i - 3))
                            if i >= 4:
                                scalar.wait_ge(g("spj"), 6 * (i - 3))
                            sw = ("srsa", "srsb", "srsc", "srsd")[i % 4]
                            scalar.dma_start(
                                out=rsl[:, (i % 4) * T1:(i % 4 + 1) * T1],
                                in_=sc_rstd.ap()[0:1, i * T1:(i + 1) * T1],
                            ).then_inc(g(sw), 16)
                            scalar.dma_start(
                                out=r1sl[:, (i % 4) * T1:(i % 4 + 1) * T1],
                                in_=sc_r1.ap()[0:1, i * T1:(i + 1) * T1],
                            ).then_inc(g(sw), 16)

                    @b1.gpsimd
                    def _(gpsimd):
                        gpsimd.wait_ge(g("s0c"), 1)
                        st2 = cc_st_in.ap().rearrange("s (t p) -> (s t) p", t=8)
                        gpsimd.dma_start(out=st2[0:8, :], in_=st_tr[0:8, :]).then_inc(g("s0g"), 16)
                        gpsimd.dma_start(out=st2[8:16, :], in_=st_tr[8:16, :]).then_inc(g("s0g"), 16)
                        gpsimd.wait_ge(g("s0g"), 32)
                        gpsimd.collective_compute(
                            "AllGather", ALU.bypass,
                            replica_groups=[list(range(NCOR))],
                            ins=[cc_st_in.ap()],
                            outs=[cc_st_out.ap()],
                        ).then_inc(g("scc0"), 1)
                        gpsimd.wait_ge(g("scc0"), 1)
                        for r in range(8):
                            gpsimd.dma_start(out=srA[r * 16:(r + 1) * 16, :],
                                             in_=cc_st_out.ap()[2 * r:2 * r + 1, :].rearrange("a (p c) -> (a p) c", p=16)).then_inc(g("s0g"), 16)
                            gpsimd.dma_start(out=srB[r * 16:(r + 1) * 16, :],
                                             in_=cc_st_out.ap()[2 * r + 1:2 * r + 2, :].rearrange("a (p c) -> (a p) c", p=16)).then_inc(g("s0g"), 16)
                        gpsimd.wait_ge(g("s0g"), 288)
                        gpsimd.nop().then_inc(g("scc0"), 1)
                        for i in range(NST if b1mode >= 2 else 0):
                            gpsimd.wait_ge(g("sxs"), 16 * max(0, i - 1))
                            for c in range(8, 16):
                                gpsimd.dma_start(
                                    out=xstage[:, ((i % 2) * 16 + c) * T1:((i % 2) * 16 + c) * T1 + T1],
                                    in_=xt[c * 128:(c + 1) * 128, i * T1:(i + 1) * T1],
                                ).then_inc(g("sxga" if i % 2 == 0 else "sxgb"), 16)



                    @b1.vector
                    def _(vector):
                        vector.memset(ones_bf[:], 1.0)
                        vector.memset(ones_b1[:], 1.0)
                        vector.memset(ones_f1[:], 1.0)
                        vector.memset(eps5_1[:], 1e-5)
                        vector.memset(eps5_128[:], 1e-5)
                        vector.memset(eps6_128[:], 1e-6)
                        vector.memset(rc_128[:], RC)
                        vector.memset(nrc_128[:], -RC)
                        for i in range(48):
                            vector.wait_ge(g("swga" if i % 2 == 0 else "swgb"), 16 * (i // 2 + 1))
                            dst = (wq_bf, wk_bf, wv_bf)[i // 16]
                            c = i % 16
                            vector.tensor_copy(
                                dst[:, c * DOUT:(c + 1) * DOUT],
                                wstg[:, (i % 2) * DOUT:(i % 2 + 1) * DOUT],
                            ).then_inc(g("srs1"), 1)
                        for t8 in range(8):
                            vector.wait_ge(g("s0xa" if t8 % 2 == 0 else "s0xb"), 16 * (t8 // 2 + 1))
                            xv = xtm_t[:, (t8 % 2) * D:(t8 % 2 + 1) * D]
                            vector.reduce_sum(stats_tm[:, t8:t8 + 1], xv, axis=AX.X)
                            vector.drain()
                            vector.tensor_tensor(xstage[:, 0:D], xv, xv, ALU.mult)
                            vector.drain()
                            vector.reduce_sum(stats_tm[:, 8 + t8:9 + t8], xstage[:, 0:D], axis=AX.X).then_inc(g("s0v"), 1)
                        vector.wait_ge(g("s0t"), 1)
                        vector.tensor_copy(st_tr[:], p_pj[5].ap()[0:16, 0:128]).then_inc(g("s0c"), 1)
                        vector.wait_ge(g("scc0"), 2)
                        vector.tensor_scalar(srC[:], srA[:], 1.0 / D, None, ALU.mult)
                        vector.tensor_scalar(srB[:], srB[:], 1.0 / D, None, ALU.mult)
                        vector.drain()
                        vector.tensor_tensor(srA[:], srC[:], srC[:], ALU.mult)
                        vector.drain()
                        vector.tensor_tensor(srB[:], srB[:], srA[:], ALU.subtract).then_inc(g("s0r"), 1)
                        vector.wait_ge(g("s0ch"), 1)
                        vector.drain()
                        vector.tensor_scalar(srA[:], srB[:], 100.0, None, ALU.mult).then_inc(g("s0ch"), 1)
                        vector.drain()
                        vector.tensor_tensor(srC[:], srC[:], srB[:], ALU.mult)
                        vector.drain()
                        vector.tensor_scalar(sr1[:], srC[:], -100.0, None, ALU.mult).then_inc(g("s0ch"), 1)
                        for pj in range(3):
                            vector.wait_ge(g("srs2"), pj + 1)
                            vector.tensor_copy(rsum[0:1, pj * DOUT:(pj + 1) * DOUT],
                                               p_pj[5].ap()[0:1, 0:DOUT]).then_inc(g("svt"), 1)
                        for i in range(NST if b1mode >= 2 else 0):
                            buf = i % 2
                            vector.wait_ge(g("sbc"), i + 1)
                            vector.drain()
                            vector.tensor_copy(R100b[:, buf * T1:(buf + 1) * T1], p_bc1.ap()).then_inc(g("sRe"), 1)
                            vector.wait_ge(g("sxa" if i % 2 == 0 else "sxb"), 128 * (i // 2 + 1))
                            vector.wait_ge(g("sxga" if i % 2 == 0 else "sxgb"), 128 * (i // 2 + 1))
                            if i >= 2:
                                vector.wait_ge(g("spj"), 6 * (i - 1))
                            vector.drain()
                            for c in range(16):
                                vector.tensor_tensor(
                                    xs_sb[:, (buf * 16 + c) * T1:(buf * 16 + c) * T1 + T1],
                                    xstage[:, (buf * 16 + c) * T1:(buf * 16 + c) * T1 + T1],
                                    R100b[:, buf * T1:(buf + 1) * T1],
                                    ALU.mult,
                                ).then_inc(g("sxs"), 1)

                            for gi in range(6):
                                pj, dt = divmod(gi, 2)
                                vector.wait_ge(g("spj"), 6 * i + gi + 1)
                                if pj < 2:
                                    dstt = (q_sb, k_sb)[pj]
                                    od = dstt[:, dt * S + i * T1:dt * S + (i + 1) * T1]
                                else:
                                    od = vtmp[:, (buf * 2 + dt) * T1:(buf * 2 + dt) * T1 + T1]
                                    if i >= 2 and b1mode >= 3:
                                        vector.wait_ge(g("str_"), 2 * (i - 2) + 2)
                                vector.tensor_scalar(od, p_pj[gi].ap(), RC, RC,
                                                     ALU.add, ALU.subtract).then_inc(g("sev"), 1)
                            if i >= 1 and b1mode >= 3:
                                for h in range(2):
                                    vector.wait_ge(g("str_"), 2 * (i - 1) + h + 1)
                                    vector.tensor_copy(
                                        vt_sb[:, (h * NCH + (i - 1) * 2) * 128:(h * NCH + (i - 1) * 2) * 128 + 256],
                                        p_tr1.ap()[:, h * 256:h * 256 + 256] if False else p_tr1.ap().rearrange("p (a b) -> p a b", a=2)[:, h, :],
                                    ).then_inc(g("svt"), 1)
                        for h in range(2 if b1mode >= 3 else 0):
                            vector.wait_ge(g("str_"), 2 * (NST - 1) + h + 1)
                            vector.tensor_copy(
                                vt_sb[:, (h * NCH + (NST - 1) * 2) * 128:(h * NCH + (NST - 1) * 2) * 128 + 256],
                                p_tr1.ap().rearrange("p (a b) -> p a b", a=2)[:, h, :],
                            ).then_inc(g("svt"), 1)

                    @b1.tensor
                    def _(tensor):
                        tensor.wait_ge(g("s0v"), 8)
                        tensor.wait_ge(g("sw0"), 64)
                        tensor.transpose(p_pj[5].ap()[0:16, 0:128], stats_tm[:], iden_f[:]).then_inc(g("s0t"), 1)
                        tensor.wait_ge(g("s0c"), 1)
                        for pj, wbf in enumerate((wq_bf, wk_bf, wv_bf)):
                            tensor.wait_ge(g("srs1"), 16 * (pj + 1))
                            tensor.wait_ge(g("svt"), pj)
                            for c in range(16):
                                mm = tensor.matmul(p_pj[5].ap()[0:1, 0:DOUT], ones_bf[:],
                                                   wbf[:, c * DOUT:(c + 1) * DOUT],
                                                   start=(c == 0), stop=(c == 15))
                                if c == 15:
                                    mm.then_inc(g("srs2"), 1)
                        def emit_bc1(i):
                            tensor.wait_ge(g(("srsa", "srsb", "srsc", "srsd")[i % 4]), 32 * (i // 4 + 1))
                            if i >= 1:
                                tensor.wait_ge(g("sRe"), i)
                            tensor.matmul(p_bc1.ap(), ones_f1[:],
                                          rsl[0:1, (i % 4) * T1:(i % 4 + 1) * T1],
                                          start=True, stop=True).then_inc(g("sbc"), 1)

                        if b1mode >= 2:
                            emit_bc1(0)
                            emit_bc1(1)
                        for i in range(NST if b1mode >= 2 else 0):
                            buf = i % 2
                            if i == 0:
                                tensor.wait_ge(g("s0c"), 1)
                            for gi in range(6):
                                pj, dt = divmod(gi, 2)
                                wbf = (wq_bf, wk_bf, wv_bf)[pj]
                                if i >= 1:
                                    tensor.wait_ge(g("sev"), 6 * (i - 1) + gi + 1)
                                for c in range(16):
                                    if gi == 0:
                                        tensor.wait_ge(g("sxs"), 16 * i + c + 1)
                                    tensor.matmul(
                                        p_pj[gi].ap(),
                                        wbf[:, c * DOUT + dt * 128:c * DOUT + dt * 128 + 128],
                                        xs_sb[:, (buf * 16 + c) * T1:(buf * 16 + c) * T1 + T1],
                                        start=(c == 0), stop=False,
                                    )
                                tensor.matmul(
                                    p_pj[gi].ap(),
                                    rsum[0:1, pj * DOUT + dt * 128:pj * DOUT + dt * 128 + 128],
                                    r1sl[0:1, (i % 4) * T1:(i % 4 + 1) * T1],
                                    start=False, stop=True,
                                ).then_inc(g("spj"), 1)
                            if i >= 1 and b1mode >= 3:
                                ip = i - 1
                                for h in range(2):
                                    tensor.wait_ge(g("sev"), 6 * ip + 4 + h + 1)
                                    tensor.wait_ge(g("svt"), 3 + 2 * ip + h)
                                    for j in range(2):
                                        tr = tensor.transpose(
                                            p_tr1.ap().rearrange("p (a b) -> p a b", a=4)[:, 2 * h + j, :],
                                            vtmp[:, ((ip % 2) * 2 + h) * T1 + j * 128:((ip % 2) * 2 + h) * T1 + (j + 1) * 128],
                                            iden_b[:],
                                        )
                                        if j == 1:
                                            tr.then_inc(g("str_"), 1)
                            if i + 2 < NST:
                                emit_bc1(i + 2)
                        ip = NST - 1
                        for h in range(2 if b1mode >= 3 else 0):
                            tensor.wait_ge(g("sev"), 6 * ip + 4 + h + 1)
                            tensor.wait_ge(g("svt"), 3 + 2 * ip + h)
                            for j in range(2):
                                tr = tensor.transpose(
                                    p_tr1.ap().rearrange("p (a b) -> p a b", a=4)[:, 2 * h + j, :],
                                    vtmp[:, ((ip % 2) * 2 + h) * T1 + j * 128:((ip % 2) * 2 + h) * T1 + (j + 1) * 128],
                                    iden_b[:],
                                )
                                if j == 1:
                                    tr.then_inc(g("str_"), 1)

            if stop < 2:
                return nc
            with contextlib.ExitStack() as c2:
                estrip = [c2.enter_context(nc.sbuf_tensor(f"estrip{i}", [128, NQT * 512], BF16)) for i in range(2)]
                dstrip = [c2.enter_context(nc.sbuf_tensor(f"dstrip{i}", [128, 512], BF16)) for i in range(2)]
                R100a = [c2.enter_context(nc.sbuf_tensor(f"R100a{i}", [128, 1024], BF16)) for i in range(2)]
                r100v = c2.enter_context(nc.sbuf_tensor("r100v", [1, 256], F32))
                r100b_ = c2.enter_context(nc.sbuf_tensor("r100b_", [1, 256], BF16))
                t_scr = [c2.enter_context(nc.sbuf_tensor(f"t_scr{i}", [128, 1024], BF16)) for i in range(2)]
                p_scr = [c2.enter_context(nc.sbuf_tensor(f"p_scr{i}", [128, 1024], BF16)) for i in range(2)]
                ao_sb = [c2.enter_context(nc.sbuf_tensor(f"ao_sb{i}", [128, TQ], BF16)) for i in range(2)]
                qsq = c2.enter_context(nc.sbuf_tensor("qsq", [128, 1024], BF16))
                nrm = c2.enter_context(nc.sbuf_tensor("nrm", [1, 8], F32))
                aqrow = c2.enter_context(nc.sbuf_tensor("aqrow", [1, NQT], F32))
                lrow = c2.enter_context(nc.sbuf_tensor("lrow", [1, NQT], F32))
                negmg_all = c2.enter_context(nc.sbuf_tensor("negmg_all", [128, NTILE], F32))

                p_sA = c2.enter_context(nc.psum_tensor("p_sA", [128, 1024], F32))
                p_sB = c2.enter_context(nc.psum_tensor("p_sB", [128, 1024], F32))
                p_z = c2.enter_context(nc.psum_tensor("p_z", [1, 256], F32))
                p_o = [c2.enter_context(nc.psum_tensor(f"p_o{i}", [128, TQ], F32)) for i in range(2)]
                p_b2 = c2.enter_context(nc.psum_tensor("p_b2", [128, 512], F32))

                tiles_plan = TILES_PLAN
                gb_end_tile = GB_END_TILE

                def strip_of(T):
                    return estrip[T % 2]

                pro_steps = []
                for h in range(HPC):
                    for kind in ("q", "k", "d"):
                        for i16 in range(16):
                            pro_steps.append((h, kind, i16))
                    for g16 in range(16):
                        pro_steps.append((h, "b", g16))
                NPRO = len(pro_steps)

                def pro_bank(si):
                    return p_sA.ap()[:, (si % 2) * 512:(si % 2) * 512 + 512]

                def emit_zlast(tensor, Tp):
                    hp, qtp = divmod(Tp, NQT)
                    NPp = npairs(qtp)
                    tensor.wait_ge(g("smask"), Tp + 1)
                    if NPp == 1 and Tp > 0:
                        tensor.wait_ge(g("sr"), Tp)
                    for half in range(2):
                        mm = tensor.matmul(p_z.ap(), ones_bf[:],
                                           dstrip[Tp % 2][:, half * 256:half * 256 + 256],
                                           start=(NPp == 1 and half == 0), stop=(half == 1))
                        if half == 1:
                            mm.then_inc(g("sz"), 1)

                with nc.Block() as b2:
                    @b2.tensor
                    def _(tensor):
                        tensor.wait_ge(g("sev"), 6 * NST)
                        tensor.wait_ge(g("svt"), 3 + 2 * NST)
                        for si, (h, kind, idx) in enumerate(pro_steps):
                            tensor.wait_ge(g("svq"), si + 1)
                            if si >= 2:
                                tensor.wait_ge(g("svr"), si - 1)
                            if kind != "b":
                                tensor.matmul(
                                    pro_bank(si)[0:1, :],
                                    ones_bf[:], qsq[:, (si % 2) * 512:(si % 2) * 512 + 512],
                                    start=True, stop=True,
                                ).then_inc(g("spn"), 1)
                            else:
                                tensor.matmul(
                                    pro_bank(si)[:, 0:1],
                                    ones_f1[:], nrm[0:1, 3:4],
                                    start=True, stop=True,
                                ).then_inc(g("spn"), 1)
                        def emit_st(T, gb, pl):
                            h, qt, NP, PB, _ = tiles_plan[T]
                            if gb >= 2:
                                tensor.wait_ge(g("se"), gb - 1)
                            if gb == 0:
                                tensor.wait_ge(g("svr"), NPRO)
                            bank = (p_sA, p_sB)[gb % 2]
                            for j, p in enumerate(pl):
                                for half in range(2):
                                    c = 2 * p + half
                                    mm = tensor.matmul(
                                        bank.ap()[:, (2 * j + half) * 256:(2 * j + half + 1) * 256],
                                        k_sb[:, h * S + c * 128:h * S + (c + 1) * 128],
                                        q_sb[:, h * S + qt * TQ:h * S + (qt + 1) * TQ],
                                        start=True, stop=True,
                                    )
                                    if j == len(pl) - 1 and half == 1:
                                        mm.then_inc(g("sst"), 1)

                        def emit_zmm(T, gb, pl):
                            h, qt, NP, PB, _ = tiles_plan[T]
                            zps = [p for p in pl if p < NP - 1]
                            if not zps:
                                return
                            tensor.wait_ge(g("se"), gb + 1)
                            if pl[0] == 0 and T > 0:
                                tensor.wait_ge(g("sr"), T)
                            for p in zps:
                                for half in range(2):
                                    tensor.matmul(
                                        p_z.ap(), ones_bf[:],
                                        strip_of(T)[:, (2 * p + half) * 256:(2 * p + half + 1) * 256],
                                        start=(p == 0 and half == 0), stop=False,
                                    )

                        def emit_bcast2(Tp):
                            tensor.wait_ge(g("sr"), Tp + 1)
                            if Tp >= 1:
                                tensor.wait_ge(g("srb"), Tp)
                            for half in range(2):
                                mm = tensor.matmul(
                                    p_b2.ap().rearrange("p (a b) -> p a b", a=2)[:, half, :],
                                    ones_b1[:], r100b_[0:1, :], start=True, stop=True)
                                if half == 1:
                                    mm.then_inc(g("sbc2"), 1)

                        def emit_pv(Tp, gb, pl, first, last_of_tile):
                            hp, qtp, NPp, PBp, _ = tiles_plan[Tp]
                            if first and Tp >= 2:
                                tensor.wait_ge(g("sao"), Tp - 1)
                            tensor.wait_ge(g("sp"), gb + 1)
                            for j, p in enumerate(pl):
                                for half in range(2):
                                    c = 2 * p + half
                                    mm = tensor.matmul(
                                        p_o[Tp % 2].ap(),
                                        vt_sb[:, (hp * NCH + c) * 128:(hp * NCH + c) * 128 + 128],
                                        p_scr[gb % 2][:, (2 * j + half) * 256:(2 * j + half + 1) * 256],
                                        start=(first and j == 0 and half == 0),
                                        stop=(p == NPp - 1 and half == 1),
                                    )
                                    if j == len(pl) - 1 and half == 1:
                                        mm.then_inc(g("spv"), 1)

                        for T in range(NTILE):
                            h, qt, NP, PB, batches = tiles_plan[T]
                            if T > 0:
                                emit_zlast(tensor, T - 1)
                            pend_ops = []
                            if T > 0:
                                pend_ops.append(("bc", T - 1))
                                pb = tiles_plan[T - 1][4]
                                for bi, (gbp, plp) in enumerate(pb):
                                    pend_ops.append(("pv", T - 1, gbp, plp, bi == 0))
                            for ki, (gb, pl) in enumerate(batches):
                                emit_st(T, gb, pl)
                                if ki >= 1:
                                    emit_zmm(T, *batches[ki - 1])
                                if pend_ops:
                                    op = pend_ops.pop(0)
                                    if op[0] == "bc":
                                        emit_bcast2(op[1])
                                    else:
                                        emit_pv(op[1], op[2], op[3], op[4], False)
                            emit_zmm(T, *batches[-1])
                            for op in pend_ops:
                                if op[0] == "bc":
                                    emit_bcast2(op[1])
                                else:
                                    emit_pv(op[1], op[2], op[3], op[4], False)
                        emit_zlast(tensor, NTILE - 1)
                        Tp = NTILE - 1
                        emit_bcast2(Tp)
                        for bi, (gbp, plp) in enumerate(tiles_plan[Tp][4]):
                            emit_pv(Tp, gbp, plp, bi == 0, False)

                    @b2.scalar
                    def _(scalar):
                        scalar.wait_ge(g("sxs"), 16 * NST)
                        scalar.wait_ge(g("spj"), 6 * NST)
                        for T in range(NTILE):
                            h, qt, NP, PB, batches = tiles_plan[T]
                            if qt == 0:
                                scalar.wait_ge(g("svm"), h + 1)
                            if T >= 2:
                                scalar.wait_ge(g("sp"), gb_end_tile[T - 2])
                            for gb, pl in batches:
                                scalar.wait_ge(g("sst"), gb + 1)
                                scalar.drain()
                                scalar.activation(
                                    strip_of(T)[:, pl[0] * 512:(pl[-1] + 1) * 512],
                                    (p_sA, p_sB)[gb % 2].ap()[:, 0:len(pl) * 512],
                                    AF.Exp, scale=CS, bias=negmg_all[:, (T // 2):(T // 2) + 1],
                                ).then_inc(g("se"), 1)
                            scalar.wait_ge(g("sbc2"), T + 1)
                            if T >= 2:
                                scalar.wait_ge(g("sp"), gb_end_tile[T - 2])
                            scalar.drain()
                            scalar.activation(R100a[T % 2][:, 0:512], p_b2.ap(), AF.Identity)
                            scalar.drain()
                            scalar.activation(R100a[T % 2][:, 512:1024], p_b2.ap(), AF.Identity).then_inc(g("srb"), 1)
                            if T >= 1:
                                scalar.wait_ge(g("spv"), gb_end_tile[T - 1])
                                if T >= 3:
                                    scalar.wait_ge(g("scwa" if (T - 1) % 2 == 0 else "scwb"),
                                                   16 * ((T - 1) // 2))
                                scalar.drain()
                                scalar.activation(ao_sb[(T - 1) % 2][:], p_o[(T - 1) % 2].ap(),
                                                  AF.Identity, scale=0.01).then_inc(g("sao"), 1)
                        T = NTILE - 1
                        scalar.wait_ge(g("spv"), gb_end_tile[T])
                        scalar.wait_ge(g("scwa" if T % 2 == 0 else "scwb"), 16 * (T // 2))
                        scalar.drain()
                        scalar.activation(ao_sb[T % 2][:], p_o[T % 2].ap(), AF.Identity,
                                          scale=0.01).then_inc(g("sao"), 1)

                    @b2.vector
                    def _(vector):
                        vector.wait_ge(g("spj"), 6 * NST)
                        vector.wait_ge(g("str_"), 2 * NST)
                        vector.drain()
                        vector.memset(nrm[:], 0.0)
                        for si, (h, kind, idx) in enumerate(pro_steps):
                            if si >= 2:
                                vector.wait_ge(g("spn"), si - 1)
                            vector.drain()
                            if kind == "q":
                                sl = q_sb[:, h * S + idx * 512:h * S + (idx + 1) * 512]
                                vector.tensor_tensor(qsq[:, (si % 2) * 512:(si % 2) * 512 + 512],
                                                     sl, sl, ALU.mult).then_inc(g("svq"), 1)
                            elif kind == "k":
                                sl = k_sb[:, h * S + idx * 512:h * S + (idx + 1) * 512]
                                vector.tensor_tensor(qsq[:, (si % 2) * 512:(si % 2) * 512 + 512],
                                                     sl, sl, ALU.mult).then_inc(g("svq"), 1)
                            elif kind == "d":
                                qsl = q_sb[:, h * S + idx * 512:h * S + (idx + 1) * 512]
                                ksl = k_sb[:, h * S + idx * 512:h * S + (idx + 1) * 512]
                                vector.tensor_tensor(qsq[:, (si % 2) * 512:(si % 2) * 512 + 512],
                                                     qsl, ksl, ALU.mult).then_inc(g("svq"), 1)
                            else:
                                vector.tensor_tensor(nrm[0:1, 4:5], aqrow[0:1, idx:idx + 1],
                                                     nrm[0:1, 1:2], ALU.add)
                                vector.drain()
                                vector.tensor_scalar(nrm[0:1, 4:5], nrm[0:1, 4:5], C2, None, ALU.mult)
                                vector.tensor_scalar(nrm[0:1, 5:6], lrow[0:1, idx:idx + 1], CS, 80.0,
                                                     ALU.mult, ALU.add)
                                vector.drain()
                                vector.tensor_tensor(nrm[0:1, 4:5], nrm[0:1, 4:5], nrm[0:1, 5:6], ALU.min)
                                vector.drain()
                                vector.tensor_scalar(nrm[0:1, 3:4], nrm[0:1, 4:5], -1.0, None,
                                                     ALU.mult).then_inc(g("svq"), 1)
                            vector.wait_ge(g("spn"), si + 1)
                            vector.drain()
                            if kind == "q":
                                vector.reduce_max(aqrow[0:1, idx:idx + 1], pro_bank(si)[0:1, :],
                                                  axis=AX.X).then_inc(g("svr"), 1)
                            elif kind == "k":
                                vector.reduce_max(nrm[0:1, 6:7], pro_bank(si)[0:1, :], axis=AX.X)
                                vector.drain()
                                vector.tensor_tensor(nrm[0:1, 1:2], nrm[0:1, 1:2], nrm[0:1, 6:7],
                                                     ALU.max).then_inc(g("svr"), 1)
                            elif kind == "d":
                                vector.tensor_reduce(lrow[0:1, idx:idx + 1], pro_bank(si)[0:1, :],
                                                     AX.X, ALU.min).then_inc(g("svr"), 1)
                            else:
                                vector.tensor_copy(negmg_all[:, h * 16 + idx:h * 16 + idx + 1],
                                                   pro_bank(si)[:, 0:1]).then_inc(g("svr"), 1)
                            if kind == "b" and idx == 15:
                                vector.nop().then_inc(g("svm"), 1)
                        for T in range(NTILE):
                            h, qt, NP, PB, batches = tiles_plan[T]
                            vector.wait_ge(g("se"), gb_end_tile[T])
                            if T >= 2:
                                vector.wait_ge(g("sz"), T - 1)
                            vector.drain()
                            vector.tensor_tensor(
                                dstrip[T % 2][:],
                                strip_of(T)[:, (NP - 1) * 512:NP * 512],
                                maskab[:], ALU.mult,
                            ).then_inc(g("smask"), 1)
                            vector.wait_ge(g("sz"), T + 1)
                            vector.drain()
                            vector.reciprocal(r100v[:], p_z.ap())
                            vector.drain()
                            vector.tensor_scalar(r100b_[:], r100v[:], 100.0, None, ALU.mult).then_inc(g("sr"), 1)

                            vector.wait_ge(g("srb"), T + 1)
                            for gb, pl in batches:
                                if gb >= 2:
                                    vector.wait_ge(g("spv"), gb - 1)
                                vector.drain()
                                wid = len(pl) * 512
                                has_diag = (pl[-1] == NP - 1)
                                Ra = R100a[T % 2]
                                if has_diag and len(pl) == 2:
                                    vector.tensor_tensor(t_scr[gb % 2][:, 0:512],
                                                         strip_of(T)[:, pl[0] * 512:(pl[0] + 1) * 512],
                                                         Ra[:, 0:512], ALU.mult)
                                    vector.drain()
                                    vector.tensor_tensor(t_scr[gb % 2][:, 512:1024],
                                                         dstrip[T % 2][:], Ra[:, 0:512], ALU.mult)
                                elif has_diag:
                                    vector.tensor_tensor(t_scr[gb % 2][:, 0:512],
                                                         dstrip[T % 2][:], Ra[:, 0:512], ALU.mult)
                                else:
                                    vector.tensor_tensor(t_scr[gb % 2][:, 0:wid],
                                                         strip_of(T)[:, pl[0] * 512:(pl[-1] + 1) * 512],
                                                         Ra[:, 0:wid], ALU.mult)
                                vector.drain()
                                vector.tensor_scalar(p_scr[gb % 2][:, 0:wid], t_scr[gb % 2][:, 0:wid],
                                                     RC, RC, ALU.add, ALU.subtract).then_inc(g("sp"), 1)


                    @b2.sync
                    def _(sync):
                        for T in range(NTILE):
                            h, qt = divmod(T, NQT)
                            sync.wait_ge(g("sao"), T + 1)
                            sync.dma_start(
                                out=cc_at_in.ap()[h * 128:(h + 1) * 128, qt * TQ:(qt + 1) * TQ],
                                in_=ao_sb[T % 2][:],
                            ).then_inc(g("scwa" if T % 2 == 0 else "scwb"), 16)

                    @b2.gpsimd
                    def _(gpsimd):
                        gpsimd.wait_ge(g("scwa"), 16 * 16)
                        gpsimd.wait_ge(g("scwb"), 16 * 16)
                        gpsimd.collective_compute(
                            "AllGather", ALU.bypass,
                            replica_groups=[list(range(NCOR))],
                            ins=[cc_at_in.ap()[0:128, :]],
                            outs=[cc_at_out0.ap()],
                        ).then_inc(g("scc"), 1)
                        gpsimd.wait_ge(g("scwa"), 16 * 32)
                        gpsimd.wait_ge(g("scwb"), 16 * 32)
                        gpsimd.collective_compute(
                            "AllGather", ALU.bypass,
                            replica_groups=[list(range(NCOR))],
                            ins=[cc_at_in.ap()[128:256, :]],
                            outs=[cc_at_out1.ap()],
                        ).then_inc(g("scc"), 1)

        if stop < 3:
            return nc
        with contextlib.ExitStack() as c34:
            xq_fm = c34.enter_context(nc.sbuf_tensor("xq_fm", [128, 16 * TSL], BF16))
            sevac_b = c34.enter_context(nc.sbuf_tensor("sevac_b", [128, TSL], F32))
            wsc_v = c34.enter_context(nc.sbuf_tensor("wsc_v", [128, 1], F32))
            wme_v = c34.enter_context(nc.sbuf_tensor("wme_v", [128, 1], F32))
            attn_g = c34.enter_context(nc.sbuf_tensor("attn_g", [128, 16 * TSL], BF16))
            wo_st = c34.enter_context(nc.sbuf_tensor("wo_st", [128, 3 * D], F32))
            xr_st = c34.enter_context(nc.sbuf_tensor("xr_st", [128, 3 * TSL], F32))

            with contextlib.ExitStack() as c3:
                xtm3 = c3.enter_context(nc.sbuf_tensor("xtm3", [128, 2 * D], F32))
                xn3 = c3.enter_context(nc.sbuf_tensor("xn3", [128, D], F32))
                xn2g = c3.enter_context(nc.sbuf_tensor("xn2g", [128, D], F32))
                xq_tm = c3.enter_context(nc.sbuf_tensor("xq_tm", [128, 2 * D], BF16))
                gam_sb = c3.enter_context(nc.sbuf_tensor("gam_sb", [1, D], F32))
                gam_b = c3.enter_context(nc.sbuf_tensor("gam_b", [128, D], F32))
                loc = c3.enter_context(nc.sbuf_tensor("loc", [128, 12], F32))
                inva = c3.enter_context(nc.sbuf_tensor("inva", [128, 8], F32))
                sevl = c3.enter_context(nc.sbuf_tensor("sevl", [128, 8], F32))
                wstg3 = c3.enter_context(nc.sbuf_tensor("wstg3", [128, 2 * D], F32))
                wabs = c3.enter_context(nc.sbuf_tensor("wabs", [128, 2 * D], BF16))
                wsc = c3.enter_context(nc.sbuf_tensor("wsc", [1, 4], F32))
                sev_r = c3.enter_context(nc.sbuf_tensor("sev_r", [1, TSL], F32))

                p_t3 = [c3.enter_context(nc.psum_tensor(f"p_t3{i}", [128, 1024], BF16)) for i in range(2)]
                p_w3 = c3.enter_context(nc.psum_tensor("p_w3", [1, 512], F32))
                p_sv = c3.enter_context(nc.psum_tensor("p_sv", [1, TSL], F32))
                p_g3 = c3.enter_context(nc.psum_tensor("p_g3", [128, 512], F32))

                with nc.Block() as b3:
                    @b3.gpsimd
                    def _(gpsimd):
                        gpsimd.wait_ge(g("sw0"), 64)
                        with gpsimd.register("roff") as roff:
                            gpsimd.reg_load(roff, s_coff[0:1, 0:1])
                            off = gpsimd.snap(roff)
                            gpsimd.wait_ge(g("scc"), 1)
                            for c in range(0, 16, 2):
                                cg = c // 2
                                gpsimd.dma_start(
                                    out=attn_g[:, c * TSL:(c + 1) * TSL],
                                    in_=cc_at_out0.ap()[cg * 128:(cg + 1) * 128, bass.ts(off, TSL)],
                                ).then_inc(g("sag"), 16)
                            gpsimd.wait_ge(g("scc"), 2)
                            for c in range(1, 16, 2):
                                cg = c // 2
                                gpsimd.dma_start(
                                    out=attn_g[:, c * TSL:(c + 1) * TSL],
                                    in_=cc_at_out1.ap()[cg * 128:(cg + 1) * 128, bass.ts(off, TSL)],
                                ).then_inc(g("sag"), 16)

                    @b3.sync
                    def _(sync):
                        sync.wait_ge(g("sp"), GBTOT)
                        sync.wait_ge(g("spv"), GBTOT)
                        sync.wait_ge(g("sao"), NTILE)
                        sync.wait_ge(g("se"), GBTOT)
                        sync.dma_start(out=gam_sb[:], in_=gam).then_inc(g("sgm"), 16)
                        for t8 in range(8):
                            sync.wait_ge(g("s3q"), max(0, t8 - 1))
                            sync.dma_start(
                                out=xtm3[:, (t8 % 2) * D:(t8 % 2 + 1) * D],
                                in_=xtm[t8 * 128:(t8 + 1) * 128, :],
                            ).then_inc(g("s3xa" if t8 % 2 == 0 else "s3xb"), 16)
                            for ot in (2 * t8, 2 * t8 + 1):
                                sync.wait_ge(g("sab"), max(0, ot - 1))
                                sync.dma_start(
                                    out=wstg3[:, (ot % 2) * D:(ot % 2 + 1) * D],
                                    in_=wfl[ot * 128:(ot + 1) * 128, :],
                                ).then_inc(g("swla" if ot % 2 == 0 else "swlb"), 16)
                        for ot in range(3):
                            sw = ("s4wa", "s4wb", "s4wc")[ot]
                            sync.dma_start(out=wo_st[:, ot * D:(ot + 1) * D],
                                           in_=wol[ot * 128:(ot + 1) * 128, :]).then_inc(g(sw), 16)
                            sync.dma_start(out=xr_st[:, ot * TSL:(ot + 1) * TSL],
                                           in_=xfm[ot * 128:(ot + 1) * 128, :]).then_inc(g(sw), 16)

                    @b3.scalar
                    def _(scalar):
                        scalar.wait_ge(g("sp"), GBTOT)
                        scalar.wait_ge(g("spv"), GBTOT)
                        def emit_abs(ot):
                            scalar.wait_ge(g("swla" if ot % 2 == 0 else "swlb"), 16 * (ot // 2 + 1))
                            if ot >= 2:
                                scalar.wait_ge(g("swm"), ot - 1)
                            scalar.drain()
                            scalar.activation(
                                wabs[:, (ot % 2) * D:(ot % 2 + 1) * D],
                                wstg3[:, (ot % 2) * D:(ot % 2 + 1) * D], AF.Abs,
                            ).then_inc(g("sab"), 1)

                        for t8 in range(8):
                            b = t8 % 2
                            scalar.wait_ge(g("s3a"), t8 + 1)
                            scalar.drain()
                            scalar.activation(loc[:, 3:4], loc[:, 2:3], AF.Ln, bias=eps5_128[:, 0:1])
                            scalar.drain()
                            scalar.activation(loc[:, 4:5], loc[:, 3:4], AF.Exp, scale=-0.5)
                            scalar.drain()
                            scalar.activation(loc[:, 8:9], loc[:, 0:1], AF.Identity, scale=loc[:, 4:5])
                            scalar.drain()
                            scalar.activation(loc[:, 9:10], loc[:, 8:9], AF.Identity, scale=-1.0)
                            scalar.wait_ge(g("s3xa" if t8 % 2 == 0 else "s3xb"), 16 * (t8 // 2 + 1))
                            scalar.drain()
                            scalar.activation(xn3[:], xtm3[:, b * D:(b + 1) * D], AF.Identity,
                                              scale=loc[:, 4:5], bias=loc[:, 9:10]).then_inc(g("s3b"), 1)
                            emit_abs(2 * t8)
                            scalar.wait_ge(g("s3c"), t8 + 1)
                            scalar.drain()
                            scalar.activation(loc[:, 6:7], loc[:, 5:6], AF.Ln, bias=eps6_128[:, 0:1], scale=1.0 / D)
                            scalar.drain()
                            scalar.activation(loc[:, 6:7], loc[:, 6:7], AF.Exp, scale=-0.5)
                            scalar.drain()
                            scalar.activation(xn2g[:], xn3[:], AF.Identity,
                                              scale=loc[:, 6:7]).then_inc(g("s3d"), 1)
                            emit_abs(2 * t8 + 1)

                    @b3.vector
                    def _(vector):
                        vector.wait_ge(g("spv"), GBTOT)
                        vector.wait_ge(g("se"), GBTOT)
                        for kpg in range(4):
                            vector.wait_ge(g("spg"), kpg + 1)
                            vector.tensor_copy(gam_b[:, kpg * 512:(kpg + 1) * 512], p_g3.ap()).then_inc(g("svg"), 1)
                        for t8 in range(8):
                            b = t8 % 2
                            if t8 >= 1:
                                vector.wait_ge(g("s3b"), t8)
                            vector.drain()
                            vector.tensor_scalar(loc[:, 0:1], stats_tm[:, t8:t8 + 1], 1.0 / D, None, ALU.mult)
                            vector.tensor_scalar(loc[:, 1:2], stats_tm[:, 8 + t8:9 + t8], 1.0 / D, None, ALU.mult)
                            vector.drain()
                            vector.tensor_tensor(loc[:, 2:3], loc[:, 0:1], loc[:, 0:1], ALU.mult)
                            vector.drain()
                            vector.tensor_tensor(loc[:, 2:3], loc[:, 1:2], loc[:, 2:3], ALU.subtract).then_inc(g("s3a"), 1)
                            vector.wait_ge(g("s3b"), t8 + 1)
                            vector.drain()
                            vector.tensor_tensor(xn2g[:], xn3[:], xn3[:], ALU.mult)
                            vector.drain()
                            vector.reduce_sum(loc[:, 5:6], xn2g[:], axis=AX.X).then_inc(g("s3c"), 1)
                            vector.wait_ge(g("s3d"), t8 + 1)
                            vector.drain()
                            vector.tensor_tensor(xn2g[:], xn2g[:], gam_b[:], ALU.mult)
                            vector.drain()
                            vector.reduce_max(loc[:, 7:8], xn2g[:], axis=AX.X, apply_absolute_value=True)
                            vector.drain()
                            vector.tensor_scalar(loc[:, 7:8], loc[:, 7:8], 1e-5, None, ALU.add)
                            vector.drain()
                            vector.tensor_scalar(inva[:, t8:t8 + 1], loc[:, 7:8], 1.0 / 127.0, None, ALU.mult)
                            vector.drain()
                            vector.reciprocal(loc[:, 5:6], loc[:, 7:8])
                            vector.drain()
                            vector.tensor_scalar(loc[:, 5:6], loc[:, 5:6], 127.0, None, ALU.mult)
                            vector.drain()
                            vector.tensor_scalar(xn3[:], xn2g[:], loc[:, 5:6], RC, ALU.mult, ALU.add)
                            if t8 >= 2:
                                vector.wait_ge(g("s3t2"), 2 * (t8 - 2) + 2)
                            vector.drain()
                            vector.tensor_scalar(xq_tm[:, b * D:(b + 1) * D], xn3[:], RC, None, ALU.subtract).then_inc(g("s3q"), 1)
                            for j in range(2):
                                vector.wait_ge(g("s3t2"), 2 * t8 + j + 1)
                                vector.tensor_copy(
                                    xq_fm.ap().rearrange("p (c t) -> p c t", c=16)[:, j * 8:(j + 1) * 8, t8 * 128:(t8 + 1) * 128],
                                    p_t3[j].ap().rearrange("p (c t) -> p c t", c=8),
                                ).then_inc(g("s3e"), 1)
                        vector.wait_ge(g("swm"), 16)
                        vector.drain()
                        vector.reduce_sum(wsc[0:1, 0:1], p_w3.ap(), axis=AX.X)
                        vector.drain()
                        vector.tensor_scalar(wsc[0:1, 1:2], wsc[0:1, 0:1], 1.0 / (D * D), 1e-5, ALU.mult, ALU.add)
                        vector.drain()
                        vector.reciprocal(wsc[0:1, 2:3], wsc[0:1, 1:2]).then_inc(g("sws1"), 1)
                        vector.wait_ge(g("spg"), 5)
                        vector.tensor_copy(wme_v[:], p_g3.ap()[:, 0:1]).then_inc(g("svg"), 1)
                        vector.wait_ge(g("spg"), 6)
                        vector.tensor_copy(wsc_v[:], p_g3.ap()[:, 0:1]).then_inc(g("svg"), 1)
                        vector.drain()
                        for t8 in range(8):
                            ts_ = vector.tensor_scalar(sevl[:, t8:t8 + 1], inva[:, t8:t8 + 1], wme_v[:, 0:1], None, ALU.mult)
                            if t8 == 7:
                                ts_.then_inc(g("ssl"), 1)
                        vector.wait_ge(g("sst3"), 1)
                        vector.drain()
                        vector.tensor_copy(sev_r[:], p_sv.ap()).then_inc(g("ssr"), 1)
                        for half in range(2):
                            vector.wait_ge(g("spg"), 7 + half)
                            vector.tensor_copy(sevac_b[:, half * 512:(half + 1) * 512], p_g3.ap()).then_inc(g("svg"), 1)
                        vector.nop().then_inc(g("sws"), 1)

                    @b3.tensor
                    def _(tensor):
                        tensor.wait_ge(g("se"), GBTOT)
                        tensor.wait_ge(g("sr"), NTILE)
                        tensor.wait_ge(g("sao"), NTILE)
                        tensor.wait_ge(g("sgm"), 16)
                        for kpg in range(4):
                            if kpg >= 1:
                                tensor.wait_ge(g("svg"), kpg)
                            tensor.matmul(p_g3.ap(), ones_f1[:], gam_sb[0:1, kpg * 512:(kpg + 1) * 512],
                                          start=True, stop=True).then_inc(g("spg"), 1)
                        for t8 in range(8):
                            b = t8 % 2
                            tensor.wait_ge(g("s3q"), t8 + 1)
                            for j in range(2):
                                if t8 >= 1:
                                    tensor.wait_ge(g("s3e"), 2 * (t8 - 1) + j + 1)
                                for cc in range(8):
                                    tr = tensor.transpose(
                                        p_t3[j].ap().rearrange("p (c t) -> p c t", c=8)[:, cc, :],
                                        xq_tm[:, b * D + (j * 8 + cc) * 128:b * D + (j * 8 + cc + 1) * 128],
                                        iden_b[:],
                                    )
                                    if cc == 7:
                                        tr.then_inc(g("s3t2"), 1)
                            for ot in (2 * t8, 2 * t8 + 1):
                                tensor.wait_ge(g("sab"), ot + 1)
                                for piece in range(4):
                                    mm = tensor.matmul(
                                        p_w3.ap(), ones_bf[:],
                                        wabs[:, (ot % 2) * D + piece * 512:(ot % 2) * D + (piece + 1) * 512],
                                        start=(ot == 0 and piece == 0), stop=(ot == 15 and piece == 3),
                                    )
                                    if piece == 3:
                                        mm.then_inc(g("swm"), 1)
                        tensor.wait_ge(g("sws1"), 1)
                        tensor.wait_ge(g("svg"), 4)
                        tensor.matmul(p_g3.ap()[:, 0:1], ones_f1[:], wsc[0:1, 1:2], start=True, stop=True).then_inc(g("spg"), 1)
                        tensor.wait_ge(g("svg"), 5)
                        tensor.matmul(p_g3.ap()[:, 0:1], ones_f1[:], wsc[0:1, 2:3], start=True, stop=True).then_inc(g("spg"), 1)
                        tensor.wait_ge(g("ssl"), 1)
                        for t8 in range(8):
                            tr = tensor.transpose(
                                p_sv.ap().rearrange("a (b c) -> a b c", b=8)[:, t8, :],
                                sevl[:, t8:t8 + 1], iden_f[:],
                            )
                            if t8 == 7:
                                tr.then_inc(g("sst3"), 1)
                        tensor.wait_ge(g("ssr"), 1)
                        for half in range(2):
                            tensor.wait_ge(g("svg"), 6 + half)
                            tensor.matmul(p_g3.ap(), ones_f1[:], sev_r[0:1, half * 512:(half + 1) * 512],
                                          start=True, stop=True).then_inc(g("spg"), 1)


            if stop < 4:
                return nc
            with contextlib.ExitStack() as c4:
                wf_st = c4.enter_context(nc.sbuf_tensor("wf_st", [128, 3 * D], F32))
                wo_bf = c4.enter_context(nc.sbuf_tensor("wo_bf", [128, 3 * D], BF16))
                wn_bf = c4.enter_context(nc.sbuf_tensor("wn_bf", [128, 3 * D], BF16))
                wnt2 = c4.enter_context(nc.sbuf_tensor("wnt2", [128, 2 * D], F32))
                t1_sb = c4.enter_context(nc.sbuf_tensor("t1_sb", [128, TSL], F32))
                f1_sb = c4.enter_context(nc.sbuf_tensor("f1_sb", [128, TSL], F32))
                o1_sb = c4.enter_context(nc.sbuf_tensor("o1_sb", [128, TSL], F32))
                out_sb = c4.enter_context(nc.sbuf_tensor("out_sb", [128, 2 * TSL], F32))
                p4 = [c4.enter_context(nc.psum_tensor(f"p4_{i}", [128, 512], F32)) for i in range(8)]

                with nc.Block() as b4:
                    @b4.gpsimd
                    def _(gpsimd):
                        for ot in range(16):
                            gpsimd.wait_ge(g("s4p"), max(0, ot - 2))
                            gpsimd.wait_ge(g("s4e"), max(0, ot - 2))
                            b = ot % 3
                            gpsimd.dma_start(
                                out=wf_st[:, b * D:(b + 1) * D],
                                in_=wfl[ot * 128:(ot + 1) * 128, :],
                            ).then_inc(g(("s4fa", "s4fb", "s4fc")[b]), 16)

                    @b4.sync
                    def _(sync):
                        sync.wait_ge(g("sws"), 1)
                        sync.wait_ge(g("s3e"), 16)
                        sync.wait_ge(g("s3t2"), 16)
                        sync.wait_ge(g("swm"), 16)
                        sync.wait_ge(g("svg"), 8)
                        def emit_out(ot):
                            sync.wait_ge(g("s4e"), ot + 1)
                            sync.dma_start(out=out[ot * 128:(ot + 1) * 128, :],
                                           in_=out_sb[:, (ot % 2) * TSL:(ot % 2 + 1) * TSL]
                                           ).then_inc(g("s4oa" if ot % 2 == 0 else "s4ob"), 16)
                        for ot in range(16):
                            b = ot % 3
                            sync.wait_ge(g("s4p"), max(0, ot - 2))
                            sync.wait_ge(g("s4e"), max(0, ot - 2))
                            if ot >= 3:
                                sw = ("s4wa", "s4wb", "s4wc")[b]
                                sync.dma_start(out=wo_st[:, b * D:(b + 1) * D],
                                               in_=wol[ot * 128:(ot + 1) * 128, :]).then_inc(g(sw), 16)
                                sync.dma_start(out=xr_st[:, b * TSL:(b + 1) * TSL],
                                               in_=xfm[ot * 128:(ot + 1) * 128, :]).then_inc(g(sw), 16)
                            if ot >= 2:
                                emit_out(ot - 2)
                        emit_out(14)
                        emit_out(15)
                        sync.wait_ge(g("s4oa"), 128)
                        sync.wait_ge(g("s4ob"), 128)

                    @b4.scalar
                    def _(scalar):
                        for ot in range(16):
                            b = ot % 3
                            sl = ot % 2
                            scalar.wait_ge(g(("s4wa", "s4wb", "s4wc")[b]), 32 * (ot // 3 + 1))
                            scalar.wait_ge(g(("s4fa", "s4fb", "s4fc")[b]), 16 * (ot // 3 + 1))
                            if ot >= 3:
                                scalar.wait_ge(g("s4g"), 4 * (ot - 2))
                            if ot >= 2:
                                scalar.wait_ge(g("s4p"), ot - 1)
                            scalar.drain()
                            scalar.activation(wo_bf[:, b * D:(b + 1) * D],
                                              wo_st[:, b * D:(b + 1) * D], AF.Identity)
                            scalar.activation(wnt2[:, sl * D:(sl + 1) * D],
                                              wf_st[:, b * D:(b + 1) * D], AF.Identity,
                                              scale=wsc_v[:, 0:1])
                            scalar.drain()
                            scalar.activation(wnt2[:, sl * D:(sl + 1) * D],
                                              wnt2[:, sl * D:(sl + 1) * D], AF.Identity,
                                              bias=rc_128[:, 0:1])
                            scalar.drain()
                            scalar.activation(wnt2[:, sl * D:(sl + 1) * D],
                                              wnt2[:, sl * D:(sl + 1) * D], AF.Identity,
                                              bias=nrc_128[:, 0:1]).then_inc(g("s4c"), 1)

                    @b4.vector
                    def _(vector):
                        for ot in range(16):
                            b = ot % 3
                            bp = ot % 2
                            vector.wait_ge(g("s4c"), ot + 1)
                            if ot >= 3:
                                vector.wait_ge(g("s4g"), 4 * (ot - 2))
                            vector.drain()
                            vector.tensor_scalar(wn_bf[:, b * D:(b + 1) * D],
                                                 wnt2[:, bp * D:(bp + 1) * D], 1.0, -1.0,
                                                 ALU.min, ALU.max).then_inc(g("s4p"), 1)
                            for th in range(2):
                                vector.wait_ge(g("s4g"), 4 * ot + th + 1)
                                vector.drain()
                                vector.tensor_tensor(f1_sb[:, th * 512:(th + 1) * 512],
                                                     p4[bp * 4 + th].ap(),
                                                     sevac_b[:, th * 512:(th + 1) * 512], ALU.mult)
                            for th in range(2):
                                vector.wait_ge(g("s4g"), 4 * ot + 2 + th + 1)
                                vector.drain()
                                vector.tensor_scalar(t1_sb[:, th * 512:(th + 1) * 512],
                                                     p4[bp * 4 + 2 + th].ap(), RC, RC, ALU.add, ALU.subtract)
                            vector.drain()
                            vector.scalar_tensor_tensor(o1_sb[:], t1_sb[:], 0.01, f1_sb[:], ALU.mult, ALU.add)
                            if ot >= 2:
                                vector.wait_ge(g("s4oa" if ot % 2 == 0 else "s4ob"), 16 * (ot // 2))
                            vector.drain()
                            vector.tensor_tensor(out_sb[:, bp * TSL:(bp + 1) * TSL], o1_sb[:],
                                                 xr_st[:, b * TSL:(b + 1) * TSL], ALU.add).then_inc(g("s4e"), 1)

                    @b4.tensor
                    def _(tensor):
                        for ot in range(16):
                            bp = ot % 2
                            tensor.wait_ge(g("s4p"), ot + 1)
                            if ot >= 2:
                                tensor.wait_ge(g("s4e"), ot - 1)
                            b3_ = ot % 3
                            for gi in range(4):
                                wsrc = wn_bf if gi < 2 else wo_bf
                                rsrc = xq_fm if gi < 2 else attn_g
                                if gi == 2 and ot == 0:
                                    tensor.wait_ge(g("sag"), 256)
                                th = gi % 2
                                for c in range(16):
                                    mm = tensor.matmul(
                                        p4[bp * 4 + gi].ap(),
                                        wsrc[:, b3_ * D + c * 128:b3_ * D + (c + 1) * 128],
                                        rsrc[:, c * TSL + th * 512:c * TSL + (th + 1) * 512],
                                        start=(c == 0), stop=(c == 15),
                                    )
                                    if c == 15:
                                        mm.then_inc(g("s4g"), 1)

    return nc


_NC_CACHE = {}


def kernel(L0_attn_x, L0_attn_Wq, L0_attn_Wk, L0_attn_Wv, L0_attn_Wo,
           L0_ffn_gamma, L0_ffn_W):
    x = np.asarray(L0_attn_x, dtype=np.float32).reshape(S, D)
    Wq = np.asarray(L0_attn_Wq, dtype=np.float32)
    Wk = np.asarray(L0_attn_Wk, dtype=np.float32)
    Wv = np.asarray(L0_attn_Wv, dtype=np.float32)
    Wo = np.asarray(L0_attn_Wo, dtype=np.float32)
    gamma = np.asarray(L0_ffn_gamma, dtype=np.float32).reshape(1, D)
    W = np.asarray(L0_ffn_W, dtype=np.float32)

    xt_full = np.ascontiguousarray(x.T)
    WqT = np.ascontiguousarray(Wq.T)
    WkT = np.ascontiguousarray(Wk.T)
    WvT = np.ascontiguousarray(Wv.T)

    def otile_layout(MT):
        blocks = []
        for ot in range(16):
            sl = MT[:, ot * 128:(ot + 1) * 128]
            blk = sl.reshape(16, 128, 128).transpose(1, 0, 2).reshape(128, 2048)
            blocks.append(blk)
        return np.ascontiguousarray(np.concatenate(blocks, axis=0))

    wol_h = otile_layout(np.ascontiguousarray(Wo.T))
    wfl_h = otile_layout(np.ascontiguousarray(W.T))

    iden_b = np.eye(128, dtype=ml_dtypes.bfloat16)
    iden_f = np.eye(128, dtype=np.float32)
    kk = np.arange(128)[:, None]
    qq = np.arange(256)[None, :]
    maskA = (kk <= qq).astype(ml_dtypes.bfloat16)
    maskB = ((kk + 128) <= qq).astype(ml_dtypes.bfloat16)
    mask = np.ascontiguousarray(np.concatenate([maskA, maskB], axis=1))

    in_maps = []
    for c in range(NCOR):
        hs = slice(c * DOUT, (c + 1) * DOUT)
        in_maps.append({
            "xt": xt_full,
            "xtm": np.ascontiguousarray(x[c * TSL:(c + 1) * TSL, :]),
            "xfm": np.ascontiguousarray(xt_full[:, c * TSL:(c + 1) * TSL]),
            "wq": np.ascontiguousarray(WqT[:, hs]),
            "wk": np.ascontiguousarray(WkT[:, hs]),
            "wv": np.ascontiguousarray(WvT[:, hs]),
            "wol": wol_h,
            "wfl": wfl_h,
            "gam": gamma,
            "idb": iden_b,
            "idf": iden_f,
            "mskab": mask,
            "coff": np.array([[c]], dtype=np.uint32),
        })

    nc = build()
    res = run_bass_kernel_spmd(nc, in_maps, core_ids=list(range(NCOR)))
    outT = np.concatenate([res.results[c]["out"] for c in range(NCOR)], axis=1)
    return np.ascontiguousarray(outT.T).reshape(1, S, D).astype(np.float32)


# version 18
# speedup vs baseline: 1.0127x; 1.0127x over previous
import contextlib

import numpy as np
import ml_dtypes
import concourse.bass as bass
import concourse.mybir as mybir
from concourse.bass_utils import run_bass_kernel_spmd

F32 = mybir.dt.float32
BF16 = mybir.dt.bfloat16
U32 = mybir.dt.uint32
AF = mybir.ActivationFunctionType
ALU = mybir.AluOpType
AX = mybir.AxisListType

S = 8192
D = 2048
NCOR = 8
TSL = S // NCOR
HPC = 2
DOUT = 256
T1 = 256
NST = S // T1
TQ = 256
NQT = S // TQ
NCH = S // 128
RC = 12582912.0
SQHD = float(np.sqrt(128.0))
CS = 1.0 / (1e4 * SQHD)
C2 = 0.5 * CS
NTILE = HPC * NQT
TOTAL_PAIRS = HPC * (NQT * (NQT + 1) // 2)


def _batch_plan():
    tiles_plan = []
    gb_count = 0
    for T in range(NTILE):
        h, qt = divmod(T, NQT)
        NP = qt + 1
        PB = h * (NQT * (NQT + 1) // 2) + qt * (qt + 1) // 2
        batches = []
        p = 0
        while p < NP:
            pl = [p] if p + 1 >= NP else [p, p + 1]
            batches.append((gb_count, pl))
            gb_count += 1
            p += len(pl)
        tiles_plan.append((h, qt, NP, PB, batches))
    return tiles_plan, gb_count


TILES_PLAN, GBTOT = _batch_plan()
GB_END_TILE = [TILES_PLAN[T][4][-1][0] + 1 for T in range(NTILE)]


def npairs(qt):
    return qt + 1


def pbase(h, qt):
    return h * (NQT * (NQT + 1) // 2) + qt * (qt + 1) // 2


def pend(T):
    if T < 0:
        return 0
    h, qt = divmod(T, NQT)
    return pbase(h, qt) + npairs(qt)


def build(stop=5, b1mode=3):
    nc = bass.Bass("TRN2", target_bir_lowering=False)

    xt = nc.dram_tensor("xt", [D, S], F32, kind="ExternalInput").ap()
    xtm = nc.dram_tensor("xtm", [TSL, D], F32, kind="ExternalInput").ap()
    xfm = nc.dram_tensor("xfm", [D, TSL], F32, kind="ExternalInput").ap()
    wq = nc.dram_tensor("wq", [D, DOUT], F32, kind="ExternalInput").ap()
    wk = nc.dram_tensor("wk", [D, DOUT], F32, kind="ExternalInput").ap()
    wv = nc.dram_tensor("wv", [D, DOUT], F32, kind="ExternalInput").ap()
    wol = nc.dram_tensor("wol", [16 * 128, D], F32, kind="ExternalInput").ap()
    wfl = nc.dram_tensor("wfl", [16 * 128, D], F32, kind="ExternalInput").ap()
    gam = nc.dram_tensor("gam", [1, D], F32, kind="ExternalInput").ap()
    idb = nc.dram_tensor("idb", [128, 128], BF16, kind="ExternalInput").ap()
    idf = nc.dram_tensor("idf", [128, 128], F32, kind="ExternalInput").ap()
    mskab = nc.dram_tensor("mskab", [128, 512], BF16, kind="ExternalInput").ap()
    coff = nc.dram_tensor("coff", [1, 1], U32, kind="ExternalInput").ap()
    out = nc.dram_tensor("out", [D, TSL], F32, kind="ExternalOutput").ap()

    sc_rstd = nc.dram_tensor("sc_rstd", [1, S], F32)
    sc_r1 = nc.dram_tensor("sc_r1", [1, S], BF16)
    sc_r100 = nc.dram_tensor("sc_r100", [2, 256], BF16)
    cc_st_in = nc.dram_tensor("cc_st_in", [2, TSL], F32)
    cc_st_out = nc.dram_tensor("cc_st_out", [16, TSL], F32)
    cc_at_in = nc.dram_tensor("cc_at_in", [DOUT, S], BF16)
    cc_at_out0 = nc.dram_tensor("cc_at_out0", [NCOR * 128, S], BF16, addr_space="Shared")
    cc_at_out1 = nc.dram_tensor("cc_at_out1", [NCOR * 128, S], BF16, addr_space="Shared")

    sem_names = [
        "sw0", "s0xa", "s0xb", "s0g", "s0v", "s0t", "s0c", "scc0", "s0r", "s0ch", "s0w",
        "swga", "swgb", "srsa", "srsb", "srsc", "srsd", "sxa", "sxb", "sxga", "sxgb", "sbc", "sRe", "sxs", "spj", "sev", "str_", "svt",
        "srs1", "srs2",
        "svq", "spn", "svr", "svm1", "spm", "svm",
        "sst", "se", "smask", "sz", "sr", "sbc2", "srb", "sp", "spv", "sao", "scwa", "scwb",
        "sgm", "s3xa", "s3xb", "s3a", "s3b", "s3c", "s3d", "s3q", "s3t2", "s3e",
        "swla", "swlb", "sab", "swm", "sws1", "sws",
        "spg", "svg", "ssl", "sst3", "ssr", "scc", "sag",
        "s4wa", "s4wb", "s4wc", "s4fa", "s4fb", "s4fc", "s4p", "s4g", "s4e", "s4oa", "s4ob", "s4c",
    ]

    with contextlib.ExitStack() as ctx:
        sems = {n: ctx.enter_context(nc.semaphore(n)) for n in sem_names}
        g = lambda n: sems[n]

        ones_bf = ctx.enter_context(nc.sbuf_tensor("ones_bf", [128, 1], BF16))
        ones_b1 = ctx.enter_context(nc.sbuf_tensor("ones_b1", [1, 128], BF16))
        ones_f1 = ctx.enter_context(nc.sbuf_tensor("ones_f1", [1, 128], F32))
        iden_b = ctx.enter_context(nc.sbuf_tensor("iden_b", [128, 128], BF16))
        iden_f = ctx.enter_context(nc.sbuf_tensor("iden_f", [128, 128], F32))
        maskab = ctx.enter_context(nc.sbuf_tensor("maskab", [128, 512], BF16))
        eps5_1 = ctx.enter_context(nc.sbuf_tensor("eps5_1", [1, 1], F32))
        eps5_128 = ctx.enter_context(nc.sbuf_tensor("eps5_128", [128, 1], F32))
        eps6_128 = ctx.enter_context(nc.sbuf_tensor("eps6_128", [128, 1], F32))
        rc_128 = ctx.enter_context(nc.sbuf_tensor("rc_128", [128, 1], F32))
        nrc_128 = ctx.enter_context(nc.sbuf_tensor("nrc_128", [128, 1], F32))
        stats_tm = ctx.enter_context(nc.sbuf_tensor("stats_tm", [128, 16], F32))
        s_coff = ctx.enter_context(nc.sbuf_tensor("s_coff", [1, 1], U32))

        with contextlib.ExitStack() as c12:
            q_sb = c12.enter_context(nc.sbuf_tensor("q_sb", [128, HPC * S], BF16))
            k_sb = c12.enter_context(nc.sbuf_tensor("k_sb", [128, HPC * S], BF16))
            vt_sb = c12.enter_context(nc.sbuf_tensor("vt_sb", [128, HPC * NCH * 128], BF16))

            with contextlib.ExitStack() as c1:
                xtm_t = c1.enter_context(nc.sbuf_tensor("xtm_t", [128, 2 * D], F32))
                st_tr = c1.enter_context(nc.sbuf_tensor("st_tr", [16, 128], F32))
                srA = c1.enter_context(nc.sbuf_tensor("srA", [128, 64], F32))
                srB = c1.enter_context(nc.sbuf_tensor("srB", [128, 64], F32))
                srC = c1.enter_context(nc.sbuf_tensor("srC", [128, 64], F32))
                sr1 = c1.enter_context(nc.sbuf_tensor("sr1", [128, 64], BF16))
                wstg = c1.enter_context(nc.sbuf_tensor("wstg", [128, 2 * DOUT], F32))
                wq_bf = c1.enter_context(nc.sbuf_tensor("wq_bf", [128, 16 * DOUT], BF16))
                wk_bf = c1.enter_context(nc.sbuf_tensor("wk_bf", [128, 16 * DOUT], BF16))
                wv_bf = c1.enter_context(nc.sbuf_tensor("wv_bf", [128, 16 * DOUT], BF16))
                rsum = c1.enter_context(nc.sbuf_tensor("rsum", [1, 3 * DOUT], BF16))
                xstage = c1.enter_context(nc.sbuf_tensor("xstage", [128, 2 * 16 * T1], F32))
                xs_sb = c1.enter_context(nc.sbuf_tensor("xs_sb", [128, 2 * 16 * T1], BF16))
                rsl = c1.enter_context(nc.sbuf_tensor("rsl", [1, 4 * T1], F32))
                r1sl = c1.enter_context(nc.sbuf_tensor("r1sl", [1, 4 * T1], BF16))
                R100b = c1.enter_context(nc.sbuf_tensor("R100b", [128, 2 * T1], F32))
                vtmp = c1.enter_context(nc.sbuf_tensor("vtmp", [128, 2 * 2 * T1], BF16))
                p_pj = [c1.enter_context(nc.psum_tensor(f"p_pj{i}", [128, T1], F32)) for i in range(6)]
                p_bc1 = c1.enter_context(nc.psum_tensor("p_bc1", [128, T1], F32))
                p_tr1 = c1.enter_context(nc.psum_tensor("p_tr1", [128, 512], BF16))

                with nc.Block() as b1:
                    @b1.sync
                    def _(sync):
                        sync.dma_start(out=iden_b[:], in_=idb).then_inc(g("sw0"), 16)
                        sync.dma_start(out=iden_f[:], in_=idf).then_inc(g("sw0"), 16)
                        sync.dma_start(out=maskab[:], in_=mskab).then_inc(g("sw0"), 16)
                        sync.dma_start(out=s_coff[:], in_=coff).then_inc(g("sw0"), 16)
                        for t8 in range(2):
                            sync.dma_start(
                                out=xtm_t[:, (t8 % 2) * D:(t8 % 2 + 1) * D],
                                in_=xtm[t8 * 128:(t8 + 1) * 128, :],
                            ).then_inc(g("s0xa" if t8 % 2 == 0 else "s0xb"), 16)
                        for pj, wsrc in enumerate((wq, wk, wv)):
                            for c in range(16):
                                i = pj * 16 + c
                                sync.wait_ge(g("srs1"), max(0, i - 1))
                                sync.dma_start(
                                    out=wstg[:, (i % 2) * DOUT:(i % 2 + 1) * DOUT],
                                    in_=wsrc[c * 128:(c + 1) * 128, :],
                                ).then_inc(g("swga" if i % 2 == 0 else "swgb"), 16)
                        for t8 in range(2, 8):
                            sync.wait_ge(g("s0v"), t8 - 1)
                            sync.dma_start(
                                out=xtm_t[:, (t8 % 2) * D:(t8 % 2 + 1) * D],
                                in_=xtm[t8 * 128:(t8 + 1) * 128, :],
                            ).then_inc(g("s0xa" if t8 % 2 == 0 else "s0xb"), 16)
                        sync.wait_ge(g("s0ch"), 2)
                        sync.dma_start(out=sc_rstd.ap().rearrange("a (p c) -> (a p) c", p=128),
                                       in_=srA[:]).then_inc(g("s0w"), 16)
                        sync.wait_ge(g("s0ch"), 3)
                        sync.dma_start(out=sc_r1.ap().rearrange("a (p c) -> (a p) c", p=128),
                                       in_=sr1[:]).then_inc(g("s0w"), 16)
                        for i in range(NST if b1mode >= 2 else 0):
                            if i < 2:
                                sync.wait_ge(g("s0v"), 8)
                            sync.wait_ge(g("sxs"), 16 * max(0, i - 1))
                            for c in range(8):
                                sync.dma_start(
                                    out=xstage[:, ((i % 2) * 16 + c) * T1:((i % 2) * 16 + c) * T1 + T1],
                                    in_=xt[c * 128:(c + 1) * 128, i * T1:(i + 1) * T1],
                                ).then_inc(g("sxa" if i % 2 == 0 else "sxb"), 16)

                    @b1.scalar
                    def _(scalar):
                        scalar.wait_ge(g("s0r"), 1)
                        scalar.activation(srB[:], srB[:], AF.Ln, bias=eps5_128[:, 0:1])
                        scalar.drain()
                        scalar.activation(srB[:], srB[:], AF.Exp, scale=-0.5).then_inc(g("s0ch"), 1)
                        scalar.wait_ge(g("s0w"), 32)
                        for i in range(NST if b1mode >= 2 else 0):
                            scalar.wait_ge(g("sbc"), max(0, i - 3))
                            if i >= 4:
                                scalar.wait_ge(g("spj"), 6 * (i - 3))
                            sw = ("srsa", "srsb", "srsc", "srsd")[i % 4]
                            scalar.dma_start(
                                out=rsl[:, (i % 4) * T1:(i % 4 + 1) * T1],
                                in_=sc_rstd.ap()[0:1, i * T1:(i + 1) * T1],
                            ).then_inc(g(sw), 16)
                            scalar.dma_start(
                                out=r1sl[:, (i % 4) * T1:(i % 4 + 1) * T1],
                                in_=sc_r1.ap()[0:1, i * T1:(i + 1) * T1],
                            ).then_inc(g(sw), 16)

                    @b1.gpsimd
                    def _(gpsimd):
                        gpsimd.wait_ge(g("s0c"), 1)
                        st2 = cc_st_in.ap().rearrange("s (t p) -> (s t) p", t=8)
                        gpsimd.dma_start(out=st2[0:8, :], in_=st_tr[0:8, :]).then_inc(g("s0g"), 16)
                        gpsimd.dma_start(out=st2[8:16, :], in_=st_tr[8:16, :]).then_inc(g("s0g"), 16)
                        gpsimd.wait_ge(g("s0g"), 32)
                        gpsimd.collective_compute(
                            "AllGather", ALU.bypass,
                            replica_groups=[list(range(NCOR))],
                            ins=[cc_st_in.ap()],
                            outs=[cc_st_out.ap()],
                        ).then_inc(g("scc0"), 1)
                        gpsimd.wait_ge(g("scc0"), 1)
                        for r in range(8):
                            gpsimd.dma_start(out=srA[r * 16:(r + 1) * 16, :],
                                             in_=cc_st_out.ap()[2 * r:2 * r + 1, :].rearrange("a (p c) -> (a p) c", p=16)).then_inc(g("s0g"), 16)
                            gpsimd.dma_start(out=srB[r * 16:(r + 1) * 16, :],
                                             in_=cc_st_out.ap()[2 * r + 1:2 * r + 2, :].rearrange("a (p c) -> (a p) c", p=16)).then_inc(g("s0g"), 16)
                        gpsimd.wait_ge(g("s0g"), 288)
                        gpsimd.nop().then_inc(g("scc0"), 1)
                        for i in range(NST if b1mode >= 2 else 0):
                            gpsimd.wait_ge(g("sxs"), 16 * max(0, i - 1))
                            for c in range(8, 16):
                                gpsimd.dma_start(
                                    out=xstage[:, ((i % 2) * 16 + c) * T1:((i % 2) * 16 + c) * T1 + T1],
                                    in_=xt[c * 128:(c + 1) * 128, i * T1:(i + 1) * T1],
                                ).then_inc(g("sxga" if i % 2 == 0 else "sxgb"), 16)



                    @b1.vector
                    def _(vector):
                        vector.memset(ones_bf[:], 1.0)
                        vector.memset(ones_b1[:], 1.0)
                        vector.memset(ones_f1[:], 1.0)
                        vector.memset(eps5_1[:], 1e-5)
                        vector.memset(eps5_128[:], 1e-5)
                        vector.memset(eps6_128[:], 1e-6)
                        vector.memset(rc_128[:], RC)
                        vector.memset(nrc_128[:], -RC)
                        for i in range(48):
                            vector.wait_ge(g("swga" if i % 2 == 0 else "swgb"), 16 * (i // 2 + 1))
                            dst = (wq_bf, wk_bf, wv_bf)[i // 16]
                            c = i % 16
                            vector.tensor_copy(
                                dst[:, c * DOUT:(c + 1) * DOUT],
                                wstg[:, (i % 2) * DOUT:(i % 2 + 1) * DOUT],
                            ).then_inc(g("srs1"), 1)
                        for t8 in range(8):
                            vector.wait_ge(g("s0xa" if t8 % 2 == 0 else "s0xb"), 16 * (t8 // 2 + 1))
                            xv = xtm_t[:, (t8 % 2) * D:(t8 % 2 + 1) * D]
                            vector.reduce_sum(stats_tm[:, t8:t8 + 1], xv, axis=AX.X)
                            vector.drain()
                            vector.tensor_tensor(xstage[:, 0:D], xv, xv, ALU.mult)
                            vector.drain()
                            vector.reduce_sum(stats_tm[:, 8 + t8:9 + t8], xstage[:, 0:D], axis=AX.X).then_inc(g("s0v"), 1)
                        vector.wait_ge(g("s0t"), 1)
                        vector.tensor_copy(st_tr[:], p_pj[5].ap()[0:16, 0:128]).then_inc(g("s0c"), 1)
                        vector.wait_ge(g("scc0"), 2)
                        vector.tensor_scalar(srC[:], srA[:], 1.0 / D, None, ALU.mult)
                        vector.tensor_scalar(srB[:], srB[:], 1.0 / D, None, ALU.mult)
                        vector.drain()
                        vector.tensor_tensor(srA[:], srC[:], srC[:], ALU.mult)
                        vector.drain()
                        vector.tensor_tensor(srB[:], srB[:], srA[:], ALU.subtract).then_inc(g("s0r"), 1)
                        vector.wait_ge(g("s0ch"), 1)
                        vector.drain()
                        vector.tensor_scalar(srA[:], srB[:], 100.0, None, ALU.mult).then_inc(g("s0ch"), 1)
                        vector.drain()
                        vector.tensor_tensor(srC[:], srC[:], srB[:], ALU.mult)
                        vector.drain()
                        vector.tensor_scalar(sr1[:], srC[:], -100.0, None, ALU.mult).then_inc(g("s0ch"), 1)
                        for pj in range(3):
                            vector.wait_ge(g("srs2"), pj + 1)
                            vector.tensor_copy(rsum[0:1, pj * DOUT:(pj + 1) * DOUT],
                                               p_pj[5].ap()[0:1, 0:DOUT]).then_inc(g("svt"), 1)
                        for i in range(NST if b1mode >= 2 else 0):
                            buf = i % 2
                            vector.wait_ge(g("sbc"), i + 1)
                            vector.drain()
                            vector.tensor_copy(R100b[:, buf * T1:(buf + 1) * T1], p_bc1.ap()).then_inc(g("sRe"), 1)
                            vector.wait_ge(g("sxa" if i % 2 == 0 else "sxb"), 128 * (i // 2 + 1))
                            vector.wait_ge(g("sxga" if i % 2 == 0 else "sxgb"), 128 * (i // 2 + 1))
                            if i >= 2:
                                vector.wait_ge(g("spj"), 6 * (i - 1))
                            vector.drain()
                            for c in range(16):
                                vector.tensor_tensor(
                                    xs_sb[:, (buf * 16 + c) * T1:(buf * 16 + c) * T1 + T1],
                                    xstage[:, (buf * 16 + c) * T1:(buf * 16 + c) * T1 + T1],
                                    R100b[:, buf * T1:(buf + 1) * T1],
                                    ALU.mult,
                                ).then_inc(g("sxs"), 1)

                            for gi in range(6):
                                pj, dt = divmod(gi, 2)
                                vector.wait_ge(g("spj"), 6 * i + gi + 1)
                                if pj < 2:
                                    dstt = (q_sb, k_sb)[pj]
                                    od = dstt[:, dt * S + i * T1:dt * S + (i + 1) * T1]
                                else:
                                    od = vtmp[:, (buf * 2 + dt) * T1:(buf * 2 + dt) * T1 + T1]
                                    if i >= 2 and b1mode >= 3:
                                        vector.wait_ge(g("str_"), 2 * (i - 2) + 2)
                                vector.tensor_scalar(od, p_pj[gi].ap(), RC, RC,
                                                     ALU.add, ALU.subtract).then_inc(g("sev"), 1)
                            if i >= 1 and b1mode >= 3:
                                for h in range(2):
                                    vector.wait_ge(g("str_"), 2 * (i - 1) + h + 1)
                                    vector.tensor_copy(
                                        vt_sb[:, (h * NCH + (i - 1) * 2) * 128:(h * NCH + (i - 1) * 2) * 128 + 256],
                                        p_tr1.ap()[:, h * 256:h * 256 + 256] if False else p_tr1.ap().rearrange("p (a b) -> p a b", a=2)[:, h, :],
                                    ).then_inc(g("svt"), 1)
                        for h in range(2 if b1mode >= 3 else 0):
                            vector.wait_ge(g("str_"), 2 * (NST - 1) + h + 1)
                            vector.tensor_copy(
                                vt_sb[:, (h * NCH + (NST - 1) * 2) * 128:(h * NCH + (NST - 1) * 2) * 128 + 256],
                                p_tr1.ap().rearrange("p (a b) -> p a b", a=2)[:, h, :],
                            ).then_inc(g("svt"), 1)

                    @b1.tensor
                    def _(tensor):
                        tensor.wait_ge(g("s0v"), 8)
                        tensor.wait_ge(g("sw0"), 64)
                        tensor.transpose(p_pj[5].ap()[0:16, 0:128], stats_tm[:], iden_f[:]).then_inc(g("s0t"), 1)
                        tensor.wait_ge(g("s0c"), 1)
                        for pj, wbf in enumerate((wq_bf, wk_bf, wv_bf)):
                            tensor.wait_ge(g("srs1"), 16 * (pj + 1))
                            tensor.wait_ge(g("svt"), pj)
                            for c in range(16):
                                mm = tensor.matmul(p_pj[5].ap()[0:1, 0:DOUT], ones_bf[:],
                                                   wbf[:, c * DOUT:(c + 1) * DOUT],
                                                   start=(c == 0), stop=(c == 15))
                                if c == 15:
                                    mm.then_inc(g("srs2"), 1)
                        def emit_bc1(i):
                            tensor.wait_ge(g(("srsa", "srsb", "srsc", "srsd")[i % 4]), 32 * (i // 4 + 1))
                            if i >= 1:
                                tensor.wait_ge(g("sRe"), i)
                            tensor.matmul(p_bc1.ap(), ones_f1[:],
                                          rsl[0:1, (i % 4) * T1:(i % 4 + 1) * T1],
                                          start=True, stop=True).then_inc(g("sbc"), 1)

                        if b1mode >= 2:
                            emit_bc1(0)
                            emit_bc1(1)
                        for i in range(NST if b1mode >= 2 else 0):
                            buf = i % 2
                            if i == 0:
                                tensor.wait_ge(g("s0c"), 1)
                            for gi in range(6):
                                pj, dt = divmod(gi, 2)
                                wbf = (wq_bf, wk_bf, wv_bf)[pj]
                                if i >= 1:
                                    tensor.wait_ge(g("sev"), 6 * (i - 1) + gi + 1)
                                for c in range(16):
                                    if gi == 0:
                                        tensor.wait_ge(g("sxs"), 16 * i + c + 1)
                                    tensor.matmul(
                                        p_pj[gi].ap(),
                                        wbf[:, c * DOUT + dt * 128:c * DOUT + dt * 128 + 128],
                                        xs_sb[:, (buf * 16 + c) * T1:(buf * 16 + c) * T1 + T1],
                                        start=(c == 0), stop=False,
                                    )
                                tensor.matmul(
                                    p_pj[gi].ap(),
                                    rsum[0:1, pj * DOUT + dt * 128:pj * DOUT + dt * 128 + 128],
                                    r1sl[0:1, (i % 4) * T1:(i % 4 + 1) * T1],
                                    start=False, stop=True,
                                ).then_inc(g("spj"), 1)
                            if i >= 1 and b1mode >= 3:
                                ip = i - 1
                                for h in range(2):
                                    tensor.wait_ge(g("sev"), 6 * ip + 4 + h + 1)
                                    tensor.wait_ge(g("svt"), 3 + 2 * ip + h)
                                    for j in range(2):
                                        tr = tensor.transpose(
                                            p_tr1.ap().rearrange("p (a b) -> p a b", a=4)[:, 2 * h + j, :],
                                            vtmp[:, ((ip % 2) * 2 + h) * T1 + j * 128:((ip % 2) * 2 + h) * T1 + (j + 1) * 128],
                                            iden_b[:],
                                        )
                                        if j == 1:
                                            tr.then_inc(g("str_"), 1)
                            if i + 2 < NST:
                                emit_bc1(i + 2)
                        ip = NST - 1
                        for h in range(2 if b1mode >= 3 else 0):
                            tensor.wait_ge(g("sev"), 6 * ip + 4 + h + 1)
                            tensor.wait_ge(g("svt"), 3 + 2 * ip + h)
                            for j in range(2):
                                tr = tensor.transpose(
                                    p_tr1.ap().rearrange("p (a b) -> p a b", a=4)[:, 2 * h + j, :],
                                    vtmp[:, ((ip % 2) * 2 + h) * T1 + j * 128:((ip % 2) * 2 + h) * T1 + (j + 1) * 128],
                                    iden_b[:],
                                )
                                if j == 1:
                                    tr.then_inc(g("str_"), 1)

            if stop < 2:
                return nc
            with contextlib.ExitStack() as c2:
                estrip = [c2.enter_context(nc.sbuf_tensor(f"estrip{i}", [128, NQT * 512], BF16)) for i in range(2)]
                dstrip = [c2.enter_context(nc.sbuf_tensor(f"dstrip{i}", [128, 512], BF16)) for i in range(2)]
                R100a = [c2.enter_context(nc.sbuf_tensor(f"R100a{i}", [128, 1024], BF16)) for i in range(2)]
                r100v = c2.enter_context(nc.sbuf_tensor("r100v", [1, 256], F32))
                r100b_ = c2.enter_context(nc.sbuf_tensor("r100b_", [1, 256], BF16))
                t_scr = [c2.enter_context(nc.sbuf_tensor(f"t_scr{i}", [128, 1024], BF16)) for i in range(2)]
                p_scr = [c2.enter_context(nc.sbuf_tensor(f"p_scr{i}", [128, 1024], BF16)) for i in range(2)]
                ao_sb = [c2.enter_context(nc.sbuf_tensor(f"ao_sb{i}", [128, TQ], BF16)) for i in range(2)]
                qsq = c2.enter_context(nc.sbuf_tensor("qsq", [128, 1024], BF16))
                nrm = c2.enter_context(nc.sbuf_tensor("nrm", [1, 8], F32))
                aqrow = c2.enter_context(nc.sbuf_tensor("aqrow", [1, NQT], F32))
                lrow = c2.enter_context(nc.sbuf_tensor("lrow", [1, NQT], F32))
                negmg_all = c2.enter_context(nc.sbuf_tensor("negmg_all", [128, NTILE], F32))

                p_sA = c2.enter_context(nc.psum_tensor("p_sA", [128, 1024], F32))
                p_sB = c2.enter_context(nc.psum_tensor("p_sB", [128, 1024], F32))
                p_z = c2.enter_context(nc.psum_tensor("p_z", [1, 256], F32))
                p_o = [c2.enter_context(nc.psum_tensor(f"p_o{i}", [128, TQ], F32)) for i in range(2)]
                p_b2 = c2.enter_context(nc.psum_tensor("p_b2", [128, 512], F32))

                tiles_plan = TILES_PLAN
                gb_end_tile = GB_END_TILE

                def strip_of(T):
                    return estrip[T % 2]

                pro_steps = []
                for h in range(HPC):
                    for kind in ("q", "k", "d"):
                        for i16 in range(16):
                            pro_steps.append((h, kind, i16))
                    for g16 in range(16):
                        pro_steps.append((h, "b", g16))
                NPRO = len(pro_steps)

                def pro_bank(si):
                    return p_sA.ap()[:, (si % 2) * 512:(si % 2) * 512 + 512]

                def emit_zlast(tensor, Tp):
                    hp, qtp = divmod(Tp, NQT)
                    NPp = npairs(qtp)
                    tensor.wait_ge(g("smask"), Tp + 1)
                    if NPp == 1 and Tp > 0:
                        tensor.wait_ge(g("sr"), Tp)
                    for half in range(2):
                        mm = tensor.matmul(p_z.ap(), ones_bf[:],
                                           dstrip[Tp % 2][:, half * 256:half * 256 + 256],
                                           start=(NPp == 1 and half == 0), stop=(half == 1))
                        if half == 1:
                            mm.then_inc(g("sz"), 1)

                with nc.Block() as b2:
                    @b2.tensor
                    def _(tensor):
                        tensor.wait_ge(g("sev"), 6 * NST)
                        tensor.wait_ge(g("svt"), 3 + 2 * NST)
                        for si, (h, kind, idx) in enumerate(pro_steps):
                            tensor.wait_ge(g("svq"), si + 1)
                            if si >= 2:
                                tensor.wait_ge(g("svr"), si - 1)
                            if kind != "b":
                                tensor.matmul(
                                    pro_bank(si)[0:1, :],
                                    ones_bf[:], qsq[:, (si % 2) * 512:(si % 2) * 512 + 512],
                                    start=True, stop=True,
                                ).then_inc(g("spn"), 1)
                            else:
                                tensor.matmul(
                                    pro_bank(si)[:, 0:1],
                                    ones_f1[:], nrm[0:1, 3:4],
                                    start=True, stop=True,
                                ).then_inc(g("spn"), 1)
                        def emit_st(T, gb, pl):
                            h, qt, NP, PB, _ = tiles_plan[T]
                            if gb >= 2:
                                tensor.wait_ge(g("se"), gb - 1)
                            if gb == 0:
                                tensor.wait_ge(g("svr"), NPRO)
                            bank = (p_sA, p_sB)[gb % 2]
                            for j, p in enumerate(pl):
                                for half in range(2):
                                    c = 2 * p + half
                                    mm = tensor.matmul(
                                        bank.ap()[:, (2 * j + half) * 256:(2 * j + half + 1) * 256],
                                        k_sb[:, h * S + c * 128:h * S + (c + 1) * 128],
                                        q_sb[:, h * S + qt * TQ:h * S + (qt + 1) * TQ],
                                        start=True, stop=True,
                                    )
                                    if j == len(pl) - 1 and half == 1:
                                        mm.then_inc(g("sst"), 1)

                        def emit_zmm(T, gb, pl):
                            h, qt, NP, PB, _ = tiles_plan[T]
                            zps = [p for p in pl if p < NP - 1]
                            if not zps:
                                return
                            tensor.wait_ge(g("se"), gb + 1)
                            if pl[0] == 0 and T > 0:
                                tensor.wait_ge(g("sr"), T)
                            for p in zps:
                                for half in range(2):
                                    tensor.matmul(
                                        p_z.ap(), ones_bf[:],
                                        strip_of(T)[:, (2 * p + half) * 256:(2 * p + half + 1) * 256],
                                        start=(p == 0 and half == 0), stop=False,
                                    )

                        def emit_bcast2(Tp):
                            tensor.wait_ge(g("sr"), Tp + 1)
                            if Tp >= 1:
                                tensor.wait_ge(g("srb"), Tp)
                            for half in range(2):
                                mm = tensor.matmul(
                                    p_b2.ap().rearrange("p (a b) -> p a b", a=2)[:, half, :],
                                    ones_b1[:], r100b_[0:1, :], start=True, stop=True)
                                if half == 1:
                                    mm.then_inc(g("sbc2"), 1)

                        def emit_pv(Tp, gb, pl, first, last_of_tile):
                            hp, qtp, NPp, PBp, _ = tiles_plan[Tp]
                            if first and Tp >= 2:
                                tensor.wait_ge(g("sao"), Tp - 1)
                            tensor.wait_ge(g("sp"), gb + 1)
                            for j, p in enumerate(pl):
                                for half in range(2):
                                    c = 2 * p + half
                                    mm = tensor.matmul(
                                        p_o[Tp % 2].ap(),
                                        vt_sb[:, (hp * NCH + c) * 128:(hp * NCH + c) * 128 + 128],
                                        p_scr[gb % 2][:, (2 * j + half) * 256:(2 * j + half + 1) * 256],
                                        start=(first and j == 0 and half == 0),
                                        stop=(p == NPp - 1 and half == 1),
                                    )
                                    if j == len(pl) - 1 and half == 1:
                                        mm.then_inc(g("spv"), 1)

                        for T in range(NTILE):
                            h, qt, NP, PB, batches = tiles_plan[T]
                            if T > 0:
                                emit_zlast(tensor, T - 1)
                            pend_ops = []
                            if T > 0:
                                pend_ops.append(("bc", T - 1))
                                pb = tiles_plan[T - 1][4]
                                for bi, (gbp, plp) in enumerate(pb):
                                    pend_ops.append(("pv", T - 1, gbp, plp, bi == 0))
                            for ki, (gb, pl) in enumerate(batches):
                                emit_st(T, gb, pl)
                                if ki >= 1:
                                    emit_zmm(T, *batches[ki - 1])
                                if pend_ops:
                                    op = pend_ops.pop(0)
                                    if op[0] == "bc":
                                        emit_bcast2(op[1])
                                    else:
                                        emit_pv(op[1], op[2], op[3], op[4], False)
                            emit_zmm(T, *batches[-1])
                            for op in pend_ops:
                                if op[0] == "bc":
                                    emit_bcast2(op[1])
                                else:
                                    emit_pv(op[1], op[2], op[3], op[4], False)
                        emit_zlast(tensor, NTILE - 1)
                        Tp = NTILE - 1
                        emit_bcast2(Tp)
                        for bi, (gbp, plp) in enumerate(tiles_plan[Tp][4]):
                            emit_pv(Tp, gbp, plp, bi == 0, False)

                    @b2.scalar
                    def _(scalar):
                        scalar.wait_ge(g("sxs"), 16 * NST)
                        scalar.wait_ge(g("spj"), 6 * NST)
                        for T in range(NTILE):
                            h, qt, NP, PB, batches = tiles_plan[T]
                            if qt == 0:
                                scalar.wait_ge(g("svm"), h + 1)
                            if T >= 2:
                                scalar.wait_ge(g("sp"), gb_end_tile[T - 2])
                            for gb, pl in batches:
                                scalar.wait_ge(g("sst"), gb + 1)
                                scalar.drain()
                                scalar.activation(
                                    strip_of(T)[:, pl[0] * 512:(pl[-1] + 1) * 512],
                                    (p_sA, p_sB)[gb % 2].ap()[:, 0:len(pl) * 512],
                                    AF.Exp, scale=CS, bias=negmg_all[:, (T // 2):(T // 2) + 1],
                                ).then_inc(g("se"), 1)
                            scalar.wait_ge(g("sbc2"), T + 1)
                            if T >= 2:
                                scalar.wait_ge(g("sp"), gb_end_tile[T - 2])
                            scalar.drain()
                            scalar.activation(R100a[T % 2][:, 0:512], p_b2.ap(), AF.Identity)
                            scalar.drain()
                            scalar.activation(R100a[T % 2][:, 512:1024], p_b2.ap(), AF.Identity).then_inc(g("srb"), 1)
                            if T >= 1:
                                scalar.wait_ge(g("spv"), gb_end_tile[T - 1])
                                if T >= 3:
                                    scalar.wait_ge(g("scwa" if (T - 1) % 2 == 0 else "scwb"),
                                                   16 * ((T - 1) // 2))
                                scalar.drain()
                                scalar.activation(ao_sb[(T - 1) % 2][:], p_o[(T - 1) % 2].ap(),
                                                  AF.Identity, scale=0.01).then_inc(g("sao"), 1)
                        T = NTILE - 1
                        scalar.wait_ge(g("spv"), gb_end_tile[T])
                        scalar.wait_ge(g("scwa" if T % 2 == 0 else "scwb"), 16 * (T // 2))
                        scalar.drain()
                        scalar.activation(ao_sb[T % 2][:], p_o[T % 2].ap(), AF.Identity,
                                          scale=0.01).then_inc(g("sao"), 1)

                    @b2.vector
                    def _(vector):
                        vector.wait_ge(g("spj"), 6 * NST)
                        vector.wait_ge(g("str_"), 2 * NST)
                        vector.drain()
                        vector.memset(nrm[:], 0.0)
                        for si, (h, kind, idx) in enumerate(pro_steps):
                            if si >= 2:
                                vector.wait_ge(g("spn"), si - 1)
                            vector.drain()
                            if kind == "q":
                                sl = q_sb[:, h * S + idx * 512:h * S + (idx + 1) * 512]
                                vector.tensor_tensor(qsq[:, (si % 2) * 512:(si % 2) * 512 + 512],
                                                     sl, sl, ALU.mult).then_inc(g("svq"), 1)
                            elif kind == "k":
                                sl = k_sb[:, h * S + idx * 512:h * S + (idx + 1) * 512]
                                vector.tensor_tensor(qsq[:, (si % 2) * 512:(si % 2) * 512 + 512],
                                                     sl, sl, ALU.mult).then_inc(g("svq"), 1)
                            elif kind == "d":
                                qsl = q_sb[:, h * S + idx * 512:h * S + (idx + 1) * 512]
                                ksl = k_sb[:, h * S + idx * 512:h * S + (idx + 1) * 512]
                                vector.tensor_tensor(qsq[:, (si % 2) * 512:(si % 2) * 512 + 512],
                                                     qsl, ksl, ALU.mult).then_inc(g("svq"), 1)
                            else:
                                vector.tensor_tensor(nrm[0:1, 4:5], aqrow[0:1, idx:idx + 1],
                                                     nrm[0:1, 1:2], ALU.add)
                                vector.drain()
                                vector.tensor_scalar(nrm[0:1, 4:5], nrm[0:1, 4:5], C2, None, ALU.mult)
                                vector.tensor_scalar(nrm[0:1, 5:6], lrow[0:1, idx:idx + 1], CS, 80.0,
                                                     ALU.mult, ALU.add)
                                vector.drain()
                                vector.tensor_tensor(nrm[0:1, 4:5], nrm[0:1, 4:5], nrm[0:1, 5:6], ALU.min)
                                vector.drain()
                                vector.tensor_scalar(nrm[0:1, 3:4], nrm[0:1, 4:5], -1.0, None,
                                                     ALU.mult).then_inc(g("svq"), 1)
                            vector.wait_ge(g("spn"), si + 1)
                            vector.drain()
                            if kind == "q":
                                vector.reduce_max(aqrow[0:1, idx:idx + 1], pro_bank(si)[0:1, :],
                                                  axis=AX.X).then_inc(g("svr"), 1)
                            elif kind == "k":
                                vector.reduce_max(nrm[0:1, 6:7], pro_bank(si)[0:1, :], axis=AX.X)
                                vector.drain()
                                vector.tensor_tensor(nrm[0:1, 1:2], nrm[0:1, 1:2], nrm[0:1, 6:7],
                                                     ALU.max).then_inc(g("svr"), 1)
                            elif kind == "d":
                                vector.tensor_reduce(lrow[0:1, idx:idx + 1], pro_bank(si)[0:1, :],
                                                     AX.X, ALU.min).then_inc(g("svr"), 1)
                            else:
                                vector.tensor_copy(negmg_all[:, h * 16 + idx:h * 16 + idx + 1],
                                                   pro_bank(si)[:, 0:1]).then_inc(g("svr"), 1)
                            if kind == "b" and idx == 15:
                                vector.nop().then_inc(g("svm"), 1)
                        for T in range(NTILE):
                            h, qt, NP, PB, batches = tiles_plan[T]
                            vector.wait_ge(g("se"), gb_end_tile[T])
                            if T >= 2:
                                vector.wait_ge(g("sz"), T - 1)
                            vector.drain()
                            vector.tensor_tensor(
                                dstrip[T % 2][:],
                                strip_of(T)[:, (NP - 1) * 512:NP * 512],
                                maskab[:], ALU.mult,
                            ).then_inc(g("smask"), 1)
                            vector.wait_ge(g("sz"), T + 1)
                            vector.drain()
                            vector.reciprocal(r100v[:], p_z.ap())
                            vector.drain()
                            vector.tensor_scalar(r100b_[:], r100v[:], 100.0, None, ALU.mult).then_inc(g("sr"), 1)

                            vector.wait_ge(g("srb"), T + 1)
                            for gb, pl in batches:
                                if gb >= 2:
                                    vector.wait_ge(g("spv"), gb - 1)
                                vector.drain()
                                wid = len(pl) * 512
                                has_diag = (pl[-1] == NP - 1)
                                Ra = R100a[T % 2]
                                if has_diag and len(pl) == 2:
                                    vector.tensor_tensor(t_scr[gb % 2][:, 0:512],
                                                         strip_of(T)[:, pl[0] * 512:(pl[0] + 1) * 512],
                                                         Ra[:, 0:512], ALU.mult)
                                    vector.drain()
                                    vector.tensor_tensor(t_scr[gb % 2][:, 512:1024],
                                                         dstrip[T % 2][:], Ra[:, 0:512], ALU.mult)
                                elif has_diag:
                                    vector.tensor_tensor(t_scr[gb % 2][:, 0:512],
                                                         dstrip[T % 2][:], Ra[:, 0:512], ALU.mult)
                                else:
                                    vector.tensor_tensor(t_scr[gb % 2][:, 0:wid],
                                                         strip_of(T)[:, pl[0] * 512:(pl[-1] + 1) * 512],
                                                         Ra[:, 0:wid], ALU.mult)
                                vector.drain()
                                vector.tensor_scalar(p_scr[gb % 2][:, 0:wid], t_scr[gb % 2][:, 0:wid],
                                                     RC, RC, ALU.add, ALU.subtract).then_inc(g("sp"), 1)


                    @b2.sync
                    def _(sync):
                        for T in range(NTILE):
                            h, qt = divmod(T, NQT)
                            sync.wait_ge(g("sao"), T + 1)
                            sync.dma_start(
                                out=cc_at_in.ap()[h * 128:(h + 1) * 128, qt * TQ:(qt + 1) * TQ],
                                in_=ao_sb[T % 2][:],
                            ).then_inc(g("scwa" if T % 2 == 0 else "scwb"), 16)

                    @b2.gpsimd
                    def _(gpsimd):
                        gpsimd.wait_ge(g("scwa"), 16 * 16)
                        gpsimd.wait_ge(g("scwb"), 16 * 16)
                        gpsimd.collective_compute(
                            "AllGather", ALU.bypass,
                            replica_groups=[list(range(NCOR))],
                            ins=[cc_at_in.ap()[0:128, :]],
                            outs=[cc_at_out0.ap()],
                        ).then_inc(g("scc"), 1)
                        gpsimd.wait_ge(g("scwa"), 16 * 32)
                        gpsimd.wait_ge(g("scwb"), 16 * 32)
                        gpsimd.collective_compute(
                            "AllGather", ALU.bypass,
                            replica_groups=[list(range(NCOR))],
                            ins=[cc_at_in.ap()[128:256, :]],
                            outs=[cc_at_out1.ap()],
                        ).then_inc(g("scc"), 1)

        if stop < 3:
            return nc
        with contextlib.ExitStack() as c34:
            xq_fm = c34.enter_context(nc.sbuf_tensor("xq_fm", [128, 16 * TSL], BF16))
            sevac_b = c34.enter_context(nc.sbuf_tensor("sevac_b", [128, TSL], F32))
            wsc_v = c34.enter_context(nc.sbuf_tensor("wsc_v", [128, 1], F32))
            wme_v = c34.enter_context(nc.sbuf_tensor("wme_v", [128, 1], F32))
            attn_g = c34.enter_context(nc.sbuf_tensor("attn_g", [128, 16 * TSL], BF16))
            wo_st = c34.enter_context(nc.sbuf_tensor("wo_st", [128, 3 * D], F32))
            xr_st = c34.enter_context(nc.sbuf_tensor("xr_st", [128, 3 * TSL], F32))

            with contextlib.ExitStack() as c3:
                xtm3 = c3.enter_context(nc.sbuf_tensor("xtm3", [128, 2 * D], F32))
                xn3 = c3.enter_context(nc.sbuf_tensor("xn3", [128, D], F32))
                xn2g = c3.enter_context(nc.sbuf_tensor("xn2g", [128, D], F32))
                xq_tm = c3.enter_context(nc.sbuf_tensor("xq_tm", [128, 2 * D], BF16))
                gam_sb = c3.enter_context(nc.sbuf_tensor("gam_sb", [1, D], F32))
                gam_b = c3.enter_context(nc.sbuf_tensor("gam_b", [128, D], F32))
                loc = c3.enter_context(nc.sbuf_tensor("loc", [128, 12], F32))
                inva = c3.enter_context(nc.sbuf_tensor("inva", [128, 8], F32))
                sevl = c3.enter_context(nc.sbuf_tensor("sevl", [128, 8], F32))
                wstg3 = c3.enter_context(nc.sbuf_tensor("wstg3", [128, 2 * D], F32))
                wabs = c3.enter_context(nc.sbuf_tensor("wabs", [128, 2 * D], BF16))
                wsc = c3.enter_context(nc.sbuf_tensor("wsc", [1, 4], F32))
                sev_r = c3.enter_context(nc.sbuf_tensor("sev_r", [1, TSL], F32))

                p_t3 = [c3.enter_context(nc.psum_tensor(f"p_t3{i}", [128, 1024], BF16)) for i in range(2)]
                p_w3 = c3.enter_context(nc.psum_tensor("p_w3", [1, 512], F32))
                p_sv = c3.enter_context(nc.psum_tensor("p_sv", [1, TSL], F32))
                p_g3 = c3.enter_context(nc.psum_tensor("p_g3", [128, 512], F32))

                with nc.Block() as b3:
                    @b3.gpsimd
                    def _(gpsimd):
                        gpsimd.wait_ge(g("sw0"), 64)
                        with gpsimd.register("roff") as roff:
                            gpsimd.reg_load(roff, s_coff[0:1, 0:1])
                            off = gpsimd.snap(roff)
                            gpsimd.wait_ge(g("scc"), 1)
                            for c in range(0, 16, 2):
                                cg = c // 2
                                gpsimd.dma_start(
                                    out=attn_g[:, c * TSL:(c + 1) * TSL],
                                    in_=cc_at_out0.ap()[cg * 128:(cg + 1) * 128, bass.ts(off, TSL)],
                                ).then_inc(g("sag"), 16)
                            gpsimd.wait_ge(g("scc"), 2)
                            for c in range(1, 16, 2):
                                cg = c // 2
                                gpsimd.dma_start(
                                    out=attn_g[:, c * TSL:(c + 1) * TSL],
                                    in_=cc_at_out1.ap()[cg * 128:(cg + 1) * 128, bass.ts(off, TSL)],
                                ).then_inc(g("sag"), 16)

                    @b3.sync
                    def _(sync):
                        sync.wait_ge(g("sp"), GBTOT)
                        sync.wait_ge(g("spv"), GBTOT)
                        sync.wait_ge(g("sao"), NTILE)
                        sync.wait_ge(g("se"), GBTOT)
                        sync.dma_start(out=gam_sb[:], in_=gam).then_inc(g("sgm"), 16)
                        for t8 in range(8):
                            sync.wait_ge(g("s3q"), max(0, t8 - 1))
                            sync.dma_start(
                                out=xtm3[:, (t8 % 2) * D:(t8 % 2 + 1) * D],
                                in_=xtm[t8 * 128:(t8 + 1) * 128, :],
                            ).then_inc(g("s3xa" if t8 % 2 == 0 else "s3xb"), 16)
                            for ot in (2 * t8, 2 * t8 + 1):
                                sync.wait_ge(g("sab"), max(0, ot - 1))
                                sync.dma_start(
                                    out=wstg3[:, (ot % 2) * D:(ot % 2 + 1) * D],
                                    in_=wfl[ot * 128:(ot + 1) * 128, :],
                                ).then_inc(g("swla" if ot % 2 == 0 else "swlb"), 16)
                        for ot in range(3):
                            sw = ("s4wa", "s4wb", "s4wc")[ot]
                            sync.dma_start(out=wo_st[:, ot * D:(ot + 1) * D],
                                           in_=wol[ot * 128:(ot + 1) * 128, :]).then_inc(g(sw), 16)
                            sync.dma_start(out=xr_st[:, ot * TSL:(ot + 1) * TSL],
                                           in_=xfm[ot * 128:(ot + 1) * 128, :]).then_inc(g(sw), 16)

                    @b3.scalar
                    def _(scalar):
                        scalar.wait_ge(g("sp"), GBTOT)
                        scalar.wait_ge(g("spv"), GBTOT)
                        def emit_abs(ot):
                            scalar.wait_ge(g("swla" if ot % 2 == 0 else "swlb"), 16 * (ot // 2 + 1))
                            if ot >= 2:
                                scalar.wait_ge(g("swm"), ot - 1)
                            scalar.drain()
                            scalar.activation(
                                wabs[:, (ot % 2) * D:(ot % 2 + 1) * D],
                                wstg3[:, (ot % 2) * D:(ot % 2 + 1) * D], AF.Abs,
                            ).then_inc(g("sab"), 1)

                        for t8 in range(8):
                            b = t8 % 2
                            scalar.wait_ge(g("s3a"), t8 + 1)
                            scalar.drain()
                            scalar.activation(loc[:, 3:4], loc[:, 2:3], AF.Ln, bias=eps5_128[:, 0:1])
                            scalar.drain()
                            scalar.activation(loc[:, 4:5], loc[:, 3:4], AF.Exp, scale=-0.5)
                            scalar.drain()
                            scalar.activation(loc[:, 9:10], loc[:, 10:11], AF.Identity, scale=loc[:, 4:5])
                            scalar.wait_ge(g("s3xa" if t8 % 2 == 0 else "s3xb"), 16 * (t8 // 2 + 1))
                            scalar.drain()
                            scalar.activation(xn3[:], xtm3[:, b * D:(b + 1) * D], AF.Identity,
                                              scale=loc[:, 4:5], bias=loc[:, 9:10]).then_inc(g("s3b"), 1)
                            scalar.wait_ge(g("s3c"), t8 + 1)
                            scalar.drain()
                            scalar.activation(loc[:, 6:7], loc[:, 5:6], AF.Ln, bias=eps6_128[:, 0:1], scale=1.0 / D)
                            scalar.drain()
                            scalar.activation(loc[:, 6:7], loc[:, 6:7], AF.Exp, scale=-0.5)
                            scalar.drain()
                            scalar.activation(xn2g[:], xn3[:], AF.Identity,
                                              scale=loc[:, 6:7]).then_inc(g("s3d"), 1)
                            emit_abs(2 * t8)
                            emit_abs(2 * t8 + 1)

                    @b3.vector
                    def _(vector):
                        vector.wait_ge(g("spv"), GBTOT)
                        vector.wait_ge(g("se"), GBTOT)
                        for kpg in range(4):
                            vector.wait_ge(g("spg"), kpg + 1)
                            vector.tensor_copy(gam_b[:, kpg * 512:(kpg + 1) * 512], p_g3.ap()).then_inc(g("svg"), 1)
                        for t8 in range(8):
                            b = t8 % 2
                            if t8 >= 1:
                                vector.wait_ge(g("s3b"), t8)
                            vector.drain()
                            vector.tensor_scalar(loc[:, 0:1], stats_tm[:, t8:t8 + 1], 1.0 / D, None, ALU.mult)
                            vector.tensor_scalar(loc[:, 10:11], stats_tm[:, t8:t8 + 1], -1.0 / D, None, ALU.mult)
                            vector.tensor_scalar(loc[:, 1:2], stats_tm[:, 8 + t8:9 + t8], 1.0 / D, None, ALU.mult)
                            vector.drain()
                            vector.tensor_tensor(loc[:, 2:3], loc[:, 0:1], loc[:, 0:1], ALU.mult)
                            vector.drain()
                            vector.tensor_tensor(loc[:, 2:3], loc[:, 1:2], loc[:, 2:3], ALU.subtract).then_inc(g("s3a"), 1)
                            vector.wait_ge(g("s3b"), t8 + 1)
                            vector.drain()
                            vector.tensor_tensor(xn2g[:], xn3[:], xn3[:], ALU.mult)
                            vector.drain()
                            vector.reduce_sum(loc[:, 5:6], xn2g[:], axis=AX.X).then_inc(g("s3c"), 1)
                            vector.wait_ge(g("s3d"), t8 + 1)
                            vector.drain()
                            vector.tensor_tensor(xn2g[:], xn2g[:], gam_b[:], ALU.mult)
                            vector.drain()
                            vector.reduce_max(loc[:, 7:8], xn2g[:], axis=AX.X, apply_absolute_value=True)
                            vector.drain()
                            vector.tensor_scalar(loc[:, 7:8], loc[:, 7:8], 1e-5, None, ALU.add)
                            vector.drain()
                            vector.tensor_scalar(inva[:, t8:t8 + 1], loc[:, 7:8], 1.0 / 127.0, None, ALU.mult)
                            vector.drain()
                            vector.reciprocal(loc[:, 5:6], loc[:, 7:8])
                            vector.drain()
                            vector.tensor_scalar(loc[:, 5:6], loc[:, 5:6], 127.0, None, ALU.mult)
                            vector.drain()
                            vector.tensor_scalar(xn3[:], xn2g[:], loc[:, 5:6], RC, ALU.mult, ALU.add)
                            if t8 >= 2:
                                vector.wait_ge(g("s3t2"), 2 * (t8 - 2) + 2)
                            vector.drain()
                            vector.tensor_scalar(xq_tm[:, b * D:(b + 1) * D], xn3[:], RC, None, ALU.subtract).then_inc(g("s3q"), 1)
                            for j in range(2):
                                vector.wait_ge(g("s3t2"), 2 * t8 + j + 1)
                                vector.tensor_copy(
                                    xq_fm.ap().rearrange("p (c t) -> p c t", c=16)[:, j * 8:(j + 1) * 8, t8 * 128:(t8 + 1) * 128],
                                    p_t3[j].ap().rearrange("p (c t) -> p c t", c=8),
                                ).then_inc(g("s3e"), 1)
                        vector.wait_ge(g("swm"), 16)
                        vector.drain()
                        vector.reduce_sum(wsc[0:1, 0:1], p_w3.ap(), axis=AX.X)
                        vector.drain()
                        vector.tensor_scalar(wsc[0:1, 1:2], wsc[0:1, 0:1], 1.0 / (D * D), 1e-5, ALU.mult, ALU.add)
                        vector.drain()
                        vector.reciprocal(wsc[0:1, 2:3], wsc[0:1, 1:2]).then_inc(g("sws1"), 1)
                        vector.wait_ge(g("spg"), 5)
                        vector.tensor_copy(wme_v[:], p_g3.ap()[:, 0:1]).then_inc(g("svg"), 1)
                        vector.wait_ge(g("spg"), 6)
                        vector.tensor_copy(wsc_v[:], p_g3.ap()[:, 0:1]).then_inc(g("svg"), 1)
                        vector.drain()
                        for t8 in range(8):
                            ts_ = vector.tensor_scalar(sevl[:, t8:t8 + 1], inva[:, t8:t8 + 1], wme_v[:, 0:1], None, ALU.mult)
                            if t8 == 7:
                                ts_.then_inc(g("ssl"), 1)
                        vector.wait_ge(g("sst3"), 1)
                        vector.drain()
                        vector.tensor_copy(sev_r[:], p_sv.ap()).then_inc(g("ssr"), 1)
                        for half in range(2):
                            vector.wait_ge(g("spg"), 7 + half)
                            vector.tensor_copy(sevac_b[:, half * 512:(half + 1) * 512], p_g3.ap()).then_inc(g("svg"), 1)
                        vector.nop().then_inc(g("sws"), 1)

                    @b3.tensor
                    def _(tensor):
                        tensor.wait_ge(g("se"), GBTOT)
                        tensor.wait_ge(g("sr"), NTILE)
                        tensor.wait_ge(g("sao"), NTILE)
                        tensor.wait_ge(g("sgm"), 16)
                        for kpg in range(4):
                            if kpg >= 1:
                                tensor.wait_ge(g("svg"), kpg)
                            tensor.matmul(p_g3.ap(), ones_f1[:], gam_sb[0:1, kpg * 512:(kpg + 1) * 512],
                                          start=True, stop=True).then_inc(g("spg"), 1)
                        for t8 in range(8):
                            b = t8 % 2
                            tensor.wait_ge(g("s3q"), t8 + 1)
                            for j in range(2):
                                if t8 >= 1:
                                    tensor.wait_ge(g("s3e"), 2 * (t8 - 1) + j + 1)
                                for cc in range(8):
                                    tr = tensor.transpose(
                                        p_t3[j].ap().rearrange("p (c t) -> p c t", c=8)[:, cc, :],
                                        xq_tm[:, b * D + (j * 8 + cc) * 128:b * D + (j * 8 + cc + 1) * 128],
                                        iden_b[:],
                                    )
                                    if cc == 7:
                                        tr.then_inc(g("s3t2"), 1)
                            for ot in (2 * t8, 2 * t8 + 1):
                                tensor.wait_ge(g("sab"), ot + 1)
                                for piece in range(4):
                                    mm = tensor.matmul(
                                        p_w3.ap(), ones_bf[:],
                                        wabs[:, (ot % 2) * D + piece * 512:(ot % 2) * D + (piece + 1) * 512],
                                        start=(ot == 0 and piece == 0), stop=(ot == 15 and piece == 3),
                                    )
                                    if piece == 3:
                                        mm.then_inc(g("swm"), 1)
                        tensor.wait_ge(g("sws1"), 1)
                        tensor.wait_ge(g("svg"), 4)
                        tensor.matmul(p_g3.ap()[:, 0:1], ones_f1[:], wsc[0:1, 1:2], start=True, stop=True).then_inc(g("spg"), 1)
                        tensor.wait_ge(g("svg"), 5)
                        tensor.matmul(p_g3.ap()[:, 0:1], ones_f1[:], wsc[0:1, 2:3], start=True, stop=True).then_inc(g("spg"), 1)
                        tensor.wait_ge(g("ssl"), 1)
                        for t8 in range(8):
                            tr = tensor.transpose(
                                p_sv.ap().rearrange("a (b c) -> a b c", b=8)[:, t8, :],
                                sevl[:, t8:t8 + 1], iden_f[:],
                            )
                            if t8 == 7:
                                tr.then_inc(g("sst3"), 1)
                        tensor.wait_ge(g("ssr"), 1)
                        for half in range(2):
                            tensor.wait_ge(g("svg"), 6 + half)
                            tensor.matmul(p_g3.ap(), ones_f1[:], sev_r[0:1, half * 512:(half + 1) * 512],
                                          start=True, stop=True).then_inc(g("spg"), 1)


            if stop < 4:
                return nc
            with contextlib.ExitStack() as c4:
                wf_st = c4.enter_context(nc.sbuf_tensor("wf_st", [128, 3 * D], F32))
                wo_bf = c4.enter_context(nc.sbuf_tensor("wo_bf", [128, 3 * D], BF16))
                wn_bf = c4.enter_context(nc.sbuf_tensor("wn_bf", [128, 3 * D], BF16))
                wnt2 = c4.enter_context(nc.sbuf_tensor("wnt2", [128, 2 * D], F32))
                t1_sb = c4.enter_context(nc.sbuf_tensor("t1_sb", [128, TSL], F32))
                f1_sb = c4.enter_context(nc.sbuf_tensor("f1_sb", [128, TSL], F32))
                o1_sb = c4.enter_context(nc.sbuf_tensor("o1_sb", [128, TSL], F32))
                out_sb = c4.enter_context(nc.sbuf_tensor("out_sb", [128, 2 * TSL], F32))
                p4 = [c4.enter_context(nc.psum_tensor(f"p4_{i}", [128, 512], F32)) for i in range(8)]

                with nc.Block() as b4:
                    @b4.gpsimd
                    def _(gpsimd):
                        for ot in range(16):
                            gpsimd.wait_ge(g("s4p"), max(0, ot - 2))
                            gpsimd.wait_ge(g("s4e"), max(0, ot - 2))
                            b = ot % 3
                            gpsimd.dma_start(
                                out=wf_st[:, b * D:(b + 1) * D],
                                in_=wfl[ot * 128:(ot + 1) * 128, :],
                            ).then_inc(g(("s4fa", "s4fb", "s4fc")[b]), 16)

                    @b4.sync
                    def _(sync):
                        sync.wait_ge(g("sws"), 1)
                        sync.wait_ge(g("s3e"), 16)
                        sync.wait_ge(g("s3t2"), 16)
                        sync.wait_ge(g("swm"), 16)
                        sync.wait_ge(g("svg"), 8)
                        def emit_out(ot):
                            sync.wait_ge(g("s4e"), ot + 1)
                            sync.dma_start(out=out[ot * 128:(ot + 1) * 128, :],
                                           in_=out_sb[:, (ot % 2) * TSL:(ot % 2 + 1) * TSL]
                                           ).then_inc(g("s4oa" if ot % 2 == 0 else "s4ob"), 16)
                        for ot in range(16):
                            b = ot % 3
                            sync.wait_ge(g("s4p"), max(0, ot - 2))
                            sync.wait_ge(g("s4e"), max(0, ot - 2))
                            if ot >= 3:
                                sw = ("s4wa", "s4wb", "s4wc")[b]
                                sync.dma_start(out=wo_st[:, b * D:(b + 1) * D],
                                               in_=wol[ot * 128:(ot + 1) * 128, :]).then_inc(g(sw), 16)
                                sync.dma_start(out=xr_st[:, b * TSL:(b + 1) * TSL],
                                               in_=xfm[ot * 128:(ot + 1) * 128, :]).then_inc(g(sw), 16)
                            if ot >= 2:
                                emit_out(ot - 2)
                        emit_out(14)
                        emit_out(15)
                        sync.wait_ge(g("s4oa"), 128)
                        sync.wait_ge(g("s4ob"), 128)

                    @b4.scalar
                    def _(scalar):
                        for ot in range(16):
                            b = ot % 3
                            sl = ot % 2
                            scalar.wait_ge(g(("s4wa", "s4wb", "s4wc")[b]), 32 * (ot // 3 + 1))
                            scalar.wait_ge(g(("s4fa", "s4fb", "s4fc")[b]), 16 * (ot // 3 + 1))
                            if ot >= 3:
                                scalar.wait_ge(g("s4g"), 4 * (ot - 2))
                            if ot >= 2:
                                scalar.wait_ge(g("s4p"), ot - 1)
                            scalar.drain()
                            scalar.activation(wo_bf[:, b * D:(b + 1) * D],
                                              wo_st[:, b * D:(b + 1) * D], AF.Identity)
                            scalar.activation(wnt2[:, sl * D:(sl + 1) * D],
                                              wf_st[:, b * D:(b + 1) * D], AF.Identity,
                                              scale=wsc_v[:, 0:1])
                            scalar.drain()
                            scalar.activation(wnt2[:, sl * D:(sl + 1) * D],
                                              wnt2[:, sl * D:(sl + 1) * D], AF.Identity,
                                              bias=rc_128[:, 0:1])
                            scalar.drain()
                            scalar.activation(wnt2[:, sl * D:(sl + 1) * D],
                                              wnt2[:, sl * D:(sl + 1) * D], AF.Identity,
                                              bias=nrc_128[:, 0:1]).then_inc(g("s4c"), 1)

                    @b4.vector
                    def _(vector):
                        for ot in range(16):
                            b = ot % 3
                            bp = ot % 2
                            vector.wait_ge(g("s4c"), ot + 1)
                            if ot >= 3:
                                vector.wait_ge(g("s4g"), 4 * (ot - 2))
                            vector.drain()
                            vector.tensor_scalar(wn_bf[:, b * D:(b + 1) * D],
                                                 wnt2[:, bp * D:(bp + 1) * D], 1.0, -1.0,
                                                 ALU.min, ALU.max).then_inc(g("s4p"), 1)
                            for th in range(2):
                                vector.wait_ge(g("s4g"), 4 * ot + th + 1)
                                vector.drain()
                                vector.tensor_tensor(f1_sb[:, th * 512:(th + 1) * 512],
                                                     p4[bp * 4 + th].ap(),
                                                     sevac_b[:, th * 512:(th + 1) * 512], ALU.mult)
                            for th in range(2):
                                vector.wait_ge(g("s4g"), 4 * ot + 2 + th + 1)
                                vector.drain()
                                vector.tensor_scalar(t1_sb[:, th * 512:(th + 1) * 512],
                                                     p4[bp * 4 + 2 + th].ap(), RC, RC, ALU.add, ALU.subtract)
                            vector.drain()
                            vector.scalar_tensor_tensor(o1_sb[:], t1_sb[:], 0.01, f1_sb[:], ALU.mult, ALU.add)
                            if ot >= 2:
                                vector.wait_ge(g("s4oa" if ot % 2 == 0 else "s4ob"), 16 * (ot // 2))
                            vector.drain()
                            vector.tensor_tensor(out_sb[:, bp * TSL:(bp + 1) * TSL], o1_sb[:],
                                                 xr_st[:, b * TSL:(b + 1) * TSL], ALU.add).then_inc(g("s4e"), 1)

                    @b4.tensor
                    def _(tensor):
                        for ot in range(16):
                            bp = ot % 2
                            tensor.wait_ge(g("s4p"), ot + 1)
                            if ot >= 2:
                                tensor.wait_ge(g("s4e"), ot - 1)
                            b3_ = ot % 3
                            for gi in range(4):
                                wsrc = wn_bf if gi < 2 else wo_bf
                                rsrc = xq_fm if gi < 2 else attn_g
                                if gi == 2 and ot == 0:
                                    tensor.wait_ge(g("sag"), 256)
                                th = gi % 2
                                for c in range(16):
                                    mm = tensor.matmul(
                                        p4[bp * 4 + gi].ap(),
                                        wsrc[:, b3_ * D + c * 128:b3_ * D + (c + 1) * 128],
                                        rsrc[:, c * TSL + th * 512:c * TSL + (th + 1) * 512],
                                        start=(c == 0), stop=(c == 15),
                                    )
                                    if c == 15:
                                        mm.then_inc(g("s4g"), 1)

    return nc


_NC_CACHE = {}


def kernel(L0_attn_x, L0_attn_Wq, L0_attn_Wk, L0_attn_Wv, L0_attn_Wo,
           L0_ffn_gamma, L0_ffn_W):
    x = np.asarray(L0_attn_x, dtype=np.float32).reshape(S, D)
    Wq = np.asarray(L0_attn_Wq, dtype=np.float32)
    Wk = np.asarray(L0_attn_Wk, dtype=np.float32)
    Wv = np.asarray(L0_attn_Wv, dtype=np.float32)
    Wo = np.asarray(L0_attn_Wo, dtype=np.float32)
    gamma = np.asarray(L0_ffn_gamma, dtype=np.float32).reshape(1, D)
    W = np.asarray(L0_ffn_W, dtype=np.float32)

    xt_full = np.ascontiguousarray(x.T)
    WqT = np.ascontiguousarray(Wq.T)
    WkT = np.ascontiguousarray(Wk.T)
    WvT = np.ascontiguousarray(Wv.T)

    def otile_layout(MT):
        blocks = []
        for ot in range(16):
            sl = MT[:, ot * 128:(ot + 1) * 128]
            blk = sl.reshape(16, 128, 128).transpose(1, 0, 2).reshape(128, 2048)
            blocks.append(blk)
        return np.ascontiguousarray(np.concatenate(blocks, axis=0))

    wol_h = otile_layout(np.ascontiguousarray(Wo.T))
    wfl_h = otile_layout(np.ascontiguousarray(W.T))

    iden_b = np.eye(128, dtype=ml_dtypes.bfloat16)
    iden_f = np.eye(128, dtype=np.float32)
    kk = np.arange(128)[:, None]
    qq = np.arange(256)[None, :]
    maskA = (kk <= qq).astype(ml_dtypes.bfloat16)
    maskB = ((kk + 128) <= qq).astype(ml_dtypes.bfloat16)
    mask = np.ascontiguousarray(np.concatenate([maskA, maskB], axis=1))

    in_maps = []
    for c in range(NCOR):
        hs = slice(c * DOUT, (c + 1) * DOUT)
        in_maps.append({
            "xt": xt_full,
            "xtm": np.ascontiguousarray(x[c * TSL:(c + 1) * TSL, :]),
            "xfm": np.ascontiguousarray(xt_full[:, c * TSL:(c + 1) * TSL]),
            "wq": np.ascontiguousarray(WqT[:, hs]),
            "wk": np.ascontiguousarray(WkT[:, hs]),
            "wv": np.ascontiguousarray(WvT[:, hs]),
            "wol": wol_h,
            "wfl": wfl_h,
            "gam": gamma,
            "idb": iden_b,
            "idf": iden_f,
            "mskab": mask,
            "coff": np.array([[c]], dtype=np.uint32),
        })

    nc = build()
    res = run_bass_kernel_spmd(nc, in_maps, core_ids=list(range(NCOR)))
    outT = np.concatenate([res.results[c]["out"] for c in range(NCOR)], axis=1)
    return np.ascontiguousarray(outT.T).reshape(1, S, D).astype(np.float32)


# version 19
# speedup vs baseline: 1.0411x; 1.0281x over previous
import contextlib

import numpy as np
import ml_dtypes
import concourse.bass as bass
import concourse.mybir as mybir
from concourse.bass_utils import run_bass_kernel_spmd

F32 = mybir.dt.float32
BF16 = mybir.dt.bfloat16
U32 = mybir.dt.uint32
AF = mybir.ActivationFunctionType
ALU = mybir.AluOpType
AX = mybir.AxisListType

S = 8192
D = 2048
NCOR = 8
TSL = S // NCOR
HPC = 2
DOUT = 256
T1 = 256
NST = S // T1
TQ = 256
NQT = S // TQ
NCH = S // 128
RC = 12582912.0
SQHD = float(np.sqrt(128.0))
CS = 1.0 / (1e4 * SQHD)
C2 = 0.5 * CS
NTILE = HPC * NQT
TOTAL_PAIRS = HPC * (NQT * (NQT + 1) // 2)


def _batch_plan():
    tiles_plan = []
    gb_count = 0
    for T in range(NTILE):
        h, qt = divmod(T, NQT)
        NP = qt + 1
        PB = h * (NQT * (NQT + 1) // 2) + qt * (qt + 1) // 2
        batches = []
        p = 0
        while p < NP:
            pl = [p] if p + 1 >= NP else [p, p + 1]
            batches.append((gb_count, pl))
            gb_count += 1
            p += len(pl)
        tiles_plan.append((h, qt, NP, PB, batches))
    return tiles_plan, gb_count


TILES_PLAN, GBTOT = _batch_plan()
GB_END_TILE = [TILES_PLAN[T][4][-1][0] + 1 for T in range(NTILE)]


def npairs(qt):
    return qt + 1


def pbase(h, qt):
    return h * (NQT * (NQT + 1) // 2) + qt * (qt + 1) // 2


def pend(T):
    if T < 0:
        return 0
    h, qt = divmod(T, NQT)
    return pbase(h, qt) + npairs(qt)


def build(stop=5, b1mode=3):
    nc = bass.Bass("TRN2", target_bir_lowering=False)

    xt = nc.dram_tensor("xt", [D, S], F32, kind="ExternalInput").ap()
    xtm = nc.dram_tensor("xtm", [TSL, D], F32, kind="ExternalInput").ap()
    xfm = nc.dram_tensor("xfm", [D, TSL], F32, kind="ExternalInput").ap()
    wq = nc.dram_tensor("wq", [D, DOUT], F32, kind="ExternalInput").ap()
    wk = nc.dram_tensor("wk", [D, DOUT], F32, kind="ExternalInput").ap()
    wv = nc.dram_tensor("wv", [D, DOUT], F32, kind="ExternalInput").ap()
    wol = nc.dram_tensor("wol", [16 * 128, D], F32, kind="ExternalInput").ap()
    wfl = nc.dram_tensor("wfl", [16 * 128, D], F32, kind="ExternalInput").ap()
    gam = nc.dram_tensor("gam", [1, D], F32, kind="ExternalInput").ap()
    idb = nc.dram_tensor("idb", [128, 128], BF16, kind="ExternalInput").ap()
    idf = nc.dram_tensor("idf", [128, 128], F32, kind="ExternalInput").ap()
    mskab = nc.dram_tensor("mskab", [128, 512], BF16, kind="ExternalInput").ap()
    coff = nc.dram_tensor("coff", [1, 1], U32, kind="ExternalInput").ap()
    out = nc.dram_tensor("out", [D, TSL], F32, kind="ExternalOutput").ap()

    sc_rstd = nc.dram_tensor("sc_rstd", [1, S], F32)
    sc_r1 = nc.dram_tensor("sc_r1", [1, S], BF16)
    sc_r100 = nc.dram_tensor("sc_r100", [2, 256], BF16)
    cc_st_in = nc.dram_tensor("cc_st_in", [2, TSL], F32)
    cc_st_out = nc.dram_tensor("cc_st_out", [16, TSL], F32)
    cc_at_in = nc.dram_tensor("cc_at_in", [DOUT, S], BF16)
    cc_at_out0 = nc.dram_tensor("cc_at_out0", [NCOR * 128, S], BF16, addr_space="Shared")
    cc_at_out1 = nc.dram_tensor("cc_at_out1", [NCOR * 128, S], BF16, addr_space="Shared")

    sem_names = [
        "sw0", "s0xa", "s0xb", "s0g", "s0v", "s0t", "s0c", "scc0", "s0r", "s0ch", "s0w",
        "swga", "swgb", "srsa", "srsb", "srsc", "srsd", "sxa", "sxb", "sxga", "sxgb", "sbc", "sRe", "sxs", "spj", "sev", "str_", "svt",
        "srs1", "srs2",
        "svq", "spn", "svr", "svm1", "spm", "svm",
        "sst", "se", "smask", "sz", "sr", "sbc2", "srb", "sp", "spv", "sao", "scwa", "scwb",
        "sgm", "s3xa", "s3xb", "s3a", "s3b", "s3c", "s3d", "s3q", "s3t2", "s3e",
        "swla", "swlb", "sab", "swm", "sws1", "sws",
        "spg", "svg", "ssl", "sst3", "ssr", "scc", "sag",
        "s4wa", "s4wb", "s4wc", "s4fa", "s4fb", "s4fc", "s4p", "s4g", "s4e", "s4oa", "s4ob", "s4c",
    ]

    with contextlib.ExitStack() as ctx:
        sems = {n: ctx.enter_context(nc.semaphore(n)) for n in sem_names}
        g = lambda n: sems[n]

        ones_bf = ctx.enter_context(nc.sbuf_tensor("ones_bf", [128, 1], BF16))
        ones_b1 = ctx.enter_context(nc.sbuf_tensor("ones_b1", [1, 128], BF16))
        ones_f1 = ctx.enter_context(nc.sbuf_tensor("ones_f1", [1, 128], F32))
        iden_b = ctx.enter_context(nc.sbuf_tensor("iden_b", [128, 128], BF16))
        iden_f = ctx.enter_context(nc.sbuf_tensor("iden_f", [128, 128], F32))
        maskab = ctx.enter_context(nc.sbuf_tensor("maskab", [128, 512], BF16))
        eps5_1 = ctx.enter_context(nc.sbuf_tensor("eps5_1", [1, 1], F32))
        eps5_128 = ctx.enter_context(nc.sbuf_tensor("eps5_128", [128, 1], F32))
        eps6_128 = ctx.enter_context(nc.sbuf_tensor("eps6_128", [128, 1], F32))
        rc_128 = ctx.enter_context(nc.sbuf_tensor("rc_128", [128, 1], F32))
        nrc_128 = ctx.enter_context(nc.sbuf_tensor("nrc_128", [128, 1], F32))
        stats_tm = ctx.enter_context(nc.sbuf_tensor("stats_tm", [128, 16], F32))
        s_coff = ctx.enter_context(nc.sbuf_tensor("s_coff", [1, 1], U32))

        with contextlib.ExitStack() as c12:
            q_sb = c12.enter_context(nc.sbuf_tensor("q_sb", [128, HPC * S], BF16))
            k_sb = c12.enter_context(nc.sbuf_tensor("k_sb", [128, HPC * S], BF16))
            vt_sb = c12.enter_context(nc.sbuf_tensor("vt_sb", [128, HPC * NCH * 128], BF16))

            with contextlib.ExitStack() as c1:
                xtm_t = c1.enter_context(nc.sbuf_tensor("xtm_t", [128, 2 * D], F32))
                st_tr = c1.enter_context(nc.sbuf_tensor("st_tr", [16, 128], F32))
                srA = c1.enter_context(nc.sbuf_tensor("srA", [128, 64], F32))
                srB = c1.enter_context(nc.sbuf_tensor("srB", [128, 64], F32))
                srC = c1.enter_context(nc.sbuf_tensor("srC", [128, 64], F32))
                sr1 = c1.enter_context(nc.sbuf_tensor("sr1", [128, 64], BF16))
                wstg = c1.enter_context(nc.sbuf_tensor("wstg", [128, 2 * DOUT], F32))
                wq_bf = c1.enter_context(nc.sbuf_tensor("wq_bf", [128, 16 * DOUT], BF16))
                wk_bf = c1.enter_context(nc.sbuf_tensor("wk_bf", [128, 16 * DOUT], BF16))
                wv_bf = c1.enter_context(nc.sbuf_tensor("wv_bf", [128, 16 * DOUT], BF16))
                rsum = c1.enter_context(nc.sbuf_tensor("rsum", [1, 3 * DOUT], BF16))
                xstage = c1.enter_context(nc.sbuf_tensor("xstage", [128, 2 * 16 * T1], F32))
                xs_sb = c1.enter_context(nc.sbuf_tensor("xs_sb", [128, 2 * 16 * T1], BF16))
                rsl = c1.enter_context(nc.sbuf_tensor("rsl", [1, 4 * T1], F32))
                r1sl = c1.enter_context(nc.sbuf_tensor("r1sl", [1, 4 * T1], BF16))
                R100b = c1.enter_context(nc.sbuf_tensor("R100b", [128, 2 * T1], F32))
                vtmp = c1.enter_context(nc.sbuf_tensor("vtmp", [128, 2 * 2 * T1], BF16))
                evsc = c1.enter_context(nc.sbuf_tensor("evsc", [128, T1], F32))
                p_pj = [c1.enter_context(nc.psum_tensor(f"p_pj{i}", [128, T1], F32)) for i in range(6)]
                p_bc1 = c1.enter_context(nc.psum_tensor("p_bc1", [128, T1], F32))
                p_tr1 = c1.enter_context(nc.psum_tensor("p_tr1", [128, 512], BF16))

                with nc.Block() as b1:
                    @b1.sync
                    def _(sync):
                        sync.dma_start(out=iden_b[:], in_=idb).then_inc(g("sw0"), 16)
                        sync.dma_start(out=iden_f[:], in_=idf).then_inc(g("sw0"), 16)
                        sync.dma_start(out=maskab[:], in_=mskab).then_inc(g("sw0"), 16)
                        sync.dma_start(out=s_coff[:], in_=coff).then_inc(g("sw0"), 16)
                        for t8 in range(2):
                            sync.dma_start(
                                out=xtm_t[:, (t8 % 2) * D:(t8 % 2 + 1) * D],
                                in_=xtm[t8 * 128:(t8 + 1) * 128, :],
                            ).then_inc(g("s0xa" if t8 % 2 == 0 else "s0xb"), 16)
                        for pj, wsrc in enumerate((wq, wk, wv)):
                            for c in range(16):
                                i = pj * 16 + c
                                sync.wait_ge(g("srs1"), max(0, i - 1))
                                sync.dma_start(
                                    out=wstg[:, (i % 2) * DOUT:(i % 2 + 1) * DOUT],
                                    in_=wsrc[c * 128:(c + 1) * 128, :],
                                ).then_inc(g("swga" if i % 2 == 0 else "swgb"), 16)
                        for t8 in range(2, 8):
                            sync.wait_ge(g("s0v"), t8 - 1)
                            sync.dma_start(
                                out=xtm_t[:, (t8 % 2) * D:(t8 % 2 + 1) * D],
                                in_=xtm[t8 * 128:(t8 + 1) * 128, :],
                            ).then_inc(g("s0xa" if t8 % 2 == 0 else "s0xb"), 16)
                        sync.wait_ge(g("s0ch"), 2)
                        sync.dma_start(out=sc_rstd.ap().rearrange("a (p c) -> (a p) c", p=128),
                                       in_=srA[:]).then_inc(g("s0w"), 16)
                        sync.wait_ge(g("s0ch"), 3)
                        sync.dma_start(out=sc_r1.ap().rearrange("a (p c) -> (a p) c", p=128),
                                       in_=sr1[:]).then_inc(g("s0w"), 16)
                        for i in range(NST if b1mode >= 2 else 0):
                            if i < 2:
                                sync.wait_ge(g("s0v"), 8)
                            sync.wait_ge(g("sxs"), 16 * max(0, i - 1))
                            for c in range(8):
                                sync.dma_start(
                                    out=xstage[:, ((i % 2) * 16 + c) * T1:((i % 2) * 16 + c) * T1 + T1],
                                    in_=xt[c * 128:(c + 1) * 128, i * T1:(i + 1) * T1],
                                ).then_inc(g("sxa" if i % 2 == 0 else "sxb"), 16)

                    @b1.scalar
                    def _(scalar):
                        scalar.wait_ge(g("s0r"), 1)
                        scalar.activation(srB[:], srB[:], AF.Ln, bias=eps5_128[:, 0:1])
                        scalar.drain()
                        scalar.activation(srB[:], srB[:], AF.Exp, scale=-0.5).then_inc(g("s0ch"), 1)
                        scalar.wait_ge(g("s0w"), 32)

                        def emit_evacs(i):
                            buf = i % 2
                            for gi in range(6):
                                pj, dt = divmod(gi, 2)
                                scalar.wait_ge(g("spj"), 6 * i + gi + 1)
                                if pj < 2:
                                    dstt = (q_sb, k_sb)[pj]
                                    od = dstt[:, dt * S + i * T1:dt * S + (i + 1) * T1]
                                else:
                                    od = vtmp[:, (buf * 2 + dt) * T1:(buf * 2 + dt) * T1 + T1]
                                    if i >= 2 and b1mode >= 3:
                                        scalar.wait_ge(g("str_"), 2 * (i - 2) + 2)
                                scalar.drain()
                                scalar.activation(evsc[:], p_pj[gi].ap(), AF.Identity,
                                                  bias=rc_128[:, 0:1])
                                scalar.drain()
                                scalar.activation(od, evsc[:], AF.Identity,
                                                  bias=nrc_128[:, 0:1]).then_inc(g("sev"), 1)

                        for i in range(NST if b1mode >= 2 else 0):
                            scalar.wait_ge(g("sbc"), max(0, i - 3))
                            if i >= 4:
                                scalar.wait_ge(g("spj"), 6 * (i - 3))
                            sw = ("srsa", "srsb", "srsc", "srsd")[i % 4]
                            scalar.dma_start(
                                out=rsl[:, (i % 4) * T1:(i % 4 + 1) * T1],
                                in_=sc_rstd.ap()[0:1, i * T1:(i + 1) * T1],
                            ).then_inc(g(sw), 16)
                            scalar.dma_start(
                                out=r1sl[:, (i % 4) * T1:(i % 4 + 1) * T1],
                                in_=sc_r1.ap()[0:1, i * T1:(i + 1) * T1],
                            ).then_inc(g(sw), 16)
                            if i >= 1 and b1mode >= 2:
                                emit_evacs(i - 1)
                        if b1mode >= 2:
                            emit_evacs(NST - 1)

                    @b1.gpsimd
                    def _(gpsimd):
                        gpsimd.wait_ge(g("s0c"), 1)
                        st2 = cc_st_in.ap().rearrange("s (t p) -> (s t) p", t=8)
                        gpsimd.dma_start(out=st2[0:8, :], in_=st_tr[0:8, :]).then_inc(g("s0g"), 16)
                        gpsimd.dma_start(out=st2[8:16, :], in_=st_tr[8:16, :]).then_inc(g("s0g"), 16)
                        gpsimd.wait_ge(g("s0g"), 32)
                        gpsimd.collective_compute(
                            "AllGather", ALU.bypass,
                            replica_groups=[list(range(NCOR))],
                            ins=[cc_st_in.ap()],
                            outs=[cc_st_out.ap()],
                        ).then_inc(g("scc0"), 1)
                        gpsimd.wait_ge(g("scc0"), 1)
                        for r in range(8):
                            gpsimd.dma_start(out=srA[r * 16:(r + 1) * 16, :],
                                             in_=cc_st_out.ap()[2 * r:2 * r + 1, :].rearrange("a (p c) -> (a p) c", p=16)).then_inc(g("s0g"), 16)
                            gpsimd.dma_start(out=srB[r * 16:(r + 1) * 16, :],
                                             in_=cc_st_out.ap()[2 * r + 1:2 * r + 2, :].rearrange("a (p c) -> (a p) c", p=16)).then_inc(g("s0g"), 16)
                        gpsimd.wait_ge(g("s0g"), 288)
                        gpsimd.nop().then_inc(g("scc0"), 1)
                        for i in range(NST if b1mode >= 2 else 0):
                            gpsimd.wait_ge(g("sxs"), 16 * max(0, i - 1))
                            for c in range(8, 16):
                                gpsimd.dma_start(
                                    out=xstage[:, ((i % 2) * 16 + c) * T1:((i % 2) * 16 + c) * T1 + T1],
                                    in_=xt[c * 128:(c + 1) * 128, i * T1:(i + 1) * T1],
                                ).then_inc(g("sxga" if i % 2 == 0 else "sxgb"), 16)



                    @b1.vector
                    def _(vector):
                        vector.memset(ones_bf[:], 1.0)
                        vector.memset(ones_b1[:], 1.0)
                        vector.memset(ones_f1[:], 1.0)
                        vector.memset(eps5_1[:], 1e-5)
                        vector.memset(eps5_128[:], 1e-5)
                        vector.memset(eps6_128[:], 1e-6)
                        vector.memset(rc_128[:], RC)
                        vector.memset(nrc_128[:], -RC)
                        for i in range(48):
                            vector.wait_ge(g("swga" if i % 2 == 0 else "swgb"), 16 * (i // 2 + 1))
                            dst = (wq_bf, wk_bf, wv_bf)[i // 16]
                            c = i % 16
                            vector.tensor_copy(
                                dst[:, c * DOUT:(c + 1) * DOUT],
                                wstg[:, (i % 2) * DOUT:(i % 2 + 1) * DOUT],
                            ).then_inc(g("srs1"), 1)
                        for t8 in range(8):
                            vector.wait_ge(g("s0xa" if t8 % 2 == 0 else "s0xb"), 16 * (t8 // 2 + 1))
                            xv = xtm_t[:, (t8 % 2) * D:(t8 % 2 + 1) * D]
                            vector.reduce_sum(stats_tm[:, t8:t8 + 1], xv, axis=AX.X)
                            vector.drain()
                            vector.tensor_tensor(xstage[:, 0:D], xv, xv, ALU.mult)
                            vector.drain()
                            vector.reduce_sum(stats_tm[:, 8 + t8:9 + t8], xstage[:, 0:D], axis=AX.X).then_inc(g("s0v"), 1)
                        vector.wait_ge(g("s0t"), 1)
                        vector.tensor_copy(st_tr[:], p_pj[5].ap()[0:16, 0:128]).then_inc(g("s0c"), 1)
                        vector.wait_ge(g("scc0"), 2)
                        vector.tensor_scalar(srC[:], srA[:], 1.0 / D, None, ALU.mult)
                        vector.tensor_scalar(srB[:], srB[:], 1.0 / D, None, ALU.mult)
                        vector.drain()
                        vector.tensor_tensor(srA[:], srC[:], srC[:], ALU.mult)
                        vector.drain()
                        vector.tensor_tensor(srB[:], srB[:], srA[:], ALU.subtract).then_inc(g("s0r"), 1)
                        vector.wait_ge(g("s0ch"), 1)
                        vector.drain()
                        vector.tensor_scalar(srA[:], srB[:], 100.0, None, ALU.mult).then_inc(g("s0ch"), 1)
                        vector.drain()
                        vector.tensor_tensor(srC[:], srC[:], srB[:], ALU.mult)
                        vector.drain()
                        vector.tensor_scalar(sr1[:], srC[:], -100.0, None, ALU.mult).then_inc(g("s0ch"), 1)
                        for pj in range(3):
                            vector.wait_ge(g("srs2"), pj + 1)
                            vector.tensor_copy(rsum[0:1, pj * DOUT:(pj + 1) * DOUT],
                                               p_pj[5].ap()[0:1, 0:DOUT]).then_inc(g("svt"), 1)
                        for i in range(NST if b1mode >= 2 else 0):
                            buf = i % 2
                            vector.wait_ge(g("sbc"), i + 1)
                            vector.drain()
                            vector.tensor_copy(R100b[:, buf * T1:(buf + 1) * T1], p_bc1.ap()).then_inc(g("sRe"), 1)
                            vector.wait_ge(g("sxa" if i % 2 == 0 else "sxb"), 128 * (i // 2 + 1))
                            vector.wait_ge(g("sxga" if i % 2 == 0 else "sxgb"), 128 * (i // 2 + 1))
                            if i >= 2:
                                vector.wait_ge(g("spj"), 6 * (i - 1))
                            vector.drain()
                            for c in range(16):
                                vector.tensor_tensor(
                                    xs_sb[:, (buf * 16 + c) * T1:(buf * 16 + c) * T1 + T1],
                                    xstage[:, (buf * 16 + c) * T1:(buf * 16 + c) * T1 + T1],
                                    R100b[:, buf * T1:(buf + 1) * T1],
                                    ALU.mult,
                                ).then_inc(g("sxs"), 1)

                            if i >= 1 and b1mode >= 3:
                                for h in range(2):
                                    vector.wait_ge(g("str_"), 2 * (i - 1) + h + 1)
                                    vector.tensor_copy(
                                        vt_sb[:, (h * NCH + (i - 1) * 2) * 128:(h * NCH + (i - 1) * 2) * 128 + 256],
                                        p_tr1.ap()[:, h * 256:h * 256 + 256] if False else p_tr1.ap().rearrange("p (a b) -> p a b", a=2)[:, h, :],
                                    ).then_inc(g("svt"), 1)
                        for h in range(2 if b1mode >= 3 else 0):
                            vector.wait_ge(g("str_"), 2 * (NST - 1) + h + 1)
                            vector.tensor_copy(
                                vt_sb[:, (h * NCH + (NST - 1) * 2) * 128:(h * NCH + (NST - 1) * 2) * 128 + 256],
                                p_tr1.ap().rearrange("p (a b) -> p a b", a=2)[:, h, :],
                            ).then_inc(g("svt"), 1)

                    @b1.tensor
                    def _(tensor):
                        tensor.wait_ge(g("s0v"), 8)
                        tensor.wait_ge(g("sw0"), 64)
                        tensor.transpose(p_pj[5].ap()[0:16, 0:128], stats_tm[:], iden_f[:]).then_inc(g("s0t"), 1)
                        tensor.wait_ge(g("s0c"), 1)
                        for pj, wbf in enumerate((wq_bf, wk_bf, wv_bf)):
                            tensor.wait_ge(g("srs1"), 16 * (pj + 1))
                            tensor.wait_ge(g("svt"), pj)
                            for c in range(16):
                                mm = tensor.matmul(p_pj[5].ap()[0:1, 0:DOUT], ones_bf[:],
                                                   wbf[:, c * DOUT:(c + 1) * DOUT],
                                                   start=(c == 0), stop=(c == 15))
                                if c == 15:
                                    mm.then_inc(g("srs2"), 1)
                        def emit_bc1(i):
                            tensor.wait_ge(g(("srsa", "srsb", "srsc", "srsd")[i % 4]), 32 * (i // 4 + 1))
                            if i >= 1:
                                tensor.wait_ge(g("sRe"), i)
                            tensor.matmul(p_bc1.ap(), ones_f1[:],
                                          rsl[0:1, (i % 4) * T1:(i % 4 + 1) * T1],
                                          start=True, stop=True).then_inc(g("sbc"), 1)

                        if b1mode >= 2:
                            emit_bc1(0)
                            emit_bc1(1)
                        for i in range(NST if b1mode >= 2 else 0):
                            buf = i % 2
                            if i == 0:
                                tensor.wait_ge(g("s0c"), 1)
                            for gi in range(6):
                                pj, dt = divmod(gi, 2)
                                wbf = (wq_bf, wk_bf, wv_bf)[pj]
                                if i >= 1:
                                    tensor.wait_ge(g("sev"), 6 * (i - 1) + gi + 1)
                                for c in range(16):
                                    if gi == 0:
                                        tensor.wait_ge(g("sxs"), 16 * i + c + 1)
                                    tensor.matmul(
                                        p_pj[gi].ap(),
                                        wbf[:, c * DOUT + dt * 128:c * DOUT + dt * 128 + 128],
                                        xs_sb[:, (buf * 16 + c) * T1:(buf * 16 + c) * T1 + T1],
                                        start=(c == 0), stop=False,
                                    )
                                tensor.matmul(
                                    p_pj[gi].ap(),
                                    rsum[0:1, pj * DOUT + dt * 128:pj * DOUT + dt * 128 + 128],
                                    r1sl[0:1, (i % 4) * T1:(i % 4 + 1) * T1],
                                    start=False, stop=True,
                                ).then_inc(g("spj"), 1)
                            if i >= 1 and b1mode >= 3:
                                ip = i - 1
                                for h in range(2):
                                    tensor.wait_ge(g("sev"), 6 * ip + 4 + h + 1)
                                    tensor.wait_ge(g("svt"), 3 + 2 * ip + h)
                                    for j in range(2):
                                        tr = tensor.transpose(
                                            p_tr1.ap().rearrange("p (a b) -> p a b", a=4)[:, 2 * h + j, :],
                                            vtmp[:, ((ip % 2) * 2 + h) * T1 + j * 128:((ip % 2) * 2 + h) * T1 + (j + 1) * 128],
                                            iden_b[:],
                                        )
                                        if j == 1:
                                            tr.then_inc(g("str_"), 1)
                            if i + 2 < NST:
                                emit_bc1(i + 2)
                        ip = NST - 1
                        for h in range(2 if b1mode >= 3 else 0):
                            tensor.wait_ge(g("sev"), 6 * ip + 4 + h + 1)
                            tensor.wait_ge(g("svt"), 3 + 2 * ip + h)
                            for j in range(2):
                                tr = tensor.transpose(
                                    p_tr1.ap().rearrange("p (a b) -> p a b", a=4)[:, 2 * h + j, :],
                                    vtmp[:, ((ip % 2) * 2 + h) * T1 + j * 128:((ip % 2) * 2 + h) * T1 + (j + 1) * 128],
                                    iden_b[:],
                                )
                                if j == 1:
                                    tr.then_inc(g("str_"), 1)

            if stop < 2:
                return nc
            with contextlib.ExitStack() as c2:
                estrip = [c2.enter_context(nc.sbuf_tensor(f"estrip{i}", [128, NQT * 512], BF16)) for i in range(2)]
                dstrip = [c2.enter_context(nc.sbuf_tensor(f"dstrip{i}", [128, 512], BF16)) for i in range(2)]
                R100a = [c2.enter_context(nc.sbuf_tensor(f"R100a{i}", [128, 1024], BF16)) for i in range(2)]
                r100v = c2.enter_context(nc.sbuf_tensor("r100v", [1, 256], F32))
                r100b_ = c2.enter_context(nc.sbuf_tensor("r100b_", [1, 256], BF16))
                t_scr = [c2.enter_context(nc.sbuf_tensor(f"t_scr{i}", [128, 1024], BF16)) for i in range(2)]
                p_scr = [c2.enter_context(nc.sbuf_tensor(f"p_scr{i}", [128, 1024], BF16)) for i in range(2)]
                ao_sb = [c2.enter_context(nc.sbuf_tensor(f"ao_sb{i}", [128, TQ], BF16)) for i in range(2)]
                qsq = c2.enter_context(nc.sbuf_tensor("qsq", [128, 1024], BF16))
                nrm = c2.enter_context(nc.sbuf_tensor("nrm", [1, 8], F32))
                aqrow = c2.enter_context(nc.sbuf_tensor("aqrow", [1, NQT], F32))
                lrow = c2.enter_context(nc.sbuf_tensor("lrow", [1, NQT], F32))
                negmg_all = c2.enter_context(nc.sbuf_tensor("negmg_all", [128, NTILE], F32))

                p_sA = c2.enter_context(nc.psum_tensor("p_sA", [128, 1024], F32))
                p_sB = c2.enter_context(nc.psum_tensor("p_sB", [128, 1024], F32))
                p_z = c2.enter_context(nc.psum_tensor("p_z", [1, 256], F32))
                p_o = [c2.enter_context(nc.psum_tensor(f"p_o{i}", [128, TQ], F32)) for i in range(2)]
                p_b2 = c2.enter_context(nc.psum_tensor("p_b2", [128, 512], F32))

                tiles_plan = TILES_PLAN
                gb_end_tile = GB_END_TILE

                def strip_of(T):
                    return estrip[T % 2]

                pro_steps = []
                for h in range(HPC):
                    for kind in ("q", "k", "d"):
                        for i16 in range(16):
                            pro_steps.append((h, kind, i16))
                    for g16 in range(16):
                        pro_steps.append((h, "b", g16))
                NPRO = len(pro_steps)

                def pro_bank(si):
                    return p_sA.ap()[:, (si % 2) * 512:(si % 2) * 512 + 512]

                def emit_zlast(tensor, Tp):
                    hp, qtp = divmod(Tp, NQT)
                    NPp = npairs(qtp)
                    tensor.wait_ge(g("smask"), Tp + 1)
                    if NPp == 1 and Tp > 0:
                        tensor.wait_ge(g("sr"), Tp)
                    for half in range(2):
                        mm = tensor.matmul(p_z.ap(), ones_bf[:],
                                           dstrip[Tp % 2][:, half * 256:half * 256 + 256],
                                           start=(NPp == 1 and half == 0), stop=(half == 1))
                        if half == 1:
                            mm.then_inc(g("sz"), 1)

                with nc.Block() as b2:
                    @b2.tensor
                    def _(tensor):
                        tensor.wait_ge(g("sev"), 6 * NST)
                        tensor.wait_ge(g("svt"), 3 + 2 * NST)
                        for si, (h, kind, idx) in enumerate(pro_steps):
                            tensor.wait_ge(g("svq"), si + 1)
                            if si >= 2:
                                tensor.wait_ge(g("svr"), si - 1)
                            if kind != "b":
                                tensor.matmul(
                                    pro_bank(si)[0:1, :],
                                    ones_bf[:], qsq[:, (si % 2) * 512:(si % 2) * 512 + 512],
                                    start=True, stop=True,
                                ).then_inc(g("spn"), 1)
                            else:
                                tensor.matmul(
                                    pro_bank(si)[:, 0:1],
                                    ones_f1[:], nrm[0:1, 3:4],
                                    start=True, stop=True,
                                ).then_inc(g("spn"), 1)
                        def emit_st(T, gb, pl):
                            h, qt, NP, PB, _ = tiles_plan[T]
                            if gb >= 2:
                                tensor.wait_ge(g("se"), gb - 1)
                            if gb == 0:
                                tensor.wait_ge(g("svr"), NPRO)
                            bank = (p_sA, p_sB)[gb % 2]
                            for j, p in enumerate(pl):
                                for half in range(2):
                                    c = 2 * p + half
                                    mm = tensor.matmul(
                                        bank.ap()[:, (2 * j + half) * 256:(2 * j + half + 1) * 256],
                                        k_sb[:, h * S + c * 128:h * S + (c + 1) * 128],
                                        q_sb[:, h * S + qt * TQ:h * S + (qt + 1) * TQ],
                                        start=True, stop=True,
                                    )
                                    if j == len(pl) - 1 and half == 1:
                                        mm.then_inc(g("sst"), 1)

                        def emit_zmm(T, gb, pl):
                            h, qt, NP, PB, _ = tiles_plan[T]
                            zps = [p for p in pl if p < NP - 1]
                            if not zps:
                                return
                            tensor.wait_ge(g("se"), gb + 1)
                            if pl[0] == 0 and T > 0:
                                tensor.wait_ge(g("sr"), T)
                            for p in zps:
                                for half in range(2):
                                    tensor.matmul(
                                        p_z.ap(), ones_bf[:],
                                        strip_of(T)[:, (2 * p + half) * 256:(2 * p + half + 1) * 256],
                                        start=(p == 0 and half == 0), stop=False,
                                    )

                        def emit_bcast2(Tp):
                            tensor.wait_ge(g("sr"), Tp + 1)
                            if Tp >= 1:
                                tensor.wait_ge(g("srb"), Tp)
                            for half in range(2):
                                mm = tensor.matmul(
                                    p_b2.ap().rearrange("p (a b) -> p a b", a=2)[:, half, :],
                                    ones_b1[:], r100b_[0:1, :], start=True, stop=True)
                                if half == 1:
                                    mm.then_inc(g("sbc2"), 1)

                        def emit_pv(Tp, gb, pl, first, last_of_tile):
                            hp, qtp, NPp, PBp, _ = tiles_plan[Tp]
                            if first and Tp >= 2:
                                tensor.wait_ge(g("sao"), Tp - 1)
                            tensor.wait_ge(g("sp"), gb + 1)
                            for j, p in enumerate(pl):
                                for half in range(2):
                                    c = 2 * p + half
                                    mm = tensor.matmul(
                                        p_o[Tp % 2].ap(),
                                        vt_sb[:, (hp * NCH + c) * 128:(hp * NCH + c) * 128 + 128],
                                        p_scr[gb % 2][:, (2 * j + half) * 256:(2 * j + half + 1) * 256],
                                        start=(first and j == 0 and half == 0),
                                        stop=(p == NPp - 1 and half == 1),
                                    )
                                    if j == len(pl) - 1 and half == 1:
                                        mm.then_inc(g("spv"), 1)

                        for T in range(NTILE):
                            h, qt, NP, PB, batches = tiles_plan[T]
                            if T > 0:
                                emit_zlast(tensor, T - 1)
                            pend_ops = []
                            if T > 0:
                                pend_ops.append(("bc", T - 1))
                                pb = tiles_plan[T - 1][4]
                                for bi, (gbp, plp) in enumerate(pb):
                                    pend_ops.append(("pv", T - 1, gbp, plp, bi == 0))
                            for ki, (gb, pl) in enumerate(batches):
                                emit_st(T, gb, pl)
                                if ki >= 1:
                                    emit_zmm(T, *batches[ki - 1])
                                if pend_ops:
                                    op = pend_ops.pop(0)
                                    if op[0] == "bc":
                                        emit_bcast2(op[1])
                                    else:
                                        emit_pv(op[1], op[2], op[3], op[4], False)
                            emit_zmm(T, *batches[-1])
                            for op in pend_ops:
                                if op[0] == "bc":
                                    emit_bcast2(op[1])
                                else:
                                    emit_pv(op[1], op[2], op[3], op[4], False)
                        emit_zlast(tensor, NTILE - 1)
                        Tp = NTILE - 1
                        emit_bcast2(Tp)
                        for bi, (gbp, plp) in enumerate(tiles_plan[Tp][4]):
                            emit_pv(Tp, gbp, plp, bi == 0, False)

                    @b2.scalar
                    def _(scalar):
                        scalar.wait_ge(g("sxs"), 16 * NST)
                        scalar.wait_ge(g("spj"), 6 * NST)
                        for T in range(NTILE):
                            h, qt, NP, PB, batches = tiles_plan[T]
                            if qt == 0:
                                scalar.wait_ge(g("svm"), h + 1)
                            if T >= 2:
                                scalar.wait_ge(g("sp"), gb_end_tile[T - 2])
                            for gb, pl in batches:
                                scalar.wait_ge(g("sst"), gb + 1)
                                scalar.drain()
                                scalar.activation(
                                    strip_of(T)[:, pl[0] * 512:(pl[-1] + 1) * 512],
                                    (p_sA, p_sB)[gb % 2].ap()[:, 0:len(pl) * 512],
                                    AF.Exp, scale=CS, bias=negmg_all[:, (T // 2):(T // 2) + 1],
                                ).then_inc(g("se"), 1)
                            scalar.wait_ge(g("sbc2"), T + 1)
                            if T >= 2:
                                scalar.wait_ge(g("sp"), gb_end_tile[T - 2])
                            scalar.drain()
                            scalar.activation(R100a[T % 2][:, 0:512], p_b2.ap(), AF.Identity)
                            scalar.drain()
                            scalar.activation(R100a[T % 2][:, 512:1024], p_b2.ap(), AF.Identity).then_inc(g("srb"), 1)
                            if T >= 1:
                                scalar.wait_ge(g("spv"), gb_end_tile[T - 1])
                                if T >= 3:
                                    scalar.wait_ge(g("scwa" if (T - 1) % 2 == 0 else "scwb"),
                                                   16 * ((T - 1) // 2))
                                scalar.drain()
                                scalar.activation(ao_sb[(T - 1) % 2][:], p_o[(T - 1) % 2].ap(),
                                                  AF.Identity, scale=0.01).then_inc(g("sao"), 1)
                        T = NTILE - 1
                        scalar.wait_ge(g("spv"), gb_end_tile[T])
                        scalar.wait_ge(g("scwa" if T % 2 == 0 else "scwb"), 16 * (T // 2))
                        scalar.drain()
                        scalar.activation(ao_sb[T % 2][:], p_o[T % 2].ap(), AF.Identity,
                                          scale=0.01).then_inc(g("sao"), 1)

                    @b2.vector
                    def _(vector):
                        vector.wait_ge(g("spj"), 6 * NST)
                        vector.wait_ge(g("str_"), 2 * NST)
                        vector.drain()
                        vector.memset(nrm[:], 0.0)
                        for si, (h, kind, idx) in enumerate(pro_steps):
                            if si >= 2:
                                vector.wait_ge(g("spn"), si - 1)
                            vector.drain()
                            if kind == "q":
                                sl = q_sb[:, h * S + idx * 512:h * S + (idx + 1) * 512]
                                vector.tensor_tensor(qsq[:, (si % 2) * 512:(si % 2) * 512 + 512],
                                                     sl, sl, ALU.mult).then_inc(g("svq"), 1)
                            elif kind == "k":
                                sl = k_sb[:, h * S + idx * 512:h * S + (idx + 1) * 512]
                                vector.tensor_tensor(qsq[:, (si % 2) * 512:(si % 2) * 512 + 512],
                                                     sl, sl, ALU.mult).then_inc(g("svq"), 1)
                            elif kind == "d":
                                qsl = q_sb[:, h * S + idx * 512:h * S + (idx + 1) * 512]
                                ksl = k_sb[:, h * S + idx * 512:h * S + (idx + 1) * 512]
                                vector.tensor_tensor(qsq[:, (si % 2) * 512:(si % 2) * 512 + 512],
                                                     qsl, ksl, ALU.mult).then_inc(g("svq"), 1)
                            else:
                                vector.tensor_tensor(nrm[0:1, 4:5], aqrow[0:1, idx:idx + 1],
                                                     nrm[0:1, 1:2], ALU.add)
                                vector.drain()
                                vector.tensor_scalar(nrm[0:1, 4:5], nrm[0:1, 4:5], C2, None, ALU.mult)
                                vector.tensor_scalar(nrm[0:1, 5:6], lrow[0:1, idx:idx + 1], CS, 80.0,
                                                     ALU.mult, ALU.add)
                                vector.drain()
                                vector.tensor_tensor(nrm[0:1, 4:5], nrm[0:1, 4:5], nrm[0:1, 5:6], ALU.min)
                                vector.drain()
                                vector.tensor_scalar(nrm[0:1, 3:4], nrm[0:1, 4:5], -1.0, None,
                                                     ALU.mult).then_inc(g("svq"), 1)
                            vector.wait_ge(g("spn"), si + 1)
                            vector.drain()
                            if kind == "q":
                                vector.reduce_max(aqrow[0:1, idx:idx + 1], pro_bank(si)[0:1, :],
                                                  axis=AX.X).then_inc(g("svr"), 1)
                            elif kind == "k":
                                vector.reduce_max(nrm[0:1, 6:7], pro_bank(si)[0:1, :], axis=AX.X)
                                vector.drain()
                                vector.tensor_tensor(nrm[0:1, 1:2], nrm[0:1, 1:2], nrm[0:1, 6:7],
                                                     ALU.max).then_inc(g("svr"), 1)
                            elif kind == "d":
                                vector.tensor_reduce(lrow[0:1, idx:idx + 1], pro_bank(si)[0:1, :],
                                                     AX.X, ALU.min).then_inc(g("svr"), 1)
                            else:
                                vector.tensor_copy(negmg_all[:, h * 16 + idx:h * 16 + idx + 1],
                                                   pro_bank(si)[:, 0:1]).then_inc(g("svr"), 1)
                            if kind == "b" and idx == 15:
                                vector.nop().then_inc(g("svm"), 1)
                        for T in range(NTILE):
                            h, qt, NP, PB, batches = tiles_plan[T]
                            vector.wait_ge(g("se"), gb_end_tile[T])
                            if T >= 2:
                                vector.wait_ge(g("sz"), T - 1)
                            vector.drain()
                            vector.tensor_tensor(
                                dstrip[T % 2][:],
                                strip_of(T)[:, (NP - 1) * 512:NP * 512],
                                maskab[:], ALU.mult,
                            ).then_inc(g("smask"), 1)
                            vector.wait_ge(g("sz"), T + 1)
                            vector.drain()
                            vector.reciprocal(r100v[:], p_z.ap())
                            vector.drain()
                            vector.tensor_scalar(r100b_[:], r100v[:], 100.0, None, ALU.mult).then_inc(g("sr"), 1)

                            vector.wait_ge(g("srb"), T + 1)
                            for gb, pl in batches:
                                if gb >= 2:
                                    vector.wait_ge(g("spv"), gb - 1)
                                vector.drain()
                                wid = len(pl) * 512
                                has_diag = (pl[-1] == NP - 1)
                                Ra = R100a[T % 2]
                                if has_diag and len(pl) == 2:
                                    vector.tensor_tensor(t_scr[gb % 2][:, 0:512],
                                                         strip_of(T)[:, pl[0] * 512:(pl[0] + 1) * 512],
                                                         Ra[:, 0:512], ALU.mult)
                                    vector.drain()
                                    vector.tensor_tensor(t_scr[gb % 2][:, 512:1024],
                                                         dstrip[T % 2][:], Ra[:, 0:512], ALU.mult)
                                elif has_diag:
                                    vector.tensor_tensor(t_scr[gb % 2][:, 0:512],
                                                         dstrip[T % 2][:], Ra[:, 0:512], ALU.mult)
                                else:
                                    vector.tensor_tensor(t_scr[gb % 2][:, 0:wid],
                                                         strip_of(T)[:, pl[0] * 512:(pl[-1] + 1) * 512],
                                                         Ra[:, 0:wid], ALU.mult)
                                vector.drain()
                                vector.tensor_scalar(p_scr[gb % 2][:, 0:wid], t_scr[gb % 2][:, 0:wid],
                                                     RC, RC, ALU.add, ALU.subtract).then_inc(g("sp"), 1)


                    @b2.sync
                    def _(sync):
                        for T in range(NTILE):
                            h, qt = divmod(T, NQT)
                            sync.wait_ge(g("sao"), T + 1)
                            sync.dma_start(
                                out=cc_at_in.ap()[h * 128:(h + 1) * 128, qt * TQ:(qt + 1) * TQ],
                                in_=ao_sb[T % 2][:],
                            ).then_inc(g("scwa" if T % 2 == 0 else "scwb"), 16)

                    @b2.gpsimd
                    def _(gpsimd):
                        gpsimd.wait_ge(g("scwa"), 16 * 16)
                        gpsimd.wait_ge(g("scwb"), 16 * 16)
                        gpsimd.collective_compute(
                            "AllGather", ALU.bypass,
                            replica_groups=[list(range(NCOR))],
                            ins=[cc_at_in.ap()[0:128, :]],
                            outs=[cc_at_out0.ap()],
                        ).then_inc(g("scc"), 1)
                        gpsimd.wait_ge(g("scwa"), 16 * 32)
                        gpsimd.wait_ge(g("scwb"), 16 * 32)
                        gpsimd.collective_compute(
                            "AllGather", ALU.bypass,
                            replica_groups=[list(range(NCOR))],
                            ins=[cc_at_in.ap()[128:256, :]],
                            outs=[cc_at_out1.ap()],
                        ).then_inc(g("scc"), 1)

        if stop < 3:
            return nc
        with contextlib.ExitStack() as c34:
            xq_fm = c34.enter_context(nc.sbuf_tensor("xq_fm", [128, 16 * TSL], BF16))
            sevac_b = c34.enter_context(nc.sbuf_tensor("sevac_b", [128, TSL], F32))
            wsc_v = c34.enter_context(nc.sbuf_tensor("wsc_v", [128, 1], F32))
            wme_v = c34.enter_context(nc.sbuf_tensor("wme_v", [128, 1], F32))
            attn_g = c34.enter_context(nc.sbuf_tensor("attn_g", [128, 16 * TSL], BF16))
            wo_st = c34.enter_context(nc.sbuf_tensor("wo_st", [128, 3 * D], F32))
            xr_st = c34.enter_context(nc.sbuf_tensor("xr_st", [128, 3 * TSL], F32))

            with contextlib.ExitStack() as c3:
                xtm3 = c3.enter_context(nc.sbuf_tensor("xtm3", [128, 2 * D], F32))
                xn3 = c3.enter_context(nc.sbuf_tensor("xn3", [128, D], F32))
                xn2g = c3.enter_context(nc.sbuf_tensor("xn2g", [128, D], F32))
                xq_tm = c3.enter_context(nc.sbuf_tensor("xq_tm", [128, 2 * D], BF16))
                gam_sb = c3.enter_context(nc.sbuf_tensor("gam_sb", [1, D], F32))
                gam_b = c3.enter_context(nc.sbuf_tensor("gam_b", [128, D], F32))
                loc = c3.enter_context(nc.sbuf_tensor("loc", [128, 8], F32))
                inva = c3.enter_context(nc.sbuf_tensor("inva", [128, 8], F32))
                sevl = c3.enter_context(nc.sbuf_tensor("sevl", [128, 8], F32))
                wstg3 = c3.enter_context(nc.sbuf_tensor("wstg3", [128, 2 * D], F32))
                wabs = c3.enter_context(nc.sbuf_tensor("wabs", [128, 2 * D], BF16))
                wsc = c3.enter_context(nc.sbuf_tensor("wsc", [1, 4], F32))
                sev_r = c3.enter_context(nc.sbuf_tensor("sev_r", [1, TSL], F32))

                p_t3 = [c3.enter_context(nc.psum_tensor(f"p_t3{i}", [128, 1024], BF16)) for i in range(2)]
                p_w3 = c3.enter_context(nc.psum_tensor("p_w3", [1, 512], F32))
                p_sv = c3.enter_context(nc.psum_tensor("p_sv", [1, TSL], F32))
                p_g3 = c3.enter_context(nc.psum_tensor("p_g3", [128, 512], F32))

                with nc.Block() as b3:
                    @b3.gpsimd
                    def _(gpsimd):
                        gpsimd.wait_ge(g("sw0"), 64)
                        with gpsimd.register("roff") as roff:
                            gpsimd.reg_load(roff, s_coff[0:1, 0:1])
                            off = gpsimd.snap(roff)
                            gpsimd.wait_ge(g("scc"), 1)
                            for c in range(0, 16, 2):
                                cg = c // 2
                                gpsimd.dma_start(
                                    out=attn_g[:, c * TSL:(c + 1) * TSL],
                                    in_=cc_at_out0.ap()[cg * 128:(cg + 1) * 128, bass.ts(off, TSL)],
                                ).then_inc(g("sag"), 16)
                            gpsimd.wait_ge(g("scc"), 2)
                            for c in range(1, 16, 2):
                                cg = c // 2
                                gpsimd.dma_start(
                                    out=attn_g[:, c * TSL:(c + 1) * TSL],
                                    in_=cc_at_out1.ap()[cg * 128:(cg + 1) * 128, bass.ts(off, TSL)],
                                ).then_inc(g("sag"), 16)

                    @b3.sync
                    def _(sync):
                        sync.wait_ge(g("sp"), GBTOT)
                        sync.wait_ge(g("spv"), GBTOT)
                        sync.wait_ge(g("sao"), NTILE)
                        sync.wait_ge(g("se"), GBTOT)
                        sync.dma_start(out=gam_sb[:], in_=gam).then_inc(g("sgm"), 16)
                        for t8 in range(8):
                            sync.wait_ge(g("s3q"), max(0, t8 - 1))
                            sync.dma_start(
                                out=xtm3[:, (t8 % 2) * D:(t8 % 2 + 1) * D],
                                in_=xtm[t8 * 128:(t8 + 1) * 128, :],
                            ).then_inc(g("s3xa" if t8 % 2 == 0 else "s3xb"), 16)
                            for ot in (2 * t8, 2 * t8 + 1):
                                sync.wait_ge(g("sab"), max(0, ot - 1))
                                sync.dma_start(
                                    out=wstg3[:, (ot % 2) * D:(ot % 2 + 1) * D],
                                    in_=wfl[ot * 128:(ot + 1) * 128, :],
                                ).then_inc(g("swla" if ot % 2 == 0 else "swlb"), 16)
                        for ot in range(3):
                            sw = ("s4wa", "s4wb", "s4wc")[ot]
                            sync.dma_start(out=wo_st[:, ot * D:(ot + 1) * D],
                                           in_=wol[ot * 128:(ot + 1) * 128, :]).then_inc(g(sw), 16)
                            sync.dma_start(out=xr_st[:, ot * TSL:(ot + 1) * TSL],
                                           in_=xfm[ot * 128:(ot + 1) * 128, :]).then_inc(g(sw), 16)

                    @b3.scalar
                    def _(scalar):
                        scalar.wait_ge(g("sp"), GBTOT)
                        scalar.wait_ge(g("spv"), GBTOT)
                        def emit_abs(ot):
                            scalar.wait_ge(g("swla" if ot % 2 == 0 else "swlb"), 16 * (ot // 2 + 1))
                            if ot >= 2:
                                scalar.wait_ge(g("swm"), ot - 1)
                            scalar.drain()
                            scalar.activation(
                                wabs[:, (ot % 2) * D:(ot % 2 + 1) * D],
                                wstg3[:, (ot % 2) * D:(ot % 2 + 1) * D], AF.Abs,
                            ).then_inc(g("sab"), 1)

                        for t8 in range(8):
                            scalar.wait_ge(g("s3a"), t8 + 1)
                            scalar.drain()
                            scalar.activation(loc[:, 3:4], loc[:, 2:3], AF.Ln, bias=eps5_128[:, 0:1])
                            scalar.drain()
                            scalar.activation(loc[:, 4:5], loc[:, 3:4], AF.Exp, scale=-0.5).then_inc(g("s3b"), 1)
                            emit_abs(2 * t8)
                            scalar.wait_ge(g("s3c"), t8 + 1)
                            scalar.drain()
                            scalar.activation(loc[:, 6:7], loc[:, 5:6], AF.Ln, bias=eps6_128[:, 0:1], scale=1.0 / D)
                            scalar.drain()
                            scalar.activation(loc[:, 6:7], loc[:, 6:7], AF.Exp, scale=-0.5).then_inc(g("s3d"), 1)
                            emit_abs(2 * t8 + 1)

                    @b3.vector
                    def _(vector):
                        vector.wait_ge(g("spv"), GBTOT)
                        vector.wait_ge(g("se"), GBTOT)
                        for kpg in range(4):
                            vector.wait_ge(g("spg"), kpg + 1)
                            vector.tensor_copy(gam_b[:, kpg * 512:(kpg + 1) * 512], p_g3.ap()).then_inc(g("svg"), 1)
                        for t8 in range(8):
                            b = t8 % 2
                            vector.drain()
                            vector.tensor_scalar(loc[:, 0:1], stats_tm[:, t8:t8 + 1], 1.0 / D, None, ALU.mult)
                            vector.tensor_scalar(loc[:, 1:2], stats_tm[:, 8 + t8:9 + t8], 1.0 / D, None, ALU.mult)
                            vector.drain()
                            vector.tensor_tensor(loc[:, 2:3], loc[:, 0:1], loc[:, 0:1], ALU.mult)
                            vector.drain()
                            vector.tensor_tensor(loc[:, 2:3], loc[:, 1:2], loc[:, 2:3], ALU.subtract).then_inc(g("s3a"), 1)
                            vector.wait_ge(g("s3b"), t8 + 1)
                            vector.wait_ge(g("s3xa" if t8 % 2 == 0 else "s3xb"), 16 * (t8 // 2 + 1))
                            vector.drain()
                            vector.tensor_scalar(xn3[:], xtm3[:, b * D:(b + 1) * D], loc[:, 0:1], loc[:, 4:5],
                                                 ALU.subtract, ALU.mult)
                            vector.drain()
                            vector.tensor_tensor(xn2g[:], xn3[:], xn3[:], ALU.mult)
                            vector.drain()
                            vector.reduce_sum(loc[:, 5:6], xn2g[:], axis=AX.X).then_inc(g("s3c"), 1)
                            vector.wait_ge(g("s3d"), t8 + 1)
                            vector.drain()
                            vector.tensor_scalar(xn2g[:], xn3[:], loc[:, 6:7], None, ALU.mult)
                            vector.drain()
                            vector.tensor_tensor(xn2g[:], xn2g[:], gam_b[:], ALU.mult)
                            vector.drain()
                            vector.reduce_max(loc[:, 7:8], xn2g[:], axis=AX.X, apply_absolute_value=True)
                            vector.drain()
                            vector.tensor_scalar(loc[:, 7:8], loc[:, 7:8], 1e-5, None, ALU.add)
                            vector.drain()
                            vector.tensor_scalar(inva[:, t8:t8 + 1], loc[:, 7:8], 1.0 / 127.0, None, ALU.mult)
                            vector.drain()
                            vector.reciprocal(loc[:, 5:6], loc[:, 7:8])
                            vector.drain()
                            vector.tensor_scalar(loc[:, 5:6], loc[:, 5:6], 127.0, None, ALU.mult)
                            vector.drain()
                            vector.tensor_scalar(xn3[:], xn2g[:], loc[:, 5:6], RC, ALU.mult, ALU.add)
                            if t8 >= 2:
                                vector.wait_ge(g("s3t2"), 2 * (t8 - 2) + 2)
                            vector.drain()
                            vector.tensor_scalar(xq_tm[:, b * D:(b + 1) * D], xn3[:], RC, None, ALU.subtract).then_inc(g("s3q"), 1)
                            for j in range(2):
                                vector.wait_ge(g("s3t2"), 2 * t8 + j + 1)
                                vector.tensor_copy(
                                    xq_fm.ap().rearrange("p (c t) -> p c t", c=16)[:, j * 8:(j + 1) * 8, t8 * 128:(t8 + 1) * 128],
                                    p_t3[j].ap().rearrange("p (c t) -> p c t", c=8),
                                ).then_inc(g("s3e"), 1)
                        vector.wait_ge(g("swm"), 16)
                        vector.drain()
                        vector.reduce_sum(wsc[0:1, 0:1], p_w3.ap(), axis=AX.X)
                        vector.drain()
                        vector.tensor_scalar(wsc[0:1, 1:2], wsc[0:1, 0:1], 1.0 / (D * D), 1e-5, ALU.mult, ALU.add)
                        vector.drain()
                        vector.reciprocal(wsc[0:1, 2:3], wsc[0:1, 1:2]).then_inc(g("sws1"), 1)
                        vector.wait_ge(g("spg"), 5)
                        vector.tensor_copy(wme_v[:], p_g3.ap()[:, 0:1]).then_inc(g("svg"), 1)
                        vector.wait_ge(g("spg"), 6)
                        vector.tensor_copy(wsc_v[:], p_g3.ap()[:, 0:1]).then_inc(g("svg"), 1)
                        vector.drain()
                        for t8 in range(8):
                            ts_ = vector.tensor_scalar(sevl[:, t8:t8 + 1], inva[:, t8:t8 + 1], wme_v[:, 0:1], None, ALU.mult)
                            if t8 == 7:
                                ts_.then_inc(g("ssl"), 1)
                        vector.wait_ge(g("sst3"), 1)
                        vector.drain()
                        vector.tensor_copy(sev_r[:], p_sv.ap()).then_inc(g("ssr"), 1)
                        for half in range(2):
                            vector.wait_ge(g("spg"), 7 + half)
                            vector.tensor_copy(sevac_b[:, half * 512:(half + 1) * 512], p_g3.ap()).then_inc(g("svg"), 1)
                        vector.nop().then_inc(g("sws"), 1)

                    @b3.tensor
                    def _(tensor):
                        tensor.wait_ge(g("se"), GBTOT)
                        tensor.wait_ge(g("sr"), NTILE)
                        tensor.wait_ge(g("sao"), NTILE)
                        tensor.wait_ge(g("sgm"), 16)
                        for kpg in range(4):
                            if kpg >= 1:
                                tensor.wait_ge(g("svg"), kpg)
                            tensor.matmul(p_g3.ap(), ones_f1[:], gam_sb[0:1, kpg * 512:(kpg + 1) * 512],
                                          start=True, stop=True).then_inc(g("spg"), 1)
                        for t8 in range(8):
                            b = t8 % 2
                            tensor.wait_ge(g("s3q"), t8 + 1)
                            for j in range(2):
                                if t8 >= 1:
                                    tensor.wait_ge(g("s3e"), 2 * (t8 - 1) + j + 1)
                                for cc in range(8):
                                    tr = tensor.transpose(
                                        p_t3[j].ap().rearrange("p (c t) -> p c t", c=8)[:, cc, :],
                                        xq_tm[:, b * D + (j * 8 + cc) * 128:b * D + (j * 8 + cc + 1) * 128],
                                        iden_b[:],
                                    )
                                    if cc == 7:
                                        tr.then_inc(g("s3t2"), 1)
                            for ot in (2 * t8, 2 * t8 + 1):
                                tensor.wait_ge(g("sab"), ot + 1)
                                for piece in range(4):
                                    mm = tensor.matmul(
                                        p_w3.ap(), ones_bf[:],
                                        wabs[:, (ot % 2) * D + piece * 512:(ot % 2) * D + (piece + 1) * 512],
                                        start=(ot == 0 and piece == 0), stop=(ot == 15 and piece == 3),
                                    )
                                    if piece == 3:
                                        mm.then_inc(g("swm"), 1)
                        tensor.wait_ge(g("sws1"), 1)
                        tensor.wait_ge(g("svg"), 4)
                        tensor.matmul(p_g3.ap()[:, 0:1], ones_f1[:], wsc[0:1, 1:2], start=True, stop=True).then_inc(g("spg"), 1)
                        tensor.wait_ge(g("svg"), 5)
                        tensor.matmul(p_g3.ap()[:, 0:1], ones_f1[:], wsc[0:1, 2:3], start=True, stop=True).then_inc(g("spg"), 1)
                        tensor.wait_ge(g("ssl"), 1)
                        for t8 in range(8):
                            tr = tensor.transpose(
                                p_sv.ap().rearrange("a (b c) -> a b c", b=8)[:, t8, :],
                                sevl[:, t8:t8 + 1], iden_f[:],
                            )
                            if t8 == 7:
                                tr.then_inc(g("sst3"), 1)
                        tensor.wait_ge(g("ssr"), 1)
                        for half in range(2):
                            tensor.wait_ge(g("svg"), 6 + half)
                            tensor.matmul(p_g3.ap(), ones_f1[:], sev_r[0:1, half * 512:(half + 1) * 512],
                                          start=True, stop=True).then_inc(g("spg"), 1)


            if stop < 4:
                return nc
            with contextlib.ExitStack() as c4:
                wf_st = c4.enter_context(nc.sbuf_tensor("wf_st", [128, 3 * D], F32))
                wo_bf = c4.enter_context(nc.sbuf_tensor("wo_bf", [128, 3 * D], BF16))
                wn_bf = c4.enter_context(nc.sbuf_tensor("wn_bf", [128, 3 * D], BF16))
                wnt2 = c4.enter_context(nc.sbuf_tensor("wnt2", [128, 2 * D], F32))
                t1_sb = c4.enter_context(nc.sbuf_tensor("t1_sb", [128, TSL], F32))
                f1_sb = c4.enter_context(nc.sbuf_tensor("f1_sb", [128, TSL], F32))
                o1_sb = c4.enter_context(nc.sbuf_tensor("o1_sb", [128, TSL], F32))
                out_sb = c4.enter_context(nc.sbuf_tensor("out_sb", [128, 2 * TSL], F32))
                p4 = [c4.enter_context(nc.psum_tensor(f"p4_{i}", [128, 512], F32)) for i in range(8)]

                with nc.Block() as b4:
                    @b4.gpsimd
                    def _(gpsimd):
                        for ot in range(16):
                            gpsimd.wait_ge(g("s4p"), max(0, ot - 2))
                            gpsimd.wait_ge(g("s4e"), max(0, ot - 2))
                            b = ot % 3
                            gpsimd.dma_start(
                                out=wf_st[:, b * D:(b + 1) * D],
                                in_=wfl[ot * 128:(ot + 1) * 128, :],
                            ).then_inc(g(("s4fa", "s4fb", "s4fc")[b]), 16)

                    @b4.sync
                    def _(sync):
                        sync.wait_ge(g("sws"), 1)
                        sync.wait_ge(g("s3e"), 16)
                        sync.wait_ge(g("s3t2"), 16)
                        sync.wait_ge(g("swm"), 16)
                        sync.wait_ge(g("svg"), 8)
                        def emit_out(ot):
                            sync.wait_ge(g("s4e"), ot + 1)
                            sync.dma_start(out=out[ot * 128:(ot + 1) * 128, :],
                                           in_=out_sb[:, (ot % 2) * TSL:(ot % 2 + 1) * TSL]
                                           ).then_inc(g("s4oa" if ot % 2 == 0 else "s4ob"), 16)
                        for ot in range(16):
                            b = ot % 3
                            sync.wait_ge(g("s4p"), max(0, ot - 2))
                            sync.wait_ge(g("s4e"), max(0, ot - 2))
                            if ot >= 3:
                                sw = ("s4wa", "s4wb", "s4wc")[b]
                                sync.dma_start(out=wo_st[:, b * D:(b + 1) * D],
                                               in_=wol[ot * 128:(ot + 1) * 128, :]).then_inc(g(sw), 16)
                                sync.dma_start(out=xr_st[:, b * TSL:(b + 1) * TSL],
                                               in_=xfm[ot * 128:(ot + 1) * 128, :]).then_inc(g(sw), 16)
                            if ot >= 2:
                                emit_out(ot - 2)
                        emit_out(14)
                        emit_out(15)
                        sync.wait_ge(g("s4oa"), 128)
                        sync.wait_ge(g("s4ob"), 128)

                    @b4.scalar
                    def _(scalar):
                        for ot in range(16):
                            b = ot % 3
                            sl = ot % 2
                            scalar.wait_ge(g(("s4wa", "s4wb", "s4wc")[b]), 32 * (ot // 3 + 1))
                            scalar.wait_ge(g(("s4fa", "s4fb", "s4fc")[b]), 16 * (ot // 3 + 1))
                            if ot >= 3:
                                scalar.wait_ge(g("s4g"), 4 * (ot - 2))
                            if ot >= 2:
                                scalar.wait_ge(g("s4p"), ot - 1)
                            scalar.drain()
                            scalar.activation(wo_bf[:, b * D:(b + 1) * D],
                                              wo_st[:, b * D:(b + 1) * D], AF.Identity)
                            scalar.activation(wnt2[:, sl * D:(sl + 1) * D],
                                              wf_st[:, b * D:(b + 1) * D], AF.Identity,
                                              scale=wsc_v[:, 0:1])
                            scalar.drain()
                            scalar.activation(wnt2[:, sl * D:(sl + 1) * D],
                                              wnt2[:, sl * D:(sl + 1) * D], AF.Identity,
                                              bias=rc_128[:, 0:1])
                            scalar.drain()
                            scalar.activation(wnt2[:, sl * D:(sl + 1) * D],
                                              wnt2[:, sl * D:(sl + 1) * D], AF.Identity,
                                              bias=nrc_128[:, 0:1]).then_inc(g("s4c"), 1)

                    @b4.vector
                    def _(vector):
                        for ot in range(16):
                            b = ot % 3
                            bp = ot % 2
                            vector.wait_ge(g("s4c"), ot + 1)
                            if ot >= 3:
                                vector.wait_ge(g("s4g"), 4 * (ot - 2))
                            vector.drain()
                            vector.tensor_scalar(wn_bf[:, b * D:(b + 1) * D],
                                                 wnt2[:, bp * D:(bp + 1) * D], 1.0, -1.0,
                                                 ALU.min, ALU.max).then_inc(g("s4p"), 1)
                            for th in range(2):
                                vector.wait_ge(g("s4g"), 4 * ot + th + 1)
                                vector.drain()
                                vector.tensor_tensor(f1_sb[:, th * 512:(th + 1) * 512],
                                                     p4[bp * 4 + th].ap(),
                                                     sevac_b[:, th * 512:(th + 1) * 512], ALU.mult)
                            for th in range(2):
                                vector.wait_ge(g("s4g"), 4 * ot + 2 + th + 1)
                                vector.drain()
                                vector.tensor_scalar(t1_sb[:, th * 512:(th + 1) * 512],
                                                     p4[bp * 4 + 2 + th].ap(), RC, RC, ALU.add, ALU.subtract)
                            vector.drain()
                            vector.scalar_tensor_tensor(o1_sb[:], t1_sb[:], 0.01, f1_sb[:], ALU.mult, ALU.add)
                            if ot >= 2:
                                vector.wait_ge(g("s4oa" if ot % 2 == 0 else "s4ob"), 16 * (ot // 2))
                            vector.drain()
                            vector.tensor_tensor(out_sb[:, bp * TSL:(bp + 1) * TSL], o1_sb[:],
                                                 xr_st[:, b * TSL:(b + 1) * TSL], ALU.add).then_inc(g("s4e"), 1)

                    @b4.tensor
                    def _(tensor):
                        for ot in range(16):
                            bp = ot % 2
                            tensor.wait_ge(g("s4p"), ot + 1)
                            if ot >= 2:
                                tensor.wait_ge(g("s4e"), ot - 1)
                            b3_ = ot % 3
                            for gi in range(4):
                                wsrc = wn_bf if gi < 2 else wo_bf
                                rsrc = xq_fm if gi < 2 else attn_g
                                if gi == 2 and ot == 0:
                                    tensor.wait_ge(g("sag"), 256)
                                th = gi % 2
                                for c in range(16):
                                    mm = tensor.matmul(
                                        p4[bp * 4 + gi].ap(),
                                        wsrc[:, b3_ * D + c * 128:b3_ * D + (c + 1) * 128],
                                        rsrc[:, c * TSL + th * 512:c * TSL + (th + 1) * 512],
                                        start=(c == 0), stop=(c == 15),
                                    )
                                    if c == 15:
                                        mm.then_inc(g("s4g"), 1)

    return nc


_NC_CACHE = {}


def kernel(L0_attn_x, L0_attn_Wq, L0_attn_Wk, L0_attn_Wv, L0_attn_Wo,
           L0_ffn_gamma, L0_ffn_W):
    x = np.asarray(L0_attn_x, dtype=np.float32).reshape(S, D)
    Wq = np.asarray(L0_attn_Wq, dtype=np.float32)
    Wk = np.asarray(L0_attn_Wk, dtype=np.float32)
    Wv = np.asarray(L0_attn_Wv, dtype=np.float32)
    Wo = np.asarray(L0_attn_Wo, dtype=np.float32)
    gamma = np.asarray(L0_ffn_gamma, dtype=np.float32).reshape(1, D)
    W = np.asarray(L0_ffn_W, dtype=np.float32)

    xt_full = np.ascontiguousarray(x.T)
    WqT = np.ascontiguousarray(Wq.T)
    WkT = np.ascontiguousarray(Wk.T)
    WvT = np.ascontiguousarray(Wv.T)

    def otile_layout(MT):
        blocks = []
        for ot in range(16):
            sl = MT[:, ot * 128:(ot + 1) * 128]
            blk = sl.reshape(16, 128, 128).transpose(1, 0, 2).reshape(128, 2048)
            blocks.append(blk)
        return np.ascontiguousarray(np.concatenate(blocks, axis=0))

    wol_h = otile_layout(np.ascontiguousarray(Wo.T))
    wfl_h = otile_layout(np.ascontiguousarray(W.T))

    iden_b = np.eye(128, dtype=ml_dtypes.bfloat16)
    iden_f = np.eye(128, dtype=np.float32)
    kk = np.arange(128)[:, None]
    qq = np.arange(256)[None, :]
    maskA = (kk <= qq).astype(ml_dtypes.bfloat16)
    maskB = ((kk + 128) <= qq).astype(ml_dtypes.bfloat16)
    mask = np.ascontiguousarray(np.concatenate([maskA, maskB], axis=1))

    in_maps = []
    for c in range(NCOR):
        hs = slice(c * DOUT, (c + 1) * DOUT)
        in_maps.append({
            "xt": xt_full,
            "xtm": np.ascontiguousarray(x[c * TSL:(c + 1) * TSL, :]),
            "xfm": np.ascontiguousarray(xt_full[:, c * TSL:(c + 1) * TSL]),
            "wq": np.ascontiguousarray(WqT[:, hs]),
            "wk": np.ascontiguousarray(WkT[:, hs]),
            "wv": np.ascontiguousarray(WvT[:, hs]),
            "wol": wol_h,
            "wfl": wfl_h,
            "gam": gamma,
            "idb": iden_b,
            "idf": iden_f,
            "mskab": mask,
            "coff": np.array([[c]], dtype=np.uint32),
        })

    nc = build()
    res = run_bass_kernel_spmd(nc, in_maps, core_ids=list(range(NCOR)))
    outT = np.concatenate([res.results[c]["out"] for c in range(NCOR)], axis=1)
    return np.ascontiguousarray(outT.T).reshape(1, S, D).astype(np.float32)
